# version 29
# baseline (speedup 1.0000x reference)
import numpy as np
import ml_dtypes

import concourse.bass as bass
import concourse.mybir as mybir
from concourse.bass_utils import run_bass_kernel_spmd

F32 = mybir.dt.float32
BF16 = mybir.dt.bfloat16
AF = mybir.ActivationFunctionType
ALU = mybir.AluOpType
AX = mybir.AxisListType.X
BF = ml_dtypes.bfloat16

NCORE = 8
DM = 2048
NCH = 16
COLS = 1024
NH, HD = 16, 128
INTER = 8192
TT = 4096
RSD = float(1.0 / np.sqrt(np.float32(128.0)))
FW1 = float(1.0 / (np.sqrt(np.float32(2048.0)) * np.float32(64.0)))
FW2 = float(1.0 / (np.sqrt(np.float32(8192.0)) * np.float32(64.0)))
ENGS = ("tensor", "vector", "scalar", "sync", "gpsimd")
USE_WBF = True
WBF_NAMES = {"wqhT", "wkhT", "wvhT", "wprojhT", "tri"}


def _hadamard(n):
    h = np.array([[1.0]])
    while h.shape[0] < n:
        h = np.block([[h, h], [h, -h]])
    return h.astype(np.float32)


def _perm_fwht(nchunk):
    perm = np.empty(nchunk * 128, dtype=np.int64)
    for jp in range(nchunk):
        for pi in range(128):
            jr, g = pi // nchunk, pi % nchunk
            perm[jp * 128 + pi] = g * 128 + (jp + nchunk * jr)
    return perm


PERM1 = _perm_fwht(16)
PERM2 = _perm_fwht(64)


def _chunk_major(vec):
    n = vec.shape[0] // 128
    return np.ascontiguousarray(vec.reshape(n, 128).T.astype(np.float32))


def host_prep(inp):
    sh, pc = {}, [dict() for _ in range(NCORE)]
    x = inp["L0_attn_x"]
    for c in range(NCORE):
        xs = np.concatenate([x[0, 512 * c:512 * (c + 1), :], x[1, 512 * c:512 * (c + 1), :]], axis=0)
        pc[c]["xT"] = np.ascontiguousarray(xs.T.astype(np.float32))

    Wqkv = inp["L0_attn_W_qkv"]
    rp = np.array(list(range(0, 128, 2)) + list(range(1, 128, 2)), dtype=np.int64)
    qrows = np.concatenate([h * 128 + rp for h in range(NH)])
    wqT = np.ascontiguousarray(Wqkv[qrows, :].T)
    wkT = np.ascontiguousarray(Wqkv[2048 + qrows, :].T)
    wvT = np.ascontiguousarray(Wqkv[4096:6144, :].T)
    for c in range(NCORE):
        pc[c]["wqhT"] = np.ascontiguousarray(wqT[:, 256 * c:256 * (c + 1)])
        pc[c]["wkhT"] = np.ascontiguousarray(wkT[:, 256 * c:256 * (c + 1)])
        pc[c]["wvhT"] = np.ascontiguousarray(wvT[:, 256 * c:256 * (c + 1)])
    wprojT = np.ascontiguousarray(inp["L0_attn_W_proj"].T)
    for c in range(NCORE):
        pc[c]["wprojhT"] = np.ascontiguousarray(wprojT[256 * c:256 * (c + 1), :])

    inv_freq = (1.0 / 10000.0 ** (np.arange(0, 128, 2, dtype=np.float32) / np.float32(128.0))).astype(np.float32)
    pos = np.arange(TT, dtype=np.float32)
    ang = pos[None, :] * inv_freq[:, None]
    cos, sin = np.cos(ang).astype(np.float32), np.sin(ang).astype(np.float32)
    sh["cs2"] = np.ascontiguousarray(np.concatenate([cos, cos], axis=0))
    sh["ss2"] = np.ascontiguousarray(np.concatenate([-sin, sin], axis=0))
    sh["tri"] = np.ascontiguousarray(np.triu(np.ones((128, 128), dtype=np.float32)))
    sh["ident"] = np.ascontiguousarray(np.eye(128, dtype=np.float32))
    sh["onesf"] = np.ones((128, 1), dtype=np.float32)
    sh["ones1"] = np.ones((1, 128), dtype=np.float32)

    sh["h128"] = _hadamard(128)
    sh["a1"] = np.ascontiguousarray(np.kron(np.eye(8, dtype=np.float32), _hadamard(16)))
    sh["a2"] = np.ascontiguousarray(np.kron(np.eye(2, dtype=np.float32), _hadamard(64)))
    sh["wupT"] = np.ascontiguousarray(inp["L0_ffn_W_up"].T[PERM1, :])
    sh["wgateT"] = np.ascontiguousarray(inp["L0_ffn_W_gate"].T[PERM1, :])
    sh["wdownT"] = np.ascontiguousarray(inp["L0_ffn_W_down"].T[PERM2, :])
    sh["su_ug"] = _chunk_major(inp["L0_ffn_SU_ug"])
    sh["su_dp"] = _chunk_major(inp["L0_ffn_SU_dp"])
    sh["ws_up"] = _chunk_major(inp["L0_ffn_Wscale_ug"][:INTER])
    sh["ws_gate"] = _chunk_major(inp["L0_ffn_Wscale_ug"][INTER:])
    sh["ws_dp"] = _chunk_major(inp["L0_ffn_Wscale_dp"])

    sh["wq1T"] = np.ascontiguousarray(inp["L1_attn_Wq"])
    sh["wk1T"] = np.ascontiguousarray(inp["L1_attn_Wk"])
    sh["wv1T"] = np.ascontiguousarray(inp["L1_attn_Wv"])
    sh["wbT"] = np.ascontiguousarray(inp["L1_attn_Wb"])
    sh["woT"] = np.ascontiguousarray(inp["L1_attn_Wo"])
    for nm, key in (("convq", "L1_attn_conv_q"), ("convk", "L1_attn_conv_k"), ("convv", "L1_attn_conv_v")):
        cv = inp[key]
        sh[nm] = np.ascontiguousarray(cv.reshape(NCH, 128, 4).transpose(1, 0, 2).reshape(128, NCH * 4).astype(np.float32))
    sh["gnorm"] = np.ascontiguousarray(inp["L1_attn_g_norm"].reshape(128, 1).astype(np.float32))

    sh["gateT"] = np.ascontiguousarray(inp["L1_ffn_gate_weight"].T)
    sh["guT"] = np.ascontiguousarray(np.stack([np.ascontiguousarray(inp["L1_ffn_gate_up_weights"][e].T) for e in range(8)]))
    sh["moedT"] = np.ascontiguousarray(np.stack([np.ascontiguousarray(inp["L1_ffn_down_weights"][e].T) for e in range(8)]))
    sh["sguT"] = np.ascontiguousarray(inp["L1_ffn_shared_gate_up"].T)
    sh["sdT"] = np.ascontiguousarray(inp["L1_ffn_shared_down"].T)
    return sh, pc


class Ev:
    __slots__ = ("sem", "val")

    def __init__(self, sem, val):
        self.sem, self.val = sem, val


class Op:
    __slots__ = ("eng", "fn", "deps", "sig", "amt", "sigval")

    def __init__(self, eng, fn, deps, sig, amt, sigval=0):
        self.eng, self.fn, self.deps, self.sig, self.amt, self.sigval = eng, fn, deps, sig, amt, sigval


class Sched:
    SIG = {"tensor": "pe", "vector": "dve", "scalar": "act", "sync": "spq", "gpsimd": "pool"}

    def __init__(self, nc):
        self.nc = nc
        self.ops = []
        self.cnt = {}
        self.handles = {}
        self._sem_ctxs = []
        self.base_deps = ()
        self.dma_guard = ()

    def _sem(self, name):
        if name not in self.handles:
            cm = self.nc.semaphore(name)
            self.handles[name] = cm.__enter__()
            self._sem_ctxs.append(cm)
            self.cnt[name] = 0
        return self.handles[name]

    DMA_POOL = 8

    def emit(self, eng, fn, deps=(), sig=None, dma=False):
        semname, amt = None, 0
        if dma:
            q = "dms" if eng == "sync" else "dmg"
            if not hasattr(self, "_dma_idx"):
                self._dma_idx = {}
                self._dma_last = {}
            idx = self._dma_idx.get(q, 0)
            self._dma_idx[q] = idx + 1
            k = idx % self.DMA_POOL
            semname, amt = f"{q}{k}", 16
            prev = self._dma_last.get(semname)
            if prev is not None:
                deps = tuple(deps) + (prev,)
        elif sig is not None:
            semname, amt = (self.SIG[eng] if sig is True else sig), 1
        ev = None
        if semname is not None:
            self._sem(semname)
            self.cnt[semname] += amt
            ev = Ev(semname, self.cnt[semname])
        if dma:
            self._dma_last[semname] = ev
        deps = tuple(d for d in deps if d is not None) + self.base_deps
        self.ops.append(Op(eng, fn, deps, semname, amt, ev.val if ev is not None else 0))
        return ev

    def dma(self, eng, out_ap, in_ap, deps=(), **kw):
        deps = tuple(deps) + self.dma_guard
        return self.emit(eng, lambda e: e.dma_start(out=out_ap, in_=in_ap, **kw), deps=deps, dma=True)

    def phase_fence(self):
        self.dma_guard = ()
        evs = []
        for e in ENGS:
            if e in ("sync",):
                ev = self.emit(e, lambda eng: eng.nop(), sig=True)
            else:
                ev = self.emit(e, lambda eng: eng.drain(), sig=True)
            evs.append(ev)
        self.base_deps = tuple(evs)

    def finalize(self):
        deps = []
        for nm in list(self.cnt):
            if nm.startswith("dms") or nm.startswith("dmg"):
                deps.append(Ev(nm, self.cnt[nm]))
        self.emit("sync", lambda e: e.nop(), deps=deps)

    def replay(self, blk):
        attr = {"tensor": blk.tensor, "vector": blk.vector, "scalar": blk.scalar,
                "sync": blk.sync, "gpsimd": blk.gpsimd}
        sched = self
        for eng in ENGS:
            myops = [op for op in self.ops if op.eng == eng]
            if not myops:
                continue

            def body(e, myops=myops, eng=eng):
                waited = {}
                own = sched.SIG[eng]
                owncnt = 0
                for op in myops:
                    for d in op.deps:
                        if d.sem == own and d.val > owncnt:
                            continue
                        if waited.get(d.sem, -1) >= d.val:
                            continue
                        e.wait_ge(sched.handles[d.sem], d.val)
                        waited[d.sem] = d.val
                    ins = op.fn(e)
                    if op.sig is not None:
                        ins.then_inc(sched.handles[op.sig], op.amt)
                        if op.sig == own:
                            owncnt = max(owncnt, op.sigval)
            attr[eng](body)
        for k, v in self.cnt.items():
            assert v < 64000, f"sem {k} overflow risk: {v}"

    def close(self):
        for cm in reversed(self._sem_ctxs):
            cm.__exit__(None, None, None)


_SBUF_UID = [0]


def mk_sbuf(nc, stack, name, shape, dtype):
    _SBUF_UID[0] += 1
    cm = nc.sbuf_tensor(f"{name}_u{_SBUF_UID[0]}", list(shape), dtype)
    h = cm.__enter__()
    stack.append(cm)
    return h


def close_stack(stack):
    for cm in reversed(stack):
        cm.__exit__(None, None, None)
    stack.clear()


class PsumMgr:
    def __init__(self, nc):
        self.banks, self._ctx, self.free_ev = [], [], [None] * 8
        for i in range(8):
            cm = nc.psum_tensor(f"pb{i}", [128, 512], F32)
            self.banks.append(cm.__enter__())
            self._ctx.append(cm)
        self.rr = 0

    def acquire(self):
        i = self.rr
        self.rr = (self.rr + 1) % 8
        return i, self.banks[i], self.free_ev[i]

    def release(self, i, ev):
        self.free_ev[i] = ev

    def close(self):
        for cm in reversed(self._ctx):
            cm.__exit__(None, None, None)


def bc(ap2d, reps):
    return ap2d.unsqueeze(2).broadcast_to([ap2d.shape[0], ap2d.shape[1], reps])


HANDOFF_SHAPES = {
    "xn0": ((DM, COLS), BF16), "xng": ((8 * DM, COLS), BF16),
    "g0": ((DM, COLS), F32), "pattn": ((256, 2 * TT), BF16), "ps0": ((8 * DM, COLS), F32),
    "ps0a": ((8 * DM, 512), F32), "ps0b": ((8 * DM, 512), F32),
    "s0a": ((DM, 512), F32), "s0b": ((DM, 512), F32),
    "s0": ((DM, COLS), F32), "x1": ((DM, COLS), F32), "xn1": ((DM, COLS), BF16),
    "g1": ((DM, COLS), F32), "s1": ((DM, COLS), F32),
}

INPUT_SHAPES = {
    "xT": ((DM, COLS), F32), "wqhT": ((DM, 256), F32), "wkhT": ((DM, 256), F32),
    "wvhT": ((DM, 256), F32), "wprojhT": ((256, DM), F32), "cs2": ((128, TT), F32), "ss2": ((128, TT), F32),
    "onesf": ((128, 1), F32), "ones1": ((1, 128), F32),
    "h128": ((128, 128), F32), "a1": ((128, 128), F32), "a2": ((128, 128), F32),
    "su_ug": ((128, 16), F32), "su_dp": ((128, 64), F32), "ws_up": ((128, 64), F32),
    "ws_gate": ((128, 64), F32), "ws_dp": ((128, 16), F32),
    "wupT": ((DM, INTER), F32), "wgateT": ((DM, INTER), F32), "wdownT": ((INTER, DM), F32),
    "tri": ((128, 128), F32), "ident": ((128, 128), F32),
    "gateT": ((DM, 8), F32), "guT": ((8, DM, 2816), F32), "moedT": ((8, 1408, DM), F32),
    "sguT": ((DM, 5632), F32), "sdT": ((2816, DM), F32),
    "wq1T": ((DM, DM), F32), "wk1T": ((DM, DM), F32), "wv1T": ((DM, DM), F32), "woT": ((DM, DM), F32),
    "wbT": ((DM, 16), F32), "convq": ((128, 64), F32), "convk": ((128, 64), F32),
    "convv": ((128, 64), F32), "gnorm": ((128, 1), F32),
}


class L:
    def __init__(self, nc, s, ps, mode):
        self.nc, self.s, self.ps, self.mode = nc, s, ps, mode
        self.dram = {}
        self.store_evs = {}
        self.consts = {}
        self.cstack = []

    def _shape(self, name):
        if name in HANDOFF_SHAPES:
            return HANDOFF_SHAPES[name]
        return INPUT_SHAPES[name]

    def din(self, name):
        if name not in self.dram:
            shp, dt = self._shape(name)
            if self.mode == "fused" and name in HANDOFF_SHAPES:
                kind = "Internal"
            else:
                kind = "ExternalInput"
            self.dram[name] = self.nc.dram_tensor(name, list(shp), dt, kind=kind).ap()
        return self.dram[name]

    def dout(self, name):
        if name not in self.dram:
            if name == "outT":
                shp, dt = (DM, COLS), F32
                kind = "ExternalOutput"
            else:
                shp, dt = self._shape(name)
                kind = "Internal" if self.mode == "fused" else "ExternalOutput"
            self.dram[name] = self.nc.dram_tensor(name, list(shp), dt, kind=kind).ap()
        return self.dram[name]

    def dscratch(self, name, shape, dtype):
        if name not in self.dram:
            self.dram[name] = self.nc.dram_tensor(name, list(shape), dtype, kind="Internal").ap()
        return self.dram[name]

    def store(self, name, ev):
        self.store_evs.setdefault(name, []).append(ev)

    def ldeps(self, name):
        return tuple(self.store_evs.get(name, ()))

    def wbf(self, name):
        if not hasattr(self, "_wbf"):
            self._wbf = {}
        if name not in self._wbf:
            shp, _ = self._shape(name)
            ap = self.din(name)
            t = self.nc.dram_tensor(f"bf_{name}", list(shp), BF16, kind="Internal").ap()
            n = 1
            for d in shp:
                n *= d
            rows = n // shp[-1]
            chunk = max(1, (4 * 1024 * 1024) // shp[-1])
            evs = []
            if len(shp) == 3:
                a2 = ap.rearrange("a b f -> (a b) f")
                t2 = t.rearrange("a b f -> (a b) f")
            else:
                a2, t2 = ap, t
            for r0 in range(0, rows, chunk):
                r1 = min(rows, r0 + chunk)
                evs.append(self.s.dma("gpsimd", t2[r0:r1, :], a2[r0:r1, :]))
            self._wbf[name] = (t2 if len(shp) > 2 else t, tuple(evs))
        t2, evs = self._wbf[name]
        return t2, evs

    def wget(self, name):
        if self.mode == "fused" and USE_WBF and name in WBF_NAMES:
            t, evs = self.wbf(name)
            return t, "sync", tuple(evs)
        return self.din(name), "gpsimd", ()

    def const(self, name, dst_dtype, stack=None):
        key = (name, dst_dtype)
        if stack is not None:
            shp, src_dtype = self._shape(name)
            t = mk_sbuf(self.nc, stack, f"p_{name}_{1 if dst_dtype == F32 else 2}", list(shp), dst_dtype)
            if src_dtype != dst_dtype and self.mode == "fused" and USE_WBF and name in WBF_NAMES:
                bap, evs = self.wbf(name)
                ev = self.s.dma("sync", t[:, :], bap, deps=tuple(evs) + self.s.base_deps)
            else:
                ap = self.din(name)
                eng = "gpsimd" if src_dtype != dst_dtype else "sync"
                ev = self.s.dma(eng, t[:, :], ap, deps=self.s.base_deps)
            return (t, ev)
        if key not in self.consts:
            shp, src_dtype = self._shape(name)
            ap = self.din(name)
            t = mk_sbuf(self.nc, self.cstack, f"c_{name}_{1 if dst_dtype == F32 else 2}", list(shp), dst_dtype)
            eng = "gpsimd" if src_dtype != dst_dtype else "sync"
            ev = self.s.dma(eng, t[:, :], ap)
            self.consts[key] = (t, ev)
        return self.consts[key]

    def epsap(self, val):
        key = ("eps", val)
        if key not in self.consts:
            t = mk_sbuf(self.nc, self.cstack, f"c_eps{len(self.consts)}", [128, 1], F32)
            ev = self.s.emit("vector", lambda e, t=t, val=val: e.memset(t[:, :], val), sig=True)
            self.consts[key] = (t, ev)
        return self.consts[key]


def emit_proj(l, wname, fcols, xtiles, xdeps, epilogue, stack, tag, nslab=2,
              tgs=((0, 512), (512, 1024)), wring=None):
    s, nc, ps = l.s, l.nc, l.ps
    wsrc, w_q, w_deps = l.wget(wname)
    wshape = INPUT_SHAPES[wname][0]
    nech = wshape[0] // 128
    fwmax = max(fw for _, fw in fcols)
    slab_w = nech * fwmax
    if wring is None:
        wbuf = mk_sbuf(nc, stack, f"{tag}_wb", [128, slab_w * nslab], BF16)
        slot_ev = [None] * nslab
    else:
        wbuf, slot_ev = wring
        assert wbuf.shape[1] >= slab_w * nslab
    for fi, (f0, fw) in enumerate(fcols):
        slot = fi % nslab
        src = wsrc[:, f0:f0 + fw].rearrange("(ec p) f -> p ec f", p=128)
        dst = wbuf[:, slot * slab_w: slot * slab_w + nech * fw].rearrange("p (ec f) -> p ec f", f=fw)
        lev = s.dma(w_q, dst, src, deps=(slot_ev[slot],) + w_deps)
        last_mm = None
        for tg_idx, (c0, c1) in enumerate(tgs):
            cw = c1 - c0
            bi, bank, bev = ps.acquire()
            mm_ev = None
            for ec in range(nech):
                lhsT = wbuf[:, slot * slab_w + ec * fw: slot * slab_w + (ec + 1) * fw]
                rhs = xtiles[ec][:, c0:c1]
                deps = [lev] + (list(xdeps) + [bev] if ec == 0 else [])
                mm_ev = s.emit("tensor",
                               (lambda e, o=bank[0:fw, 0:cw], lh=lhsT, r=rhs, st=(ec == 0), sp=(ec == nech - 1):
                                e.matmul(o, lh, r, start=st, stop=sp)),
                               deps=deps, sig=(True if ec == nech - 1 else None))
            last_mm = mm_ev
            free_ev = epilogue(fi, tg_idx, bank, mm_ev, f0, fw, c0, cw)
            ps.release(bi, free_ev)
        slot_ev[slot] = last_mm


def emit_ln(l, xsrc_tiles, xsrc_evs, xn_dram_name, xn_tiles, stack, tag):
    s, nc, ps = l.s, l.nc, l.ps
    xn_dram = l.dout(xn_dram_name)
    onesf, ev_of = l.const("onesf", F32)
    ones1, ev_o1 = l.const("ones1", F32)
    sq = mk_sbuf(nc, stack, f"{tag}sq", [128, 1024], F32)
    rows = mk_sbuf(nc, stack, f"{tag}rows", [1, 2048], F32)
    rb = mk_sbuf(nc, stack, f"{tag}rb", [128, 2048], F32)
    ready = []
    sq_mm_prev = None
    prev_apply = []
    for tg in range(2):
        c0 = tg * 512
        bi_s, bank_s, bev_s = ps.acquire()
        bi_q, bank_q, bev_q = ps.acquire()
        sum_ev = None
        sq_evs = [None, None]
        sq_mm = [sq_mm_prev, sq_mm_prev]
        for i in range(NCH):
            ssl = i % 2
            dep = [xsrc_evs[i], ev_of]
            sq_evs[ssl] = s.emit("scalar", (lambda e, x=xsrc_tiles[i], c=c0, ssl=ssl:
                                            e.activation(sq[:, ssl * 512:(ssl + 1) * 512], x[:, c:c + 512], AF.Square)),
                                 deps=dep + [sq_mm[ssl]], sig=True)
            sum_ev = s.emit("tensor", (lambda e, o=bank_s[0:1, :], x=xsrc_tiles[i], c=c0,
                                       st=(i == 0), sp=(i == NCH - 1):
                                       e.matmul(o, onesf[:, :], x[:, c:c + 512], start=st, stop=sp)),
                            deps=dep + ([bev_s] if i == 0 else []),
                            sig=(True if i == NCH - 1 else None))
            sq_mm[ssl] = s.emit("tensor", (lambda e, o=bank_q[0:1, :], ssl=ssl, st=(i == 0), sp=(i == NCH - 1):
                                           e.matmul(o, onesf[:, :], sq[:, ssl * 512:(ssl + 1) * 512], start=st, stop=sp)),
                               deps=[sq_evs[ssl]] + ([bev_q] if i == 0 else []),
                               sig=True)
            sq_mm_prev = sq_mm[ssl]
        mean = rows[0:1, c0:c0 + 512]
        var = rows[0:1, 1024 + c0:1024 + c0 + 512]
        e1 = s.emit("vector", lambda e, m=mean, b=bank_s: e.tensor_scalar_mul(m, b[0:1, :], 1.0 / 2048.0),
                    deps=[sum_ev], sig=True)
        e2 = s.emit("vector", lambda e, v=var, b=bank_q: e.tensor_scalar_mul(v, b[0:1, :], 1.0 / 2048.0),
                    deps=[sq_mm_prev], sig=True)
        ps.release(bi_s, e1)
        ps.release(bi_q, e2)
        e5 = s.emit("vector", lambda e, m=mean: e.tensor_mul(rb[0:1, 0:512], m, m), deps=[e1] + prev_apply, sig=True)
        e6 = s.emit("vector", lambda e, v=var: e.tensor_sub(v, v, rb[0:1, 0:512]), deps=[e2, e5], sig=True)
        eps5, ev_eps5 = l.epsap(1.0e-5)
        e7 = s.emit("scalar", lambda e, v=var: e.activation(v, v, AF.Ln, bias=eps5[0:1, 0:1]), deps=[e6, ev_eps5], sig=True)
        e8 = s.emit("scalar", lambda e, v=var: e.activation(v, v, AF.Exp, scale=-0.5), deps=[e7], sig=True)
        e9 = s.emit("vector", lambda e, m=mean, v=var: e.tensor_mul(m, m, v), deps=[e8], sig=True)
        bi_b, bank_b, bev_b = ps.acquire()
        eb1 = s.emit("tensor", lambda e, o=bank_b, v=var: e.matmul(o[:, :], ones1[:, :], v, start=True, stop=True),
                     deps=[e8, bev_b, ev_o1], sig=True)
        ec1 = s.emit("scalar", lambda e, b=bank_b, c=c0: e.copy(rb[:, c:c + 512], b[:, :]), deps=[eb1], sig=True)
        ps.release(bi_b, ec1)
        bi_b2, bank_b2, bev_b2 = ps.acquire()
        eb2 = s.emit("tensor", lambda e, o=bank_b2, m=mean: e.matmul(o[:, :], ones1[:, :], m, start=True, stop=True),
                     deps=[e9, bev_b2], sig=True)
        ec2 = s.emit("scalar", lambda e, b=bank_b2, c=c0: e.copy(rb[:, 1024 + c:1024 + c + 512], b[:, :]), deps=[eb2], sig=True)
        ps.release(bi_b2, ec2)
        for i in range(NCH):
            ea = s.emit("vector", (lambda e, o=xn_tiles[i], x=xsrc_tiles[i], c=c0:
                                   e.tensor_mul(o[:, c:c + 512], x[:, c:c + 512], rb[:, c:c + 512])),
                        deps=[xsrc_evs[i], ec1], sig=True)
            eb = s.emit("vector", (lambda e, o=xn_tiles[i], c=c0:
                                   e.tensor_sub(o[:, c:c + 512], o[:, c:c + 512], rb[:, 1024 + c:1024 + c + 512])),
                        deps=[ea, ec2], sig=True)
            dev = s.dma("sync", xn_dram[i * 128:(i + 1) * 128, c0:c0 + 512], xn_tiles[i][:, c0:c0 + 512], deps=[eb])
            l.store(xn_dram_name, dev)
            ready.append(eb)
        prev_apply = ready[-NCH:]
    return ready


def emit_ln0(l):
    s, nc = l.s, l.nc
    stack = []
    xT = l.din("xT")
    x_t = [mk_sbuf(nc, stack, f"lx{i}", [128, COLS], F32) for i in range(NCH)]
    x_ev = [s.dma("sync", x_t[i][:, :], xT[i * 128:(i + 1) * 128, :]) for i in range(NCH)]
    xn_t = [mk_sbuf(nc, stack, f"lxn{i}", [128, COLS], BF16) for i in range(NCH)]
    emit_ln(l, x_t, x_ev, "xn0", xn_t, stack, "ln0")
    close_stack(stack)


def emit_attq(l, upto="all"):
    s, nc, ps = l.s, l.nc, l.ps
    stack = []
    xng = l.din("xng")
    a2ai = l.dout("pattn")
    ps0 = l.dout("ps0") if l.mode == "unfused" else None
    cs2d = l.din("cs2")
    ss2d = l.din("ss2")
    cs2 = mk_sbuf(nc, stack, "acs2", [128, 512], F32)
    ss2 = mk_sbuf(nc, stack, "ass2", [128, 512], F32)
    tri, ev_tri = l.const("tri", BF16, stack)
    ones1, ev_o1 = l.const("ones1", F32)
    onesb = mk_sbuf(nc, stack, "aob", [128, 1], BF16)
    ev_ob = s.emit("vector", lambda e: e.memset(onesb[:, :], 1.0), sig=True)

    wq = mk_sbuf(nc, stack, "awq", [128, NCH * 256], BF16)
    wk = mk_sbuf(nc, stack, "awk", [128, NCH * 256], BF16)
    wv = mk_sbuf(nc, stack, "awv", [128, NCH * 256], BF16)
    wpj = mk_sbuf(nc, stack, "awpj", [128, 2 * NCH * 128], BF16)
    _wqs, _wq_q, _wq_d = l.wget("wqhT")
    _wks, _wk_q, _wk_d = l.wget("wkhT")
    _wvs, _wv_q, _wv_d = l.wget("wvhT")
    _wps, _wp_q, _wp_d = l.wget("wprojhT")
    ev_wq = s.dma(_wq_q, wq[:, :].rearrange("p (ec f) -> p ec f", f=256),
                  _wqs[:, :].rearrange("(ec p) f -> p ec f", p=128), deps=_wq_d)
    ev_wk = s.dma(_wk_q, wk[:, :].rearrange("p (ec f) -> p ec f", f=256),
                  _wks[:, :].rearrange("(ec p) f -> p ec f", p=128), deps=_wk_d)
    ev_wv = s.dma(_wv_q, wv[:, :].rearrange("p (ec f) -> p ec f", f=256),
                  _wvs[:, :].rearrange("(ec p) f -> p ec f", p=128), deps=_wv_d)
    ev_wpj = s.dma(_wp_q, wpj[:, :].rearrange("p (ec f) -> p ec f", f=DM),
                   _wps[:, :].rearrange("(ec p) f -> p ec f", p=128), deps=_wp_d)

    Q = [mk_sbuf(nc, stack, f"aQ{i}", [128, TT], BF16) for i in range(2)]
    K = [mk_sbuf(nc, stack, f"aK{i}", [128, TT], BF16) for i in range(2)]
    V = [mk_sbuf(nc, stack, f"aV{i}", [128, TT], BF16) for i in range(2)]
    xg = [[mk_sbuf(nc, stack, f"axg{sb_}_{i}", [128, 512], BF16) for i in range(NCH)] for sb_ in range(2)]
    qf = mk_sbuf(nc, stack, "aqf", [128, 512], F32)
    qsw = mk_sbuf(nc, stack, "aqsw", [128, 512], F32)
    r1 = mk_sbuf(nc, stack, "ar1", [128, 512], F32)
    r2 = mk_sbuf(nc, stack, "ar2", [128, 512], F32)
    P = [mk_sbuf(nc, stack, f"apb{i}", [128, 512], BF16) for i in range(2)]
    lrow = mk_sbuf(nc, stack, "alr", [1, 512], F32)
    lb = mk_sbuf(nc, stack, "alb", [128, 512], F32)
    osb = [mk_sbuf(nc, stack, f"aosb{i}", [128, TT], BF16) for i in range(2)]
    pout = [mk_sbuf(nc, stack, f"apo{i}", [128, 512], F32) for i in range(4)]
    pout_dma = [None, None, None, None]

    xg_reads = [[None] * NCH, [None] * NCH]
    osb_reads = [None, None]
    rope_last = None
    cs_loaded = [None, None]
    lrow_read = None
    it = 0
    itp = 0
    sidx = 0
    sfree = [ps.free_ev[i] for i in range(6)]
    o_free = ps.free_ev[6]
    l_free = ps.free_ev[7]
    bank_o = ps.banks[6]
    bank_l = ps.banks[7]
    pat_dep = []

    for b in range(2):
        qkv_done = []
        for r in range(8):
            sl = r % 2
            ev_cs = s.dma("sync", cs2[:, :], cs2d[:, r * 512:(r + 1) * 512], deps=(rope_last,))
            ev_ss = s.dma("sync", ss2[:, :], ss2d[:, r * 512:(r + 1) * 512], deps=(rope_last,))
            xg_ev = [None] * NCH
            for ec in range(NCH):
                xg_ev[ec] = s.dma("sync", xg[sl][ec][:, :],
                                  xng[(r * NCH + ec) * 128:(r * NCH + ec + 1) * 128, b * 512:(b + 1) * 512],
                                  deps=list(l.ldeps("xng")) + [xg_reads[sl][ec]])
            last_mm = None
            for hl in range(2):
                for which, wbuf_, wev, dst in (("q", wq, ev_wq, Q[hl]), ("k", wk, ev_wk, K[hl])):
                    bi, bank, bev = ps.acquire()
                    mm = None
                    for ec in range(NCH):
                        mm = s.emit("tensor", (lambda e, bank=bank, w=wbuf_, ec=ec, hl=hl, sl=sl,
                                               st=(ec == 0), sp=(ec == NCH - 1):
                                               e.matmul(bank[:, :], w[:, ec * 256 + hl * 128: ec * 256 + (hl + 1) * 128],
                                                        xg[sl][ec][:, :], start=st, stop=sp)),
                                      deps=[xg_ev[ec], wev] + ([bev] if ec == 0 else []),
                                      sig=(True if ec == NCH - 1 else None))
                    last_mm = mm
                    c_dst = r * 512
                    ev0 = s.emit("scalar", lambda e, bank=bank: e.copy(qf[:, :], bank[:, :]),
                                 deps=[mm, rope_last], sig=True)
                    ps.release(bi, ev0)
                    ev1 = s.emit("vector", lambda e: e.tensor_copy(qsw[0:64, :], qf[64:128, :]), deps=[ev0], sig=True)
                    ev2 = s.emit("vector", lambda e: e.tensor_copy(qsw[64:128, :], qf[0:64, :]), deps=[ev0], sig=True)
                    ev3 = s.emit("vector", (lambda e: e.tensor_mul(r1[:, :], qf[:, :], cs2[:, :])),
                                 deps=[ev0, ev_cs], sig=True)
                    ev4 = s.emit("vector", (lambda e: e.tensor_mul(r2[:, :], qsw[:, :], ss2[:, :])),
                                 deps=[ev1, ev2, ev_ss], sig=True)
                    evr = s.emit("vector", (lambda e, dst=dst, c_dst=c_dst:
                                            e.tensor_add(dst[:, c_dst:c_dst + 512], r1[:, :], r2[:, :])),
                                 deps=[ev3, ev4], sig=True)
                    rope_last = evr
                    qkv_done.append(evr)
            for ts in range(4):
                bi, bank, bev = ps.acquire()
                mm = None
                for ec in range(NCH):
                    mm = s.emit("tensor", (lambda e, bank=bank, ec=ec, ts=ts, sl=sl,
                                           st=(ec == 0), sp=(ec == NCH - 1):
                                           e.matmul(bank[:, 0:256], xg[sl][ec][:, ts * 128:(ts + 1) * 128],
                                                    wv[:, ec * 256:(ec + 1) * 256], start=st, stop=sp)),
                              deps=[xg_ev[ec], ev_wv] + ([bev] if ec == 0 else []),
                              sig=(True if ec == NCH - 1 else None))
                last_mm = mm
                tt_i = r * 4 + ts
                ev_v0 = s.emit("vector", (lambda e, bank=bank, tt_i=tt_i:
                                          e.tensor_copy(V[0][:, tt_i * 128:(tt_i + 1) * 128], bank[:, 0:128])),
                               deps=[mm], sig=True)
                ev_v1 = s.emit("vector", (lambda e, bank=bank, tt_i=tt_i:
                                          e.tensor_copy(V[1][:, tt_i * 128:(tt_i + 1) * 128], bank[:, 128:256])),
                               deps=[mm], sig=True)
                ps.release(bi, ev_v1)
                qkv_done.append(ev_v0)
                qkv_done.append(ev_v1)
            for ec in range(NCH):
                xg_reads[sl][ec] = last_mm

        for hl in range(2):
            for qg in range(8):
                pdep = [None, None]
                l_ev = av_ev = None
                ktmax = 4 * qg + 3

                def emit_qk(kt, hl=hl, qg=qg, first=False):
                    nonlocal_sidx = state_sidx[0]
                    bi_s = nonlocal_sidx % 6
                    state_sidx[0] = nonlocal_sidx + 1
                    bank_s = ps.banks[bi_s]
                    bev_s = sfree[bi_s]
                    w0 = max(0, (kt - 4 * qg) * 128)
                    mm = s.emit("tensor", (lambda e, bank_s=bank_s, hl=hl, kt=kt, qg=qg, w0=w0:
                                           e.matmul(bank_s[:, w0:512], K[hl][:, kt * 128:(kt + 1) * 128],
                                                    Q[hl][:, qg * 512 + w0:(qg + 1) * 512],
                                                    start=True, stop=True)),
                                deps=(qkv_done + [bev_s]) if first else [bev_s], sig=True)
                    return (mm, bi_s, bank_s, w0)

                state_sidx = [sidx]
                pend = [emit_qk(0, first=(qg == 0 and hl == 0))]
                if ktmax >= 1:
                    pend.append(emit_qk(1))
                for kt in range(ktmax + 1):
                    mm_s, bi_s, bank_s, w0 = pend.pop(0)
                    if kt + 2 <= ktmax:
                        pend.append(emit_qk(kt + 2))
                    psl = kt % 2
                    e_exp = s.emit("scalar", (lambda e, bank_s=bank_s, psl=psl, w0=w0:
                                              e.activation(P[psl][:, w0:512], bank_s[:, w0:512], AF.Exp, scale=RSD)),
                                   deps=[mm_s, pdep[psl]], sig=True)
                    sfree[bi_s] = e_exp
                    p_ready = e_exp
                    if kt >= 4 * qg:
                        qi = kt - 4 * qg
                        p_ready = s.emit("vector", (lambda e, psl=psl, qi=qi:
                                                    e.tensor_mul(P[psl][:, qi * 128:(qi + 1) * 128],
                                                                 P[psl][:, qi * 128:(qi + 1) * 128], tri[:, :])),
                                         deps=[e_exp, ev_tri], sig=True)
                    l_ev = s.emit("tensor", (lambda e, psl=psl, w0=w0,
                                             st=(kt == 0), sp=(kt == ktmax):
                                             e.matmul(bank_l[0:1, w0:512], onesb[:, :], P[psl][:, w0:512],
                                                      start=st, stop=sp)),
                                  deps=[p_ready, ev_ob] + ([l_free] if kt == 0 else []), sig=True)
                    av_ev = s.emit("tensor", (lambda e, hl=hl, kt=kt, psl=psl, w0=w0,
                                              st=(kt == 0), sp=(kt == ktmax):
                                              e.matmul(bank_o[:, w0:512], V[hl][:, kt * 128:(kt + 1) * 128],
                                                       P[psl][:, w0:512], start=st, stop=sp)),
                                   deps=[p_ready] + ([o_free] if kt == 0 else []), sig=True)
                    pdep[psl] = av_ev
                sidx = state_sidx[0]
                e_li = s.emit("vector", lambda e: e.reciprocal(lrow[0:1, :], bank_l[0:1, :]),
                              deps=[l_ev, lrow_read], sig=True)
                l_free = e_li
                bi_b = sidx % 6
                sidx += 1
                bank_b = ps.banks[bi_b]
                eb = s.emit("tensor", lambda e, bank_b=bank_b: e.matmul(bank_b[:, :], ones1[:, :], lrow[0:1, :],
                                                                        start=True, stop=True),
                            deps=[e_li, sfree[bi_b], ev_o1], sig=True)
                lrow_read = eb
                ec_ = s.emit("scalar", lambda e, bank_b=bank_b: e.copy(lb[:, :], bank_b[:, :]),
                             deps=[eb, o_free], sig=True)
                sfree[bi_b] = ec_
                eo = s.emit("vector", (lambda e, hl=hl, qg=qg:
                                       e.tensor_mul(osb[hl][:, qg * 512:(qg + 1) * 512], bank_o[:, :], lb[:, :])),
                            deps=[av_ev, ec_, osb_reads[hl]], sig=True)
                o_free = eo
                if l.mode == "unfused":
                    dev = s.dma("sync", a2ai[hl * 128:(hl + 1) * 128, b * TT + qg * 512: b * TT + (qg + 1) * 512],
                                osb[hl][:, qg * 512:(qg + 1) * 512], deps=[eo])
                    l.store("pattn", dev)
                pat_dep.append(eo)
                it += 1

        proj_reads = []
        for qg in range(8):
            for oc in range(NCH):
                bi, bank, bev = ps.acquire()
                mm = None
                for ec in range(2):
                    mm = s.emit("tensor", (lambda e, bank=bank, oc=oc, ec=ec, qg=qg,
                                           st=(ec == 0), sp=(ec == 1):
                                           e.matmul(bank[:, :], wpj[:, ec * NCH * 128 + oc * 128: ec * NCH * 128 + (oc + 1) * 128],
                                                    osb[ec][:, qg * 512:(qg + 1) * 512], start=st, stop=sp)),
                              deps=[ev_wpj, pat_dep[-1]] + ([bev] if ec == 0 else []),
                              sig=(True if ec == 1 else None))
                slp = itp % 4
                eev = s.emit("scalar", (lambda e, bank=bank, slp=slp: e.copy(pout[slp][:, :], bank[:, :])),
                             deps=[mm, pout_dma[slp]], sig=True)
                ps.release(bi, eev)
                if l.mode == "fused":
                    psb = l.dout("ps0a") if b == 0 else l.dout("ps0b")
                    dev = s.dma("sync", psb[qg * DM + oc * 128: qg * DM + (oc + 1) * 128, :],
                                pout[slp][:, :], deps=[eev])
                    l.store("ps0a" if b == 0 else "ps0b", dev)
                else:
                    dev = s.dma("sync", ps0[qg * DM + oc * 128: qg * DM + (oc + 1) * 128, b * 512:(b + 1) * 512],
                                pout[slp][:, :], deps=[eev])
                    l.store("ps0", dev)
                pout_dma[slp] = dev
                itp += 1
                proj_reads.append(mm)
        osb_reads = [proj_reads[-1], proj_reads[-1]]
        if getattr(l, "rs_hook", None) is not None:
            l.rs_hook(b)
    for i in range(6):
        ps.free_ev[i] = sfree[i]
    ps.free_ev[6] = o_free
    ps.free_ev[7] = l_free
    ps.rr = 0
    for d in pat_dep:
        l.store("pattn", d)
    close_stack(stack)


def emit_cmb0(l):
    stack = []
    emit_cmb(l, "xT", "s0", "g0", "x1", "xn1", stack)
    close_stack(stack)


def emit_cmb(l, x_name, s_name, g_name, out_x, out_xn, stack=None):
    s, nc = l.s, l.nc
    own_stack = stack is None
    if own_stack:
        stack = []
    xT = l.din(x_name)
    sT = l.din(s_name)
    gT = l.din(g_name)
    x_t = [mk_sbuf(nc, stack, f"cx{i}", [128, COLS], F32) for i in range(NCH)]
    tb = [mk_sbuf(nc, stack, f"ct{i}", [128, COLS], F32) for i in range(2)]
    gb = [mk_sbuf(nc, stack, f"cg{i}", [128, COLS], F32) for i in range(2)]
    x_ev = []
    prev = [None, None]
    out_dram = l.dout(out_x) if out_x else None
    for i in range(NCH):
        sl = i % 2
        e1 = s.dma("sync", x_t[i][:, :], xT[i * 128:(i + 1) * 128, :], deps=l.ldeps(x_name))
        e2 = s.dma("sync", tb[sl][:, :], sT[i * 128:(i + 1) * 128, :], deps=list(l.ldeps(s_name)) + [prev[sl]])
        e3 = s.dma("sync", gb[sl][:, :], gT[i * 128:(i + 1) * 128, :], deps=list(l.ldeps(g_name)) + [prev[sl]])
        e4 = s.emit("vector", (lambda e, sl=sl: e.tensor_mul(tb[sl][:, :], tb[sl][:, :], gb[sl][:, :])),
                    deps=[e2, e3], sig=True)
        e5 = s.emit("vector", (lambda e, i=i, sl=sl: e.tensor_add(x_t[i][:, :], x_t[i][:, :], tb[sl][:, :])),
                    deps=[e1, e4], sig=True)
        prev[sl] = e5
        x_ev.append(e5)
        if out_dram is not None:
            dev = s.dma("sync", out_dram[i * 128:(i + 1) * 128, :], x_t[i][:, :], deps=[e5])
            l.store(out_x, dev)
    if out_xn:
        xn_t = [mk_sbuf(nc, stack, f"cxn{i}", [128, COLS], BF16) for i in range(NCH)]
        emit_ln(l, x_t, x_ev, out_xn, xn_t, stack, "cln")
    if own_stack:
        close_stack(stack)


def emit_cmb1(l):
    stack = []
    emit_cmb(l, "x1", "s1", "g1", "outT", None, stack)
    close_stack(stack)


def emit_ffn0(l, upto="all", phase="all", outer_stack=None):
    s, nc, ps = l.s, l.nc, l.ps
    stack = []
    ostack = outer_stack if outer_stack is not None else stack
    xn0 = l.din("xn0")
    g0 = l.dout("g0")
    y1 = l.dscratch("y1d", (128, 16 * COLS), BF16)
    y2 = l.dscratch("y2d", (128, 64 * COLS), BF16)
    h128, ev_h = l.const("h128", BF16, ostack)
    a2, ev_a2 = l.const("a2", BF16, ostack)
    scal = mk_sbuf(nc, ostack, "fscal", [128, 208], F32)
    z1 = [mk_sbuf(nc, ostack, f"fz{i}", [128, COLS], BF16) for i in range(NCH)]
    l.ffn_ub = mk_sbuf(nc, ostack, "fub", [128, NCH * 128 * 2], BF16)
    l.ffn_gb = mk_sbuf(nc, ostack, "fgb", [128, NCH * 128 * 2], BF16)
    l.ffn_pref = {}
    l.ffn_h128 = h128
    l.ffn_a2 = (a2, ev_a2)
    a1, ev_a1 = l.const("a1", BF16, stack)
    su_ug, ev_s1 = l.const("su_ug", F32, stack)
    su_dp, ev_s2 = l.const("su_dp", F32, stack)
    ws_up, ev_s3 = l.const("ws_up", F32, stack)
    ws_gate, ev_s4 = l.const("ws_gate", F32, stack)
    ws_dp, ev_s5 = l.const("ws_dp", F32, stack)
    es1 = s.emit("vector", lambda e: e.tensor_scalar_mul(scal[:, 0:64], su_dp[:, :], FW2), deps=[ev_s2], sig=True)
    es2 = s.emit("vector", lambda e: e.tensor_scalar_mul(scal[:, 64:128], ws_up[:, :], 64.0), deps=[ev_s3], sig=True)
    es3 = s.emit("vector", lambda e: e.tensor_scalar_mul(scal[:, 128:192], ws_gate[:, :], 64.0), deps=[ev_s4], sig=True)
    es4 = s.emit("vector", lambda e: e.tensor_scalar_mul(scal[:, 192:208], ws_dp[:, :], 64.0), deps=[ev_s5], sig=True)

    xn_t = [mk_sbuf(nc, stack, f"fx{i}", [128, COLS], BF16) for i in range(NCH)]
    xn_ev = [s.dma("sync", xn_t[i][:, :], xn0[i * 128:(i + 1) * 128, :], deps=l.ldeps("xn0")) for i in range(NCH)]
    xs_t = xn_t
    xs_ev = [s.emit("vector", (lambda e, i=i: e.tensor_scalar(xs_t[i][:, :], xn_t[i][:, :],
                                                              su_ug[:, i:i + 1], FW1, ALU.mult, ALU.mult)),
                    deps=[xn_ev[i], ev_s1], sig=True) for i in range(NCH)]

    if upto == "xs":
        for jp in range(16):
            dev = s.dma("gpsimd", g0[jp * 128:(jp + 1) * 128, :], xs_t[jp][:, :], deps=xs_ev)
            l.store("g0", dev)
        close_stack(stack)
        return
    ybf = [mk_sbuf(nc, stack, f"fyb{i}", [128, 512], BF16) for i in range(2)]
    ybf_dma = [None, None]
    it = 0
    y1_evs = []
    for g in range(NCH):
        for tg in range(2):
            bi, bank, bev = ps.acquire()
            mm = s.emit("tensor", (lambda e, b_=bank, g=g, tg=tg:
                                   e.matmul(b_[:, :], h128[:, :], xs_t[g][:, tg * 512:(tg + 1) * 512],
                                            start=True, stop=True)),
                        deps=[xs_ev[g], ev_h, bev], sig=True)
            sl = it % 2
            eev = s.emit("scalar", lambda e, b_=bank, sl=sl: e.copy(ybf[sl][:, :], b_[:, :]),
                         deps=[mm, ybf_dma[sl]], sig=True)
            ps.release(bi, eev)
            dev = s.dma("sync", y1[:, g * COLS + tg * 512: g * COLS + tg * 512 + 512], ybf[sl][:, :], deps=[eev])
            ybf_dma[sl] = dev
            y1_evs.append(dev)
            it += 1

    if upto == "y1":
        for g in range(16):
            dev = s.dma("gpsimd", g0[g * 128:(g + 1) * 128, :].rearrange("p t -> p t"),
                        y1[:, g * COLS:(g + 1) * COLS], deps=y1_evs)
            l.store("g0", dev)
        close_stack(stack)
        return
    y1f = y1.rearrange("r c -> (r c)")
    x2_t = z1
    x2_ev = []
    for jp in range(16):
        src = bass.AP(y1f.tensor, jp * (16 * COLS), [[16 * (16 * COLS), 8], [COLS, 16], [1, COLS]])
        dst = z1[jp][:, :]
        zev = s.dma("sync", dst, src, deps=y1_evs)
        if upto == "z1":
            dev = s.dma("gpsimd", g0[jp * 128:(jp + 1) * 128, :], z1[jp][:, :], deps=[zev])
            l.store("g0", dev)
            continue
        for tg in range(2):
            bi, bank, bev = ps.acquire()
            mm = s.emit("tensor", (lambda e, b_=bank, jp=jp, tg=tg:
                                   e.matmul(b_[:, :], a1[:, :], z1[jp][:, tg * 512:(tg + 1) * 512],
                                            start=True, stop=True)),
                        deps=[zev, ev_a1, bev], sig=True)
            eev = s.emit("scalar", (lambda e, b_=bank, jp=jp, tg=tg:
                                    e.copy(x2_t[jp][:, tg * 512:(tg + 1) * 512], b_[:, :])), deps=[mm], sig=True)
            ps.release(bi, eev)
            x2_ev.append(eev)
    if upto == "x2":
        for jp in range(16):
            dev = s.dma("gpsimd", g0[jp * 128:(jp + 1) * 128, :], x2_t[jp][:, :], deps=x2_ev)
            l.store("g0", dev)
        close_stack(stack)
        return

    if upto == "z1":
        close_stack(stack)
        return
    if phase == "head":
        close_stack(stack)
        l.ffn_state = (x2_t, x2_ev, scal, es1, es2, es3, es4)
        return
    _ffn0_tail(l, x2_t, x2_ev, scal, es1, es2, es3, es4, stack)


def emit_ffn0_tail(l):
    (x2_t, x2_ev, scal, es1, es2, es3, es4) = l.ffn_state
    _ffn0_tail(l, x2_t, x2_ev, scal, es1, es2, es3, es4, [])
    l.ffn_state = None


def _ffn0_tail(l, x2_t, x2_ev, scal, es1, es2, es3, es4, stack, upto="all"):
    s, nc, ps = l.s, l.nc, l.ps
    g0 = l.dout("g0")
    y2 = l.dscratch("y2d", (128, 64 * COLS), BF16)
    h128 = l.ffn_h128
    a2, ev_a2 = l.ffn_a2
    ybf = [mk_sbuf(nc, stack, f"fybT{i}", [128, 512], BF16) for i in range(2)]
    ybf_dma = [None, None]
    xhb = [mk_sbuf(nc, stack, f"fxh2{i}", [128, 512], BF16) for i in range(2)]
    pend_xh = []

    def _flush_xh(item):
        fi, tg, xsl, e_xh = item
        bi2, bank2, bev2 = ps.acquire()
        mm2 = s.emit("tensor", (lambda e, b_=bank2, xsl=xsl:
                                e.matmul(b_[:, :], h128[:, :], xhb[xsl][:, :], start=True, stop=True)),
                     deps=[e_xh, bev2], sig=True)
        st2["xh"][xsl] = mm2
        sl2 = (fi * 2 + tg) % 2
        eev = s.emit("scalar", (lambda e, b_=bank2, sl2=sl2: e.copy(ybf[sl2][:, :], b_[:, :])),
                     deps=[mm2, ybf_dma[sl2]], sig=True)
        ps.release(bi2, eev)
        dev = s.dma("sync", y2[:, fi * COLS + tg * 512: fi * COLS + tg * 512 + 512], ybf[sl2][:, :], deps=[eev])
        ybf_dma[sl2] = dev
        y2_evs[tg].append(dev)

    wup, wq_q, wup_d = l.wget("wupT")
    wgt, wg_q, wgt_d = l.wget("wgateT")
    NS = 2
    ub = l.ffn_ub
    gb = l.ffn_gb
    ub_ev, gb_ev = [None] * NS, [None] * NS
    sgb = mk_sbuf(nc, stack, "fsgb", [128, 512], BF16)
    xub = mk_sbuf(nc, stack, "fxub", [128, 512], BF16)
    y2_evs = [[], []]
    st2 = {"xh": [None, None]}
    for fi in range(64):
        slot = fi % NS
        if ("u", fi) in l.ffn_pref:
            ulev = l.ffn_pref[("u", fi)]
            glev = l.ffn_pref[("g", fi)]
        else:
            usrc = wup[:, fi * 128:(fi + 1) * 128].rearrange("(ec p) f -> p ec f", p=128)
            udst = ub[:, slot * NCH * 128:(slot + 1) * NCH * 128].rearrange("p (ec f) -> p ec f", f=128)
            ulev = s.dma(wq_q, udst, usrc, deps=(ub_ev[slot],) + wup_d)
            gsrc = wgt[:, fi * 128:(fi + 1) * 128].rearrange("(ec p) f -> p ec f", p=128)
            gdst = gb[:, slot * NCH * 128:(slot + 1) * NCH * 128].rearrange("p (ec f) -> p ec f", f=128)
            glev = s.dma(wg_q, gdst, gsrc, deps=(gb_ev[slot],) + wgt_d)
        mm_u = mm_g = None
        for tg in range(2):
            bi_u, bank_u, bev_u = ps.acquire()
            bi_g, bank_g, bev_g = ps.acquire()
            for ec in range(NCH):
                mm_u = s.emit("tensor", (lambda e, b_=bank_u, ec=ec, slot=slot, tg=tg,
                                         st=(ec == 0), sp=(ec == NCH - 1):
                                         e.matmul(b_[:, :], ub[:, slot * NCH * 128 + ec * 128: slot * NCH * 128 + (ec + 1) * 128],
                                                  x2_t[ec][:, tg * 512:(tg + 1) * 512], start=st, stop=sp)),
                              deps=[ulev] + (x2_ev + [bev_u] if ec == 0 else []),
                              sig=(True if ec == NCH - 1 else None))
            for ec in range(NCH):
                mm_g = s.emit("tensor", (lambda e, b_=bank_g, ec=ec, slot=slot, tg=tg,
                                         st=(ec == 0), sp=(ec == NCH - 1):
                                         e.matmul(b_[:, :], gb[:, slot * NCH * 128 + ec * 128: slot * NCH * 128 + (ec + 1) * 128],
                                                  x2_t[ec][:, tg * 512:(tg + 1) * 512], start=st, stop=sp)),
                              deps=[glev] + ([bev_g] if ec == 0 else []),
                              sig=(True if ec == NCH - 1 else None))
            e_sg = s.emit("scalar", (lambda e, b_=bank_g, fi=fi:
                                     e.activation(sgb[:, :], b_[:, :], AF.Silu, scale=scal[:, 128 + fi:129 + fi])),
                          deps=[mm_g, es3] + list(st2["xh"]), sig=True)
            e_xu = s.emit("scalar", (lambda e, b_=bank_u, fi=fi:
                                     e.activation(xub[:, :], b_[:, :], AF.Copy, scale=scal[:, 64 + fi:65 + fi])),
                          deps=[mm_u, es2] + list(st2["xh"]), sig=True)
            ps.release(bi_g, e_sg)
            ps.release(bi_u, e_xu)
            xsl = (fi * 2 + tg) % 2
            e_xh = s.emit("vector", (lambda e, fi=fi, xsl=xsl:
                                     e.scalar_tensor_tensor(xhb[xsl][:, :], sgb[:, :], scal[:, fi:fi + 1], xub[:, :],
                                                            ALU.mult, ALU.mult)),
                          deps=[e_sg, e_xu, es1, st2["xh"][xsl]], sig=True)
            pend_xh.append((fi, tg, xsl, e_xh))
            if len(pend_xh) > 1:
                _flush_xh(pend_xh.pop(0))
        ub_ev[slot] = mm_u
        gb_ev[slot] = mm_g

    if upto == "y2":
        for fi in range(16):
            dev = s.dma("gpsimd", g0[fi * 128:(fi + 1) * 128, :], y2[:, fi * COLS:(fi + 1) * COLS],
                        deps=y2_evs[0] + y2_evs[1])
            l.store("g0", dev)
        close_stack(stack)
        return
    while pend_xh:
        _flush_xh(pend_xh.pop(0))
    wdn, wd_q, wdn_d = l.wget("wdownT")
    y2f = y2.rearrange("r c -> (r c)")
    z2 = [mk_sbuf(nc, stack, f"fz2_{i}", [128, 512], BF16) for i in range(64)]
    dnb = mk_sbuf(nc, stack, "fdnb", [128, 64 * 128 * 2], BF16)
    dnb_ev = [None, None]
    g0b = [mk_sbuf(nc, stack, f"fg0b{i}", [128, 512], F32) for i in range(2)]
    g0b_dma = [None, None]
    it2 = 0
    prev_down_mm = []
    for th in range(2):
        z_ev = []
        for jp in range(64):
            src = bass.AP(y2f.tensor, jp * (64 * COLS) + th * 512, [[64 * (64 * COLS), 2], [COLS, 64], [1, 512]])
            dst = z2[jp][:, :]
            zev = s.dma("sync", dst, src, deps=list(y2_evs[th]) + prev_down_mm)
            bi, bank, bev = ps.acquire()
            mm = s.emit("tensor", (lambda e, b_=bank, jp=jp:
                                   e.matmul(b_[:, :], a2[:, :], z2[jp][:, :], start=True, stop=True)),
                        deps=[zev, ev_a2, bev], sig=True)
            eev = s.emit("scalar", (lambda e, b_=bank, jp=jp: e.copy(z2[jp][:, :], b_[:, :])), deps=[mm], sig=True)
            ps.release(bi, eev)
            z_ev.append(eev)
        for oc in range(NCH):
            slot = oc % 2
            src = wdn[:, oc * 128:(oc + 1) * 128].rearrange("(ec p) f -> p ec f", p=128)
            dst = dnb[:, slot * 64 * 128:(slot + 1) * 64 * 128].rearrange("p (ec f) -> p ec f", f=128)
            lev = s.dma(wd_q, dst, src, deps=(dnb_ev[slot],) + wdn_d)
            bi, bank, bev = ps.acquire()
            mm = None
            for jp in range(64):
                mm = s.emit("tensor", (lambda e, b_=bank, jp=jp, slot=slot, st=(jp == 0), sp=(jp == 63):
                                       e.matmul(b_[:, :], dnb[:, slot * 64 * 128 + jp * 128: slot * 64 * 128 + (jp + 1) * 128],
                                                z2[jp][:, :], start=st, stop=sp)),
                             deps=[lev] + (z_ev + [bev] if jp == 0 else []),
                             sig=(True if jp == 63 else None))
            dnb_ev[slot] = mm
            if oc == NCH - 1:
                prev_down_mm = [mm]
            sl2 = it2 % 2
            eev = s.emit("scalar", (lambda e, b_=bank, oc=oc, sl2=sl2:
                                    e.activation(g0b[sl2][:, :], b_[:, :], AF.Silu, scale=scal[:, 192 + oc:193 + oc])),
                         deps=[mm, es4, g0b_dma[sl2]], sig=True)
            ps.release(bi, eev)
            dev = s.dma("sync", g0[oc * 128:(oc + 1) * 128, th * 512:(th + 1) * 512], g0b[sl2][:, :], deps=[eev])
            g0b_dma[sl2] = dev
            l.store("g0", dev)
            it2 += 1
    close_stack(stack)


def emit_moe(l, upto="all"):
    s, nc, ps = l.s, l.nc, l.ps
    stack = []
    xn1 = l.din("xn1")
    g1 = l.dout("g1")
    ident, ev_id = l.const("ident", F32, stack)
    ones1, ev_o1 = l.const("ones1", F32)
    xn_t = [mk_sbuf(nc, stack, f"mx{i}", [128, COLS], BF16) for i in range(NCH)]
    xn_ev = [s.dma("sync", xn_t[i][:, :], xn1[i * 128:(i + 1) * 128, :], deps=l.ldeps("xn1")) for i in range(NCH)]

    gateT = l.din("gateT")
    comb_b = [mk_sbuf(nc, stack, f"mcb{e}", [128, COLS], BF16) for e in range(8)]
    rstack = []
    gt_sb = mk_sbuf(nc, rstack, "mgt", [128, NCH * 8], F32)
    ev_gt = s.dma("sync", gt_sb[:, :].rearrange("p (ec f) -> p ec f", f=8),
                  gateT[:, :].rearrange("(ec p) f -> p ec f", p=128))
    xnf = [mk_sbuf(nc, rstack, f"mxf{i}", [128, 512], F32) for i in range(2)]
    xnf_mm = [None, None]
    lgT = mk_sbuf(nc, rstack, "mlg", [8, COLS], F32)
    lg_ev = []
    for tg in range(2):
        bi, bank, bev = ps.acquire()
        mm = None
        for ec in range(NCH):
            sl = ec % 2
            xev = s.dma("gpsimd", xnf[sl][:, :], xn1[ec * 128:(ec + 1) * 128, tg * 512:(tg + 1) * 512],
                        deps=list(l.ldeps("xn1")) + [xnf_mm[sl]])
            mm = s.emit("tensor", (lambda e, bank=bank, sl=sl, ec=ec, st=(ec == 0), sp=(ec == NCH - 1):
                                   e.matmul(bank[0:8, :], gt_sb[:, ec * 8:(ec + 1) * 8], xnf[sl][:, :],
                                            start=st, stop=sp)),
                        deps=[xev, ev_gt] + ([bev] if ec == 0 else []), sig=True)
            xnf_mm[sl] = mm
        eev = s.emit("scalar", (lambda e, bank=bank, tg=tg: e.copy(lgT[:, tg * 512:(tg + 1) * 512], bank[0:8, :])),
                     deps=[mm], sig=True)
        ps.release(bi, eev)
        lg_ev.append(eev)

    if upto == "logits":
        dev = s.dma("sync", g1[0:8, :], lgT[:, :], deps=lg_ev)
        l.store("g1", dev)
        close_stack(stack)
        return
    lgm = mk_sbuf(nc, rstack, "mlgm", [128, 8], F32)
    scr = mk_sbuf(nc, rstack, "mscr", [128, 16], F32)
    eqs = mk_sbuf(nc, rstack, "meq", [128, 24], F32)
    combT = mk_sbuf(nc, rstack, "mcbT", [8, COLS], F32)
    comb_b_ev = [None] * 8
    crow = mk_sbuf(nc, rstack, "mcrow", [1, COLS], F32)
    cb_evs = []
    prev_route = None
    for cb_i in range(8):
        bi, bank, bev = ps.acquire()
        et = s.emit("tensor", (lambda e, bank=bank, cb_i=cb_i:
                               e.transpose(bank[0:128, 0:8], lgT[0:8, cb_i * 128:(cb_i + 1) * 128], ident[0:8, 0:8])),
                    deps=lg_ev + [ev_id, bev], sig=True)
        e0 = s.emit("scalar", lambda e, bank=bank: e.copy(lgm[:, :], bank[0:128, 0:8]), deps=[et, prev_route], sig=True)
        ps.release(bi, e0)
        if upto == "lgm":
            dev = s.dma("sync", g1[cb_i * 128:(cb_i + 1) * 128, 0:8], lgm[:, :], deps=[e0])
            l.store("g1", dev)
            prev_route = dev
            continue
        m1, m2, m1n = scr[:, 0:1], scr[:, 1:2], scr[:, 2:3]
        Z, Zi, w2, e2m = scr[:, 3:4], scr[:, 4:5], scr[:, 5:6], scr[:, 6:7]
        ex = scr[:, 8:16]
        e1 = s.emit("vector", lambda e: e.reduce_max(m1, lgm[:, :], axis=AX), deps=[e0], sig=True)
        e2a = s.emit("vector", lambda e: e.tensor_tensor(eqs[:, 0:8], lgm[:, :], m1.broadcast_to([128, 8]), op=ALU.subtract),
                     deps=[e1], sig=True)
        e2b = s.emit("vector", lambda e: e.tensor_mul(eqs[:, 0:8], eqs[:, 0:8], eqs[:, 0:8]), deps=[e2a], sig=True)
        e2c = s.emit("vector", lambda e: e.tensor_scalar_mul(eqs[:, 0:8], eqs[:, 0:8], 1.0e30),
                     deps=[e2b], sig=True)
        e2d = s.emit("vector", lambda e: e.tensor_scalar_min(eqs[:, 0:8], eqs[:, 0:8], 1.0),
                     deps=[e2c], sig=True)
        e2e = s.emit("vector", lambda e: e.tensor_scalar_mul(eqs[:, 0:8], eqs[:, 0:8], -1.0), deps=[e2d], sig=True)
        e2 = s.emit("vector", lambda e: e.tensor_scalar_add(eqs[:, 0:8], eqs[:, 0:8], 1.0), deps=[e2e], sig=True)
        e3 = s.emit("vector", lambda e: e.scalar_tensor_tensor(eqs[:, 16:24], eqs[:, 0:8], -1.0e30, lgm[:, :],
                                                               ALU.mult, ALU.add), deps=[e2], sig=True)
        e4 = s.emit("vector", lambda e: e.reduce_max(m2, eqs[:, 16:24], axis=AX), deps=[e3], sig=True)
        e5 = s.emit("vector", lambda e: e.tensor_scalar_mul(m1n, m1, -1.0), deps=[e1], sig=True)
        e6 = s.emit("scalar", lambda e: e.activation(ex, lgm[:, :], AF.Exp, bias=m1n), deps=[e5, e0], sig=True)
        e7 = s.emit("vector", lambda e: e.reduce_sum(Z, ex, axis=AX), deps=[e6], sig=True)
        e8 = s.emit("vector", lambda e: e.reciprocal(Zi, Z), deps=[e7], sig=True)
        e9 = s.emit("scalar", lambda e: e.activation(e2m, m2, AF.Exp, bias=m1n), deps=[e4, e5], sig=True)
        e10 = s.emit("vector", lambda e: e.tensor_mul(w2, e2m, Zi), deps=[e8, e9], sig=True)
        e11a = s.emit("vector", lambda e: e.tensor_tensor(eqs[:, 8:16], lgm[:, :], m2.broadcast_to([128, 8]), op=ALU.subtract),
                      deps=[e4, e6], sig=True)
        e11b = s.emit("vector", lambda e: e.tensor_mul(eqs[:, 8:16], eqs[:, 8:16], eqs[:, 8:16]), deps=[e11a], sig=True)
        e11c = s.emit("vector", lambda e: e.tensor_scalar_mul(eqs[:, 8:16], eqs[:, 8:16], 1.0e30),
                      deps=[e11b], sig=True)
        e11d = s.emit("vector", lambda e: e.tensor_scalar_min(eqs[:, 8:16], eqs[:, 8:16], 1.0),
                      deps=[e11c], sig=True)
        e11e = s.emit("vector", lambda e: e.tensor_scalar_mul(eqs[:, 8:16], eqs[:, 8:16], -1.0), deps=[e11d], sig=True)
        e11 = s.emit("vector", lambda e: e.tensor_scalar_add(eqs[:, 8:16], eqs[:, 8:16], 1.0), deps=[e11e], sig=True)
        e12 = s.emit("vector", lambda e: e.tensor_scalar(eqs[:, 8:16], eqs[:, 8:16], w2, None, ALU.mult),
                     deps=[e10, e11], sig=True)
        e13 = s.emit("vector", lambda e: e.scalar_tensor_tensor(lgm[:, :], eqs[:, 0:8], Zi, eqs[:, 8:16],
                                                                ALU.mult, ALU.add), deps=[e8, e12, e2], sig=True)
        prev_route = e13
        if upto == "route":
            dev = s.dma("sync", g1[cb_i * 128:(cb_i + 1) * 128, 0:8], lgm[:, :], deps=[e13])
            l.store("g1", dev)
            prev_route = dev
            continue
        bi2, bank2, bev2 = ps.acquire()
        et2 = s.emit("tensor", (lambda e, bank2=bank2: e.transpose(bank2[0:8, 0:128], lgm[:, :], ident[:, :])),
                     deps=[e13, bev2], sig=True)
        ecp = s.emit("scalar", (lambda e, bank2=bank2, cb_i=cb_i:
                                e.copy(combT[:, cb_i * 128:(cb_i + 1) * 128], bank2[0:8, 0:128])),
                     deps=[et2], sig=True)
        ps.release(bi2, ecp)
        cb_evs.append(ecp)
    if upto in ("route", "lgm"):
        close_stack(stack)
        return
    for e_ in range(8):
        er = s.dma("sync", crow[0:1, :], combT[e_:e_ + 1, :],
                   deps=cb_evs + [comb_b_ev[e_ - 1] if e_ > 0 else None])
        for tg in range(2):
            bi, bank, bev = ps.acquire()
            eb = s.emit("tensor", (lambda e, bank=bank, tg=tg:
                                   e.matmul(bank[:, :], ones1[:, :], crow[0:1, tg * 512:(tg + 1) * 512],
                                            start=True, stop=True)),
                        deps=[er, bev, ev_o1], sig=True)
            ecb = s.emit("scalar", (lambda e, bank=bank, e_=e_, tg=tg:
                                    e.copy(comb_b[e_][:, tg * 512:(tg + 1) * 512], bank[:, :])), deps=[eb], sig=True)
            ps.release(bi, ecb)
            comb_b_ev[e_] = ecb

    if upto == "comb":
        for e_ in range(8):
            dev = s.dma("gpsimd", g1[e_ * 128:(e_ + 1) * 128, :], comb_b[e_][:, :], deps=[comb_b_ev[e_]])
            l.store("g1", dev)
        for tg in range(2):
            dev = s.dma("gpsimd", g1[1024 + tg * 128:1024 + (tg + 1) * 128, 0:8].rearrange("p f -> p f"),
                        lgm[:, :], deps=[prev_route])
            l.store("g1", dev)
        close_stack(stack)
        return
    close_stack(rstack)
    y_t = [mk_sbuf(nc, stack, f"my{i}", [128, COLS], F32) for i in range(NCH)]
    y_ev = [None] * NCH
    guT, gu_q, gu_d = l.wget("guT")
    mdT, md_q, md_d = l.wget("moedT")
    act_t = [mk_sbuf(nc, stack, f"ma{i}", [128, COLS], BF16) for i in range(11)]
    gub = mk_sbuf(nc, stack, "mgub", [128, NCH * 128 * 3], BF16)
    gub_ev = [None, None, None]
    sgb = [mk_sbuf(nc, stack, f"msg{i}", [128, 512], BF16) for i in range(2)]
    sg_prev = [None, None]
    dnb = mk_sbuf(nc, stack, "mdnb", [128, 11 * 128 * 3], BF16)
    dnb_ev = [None, None, None]
    for e_ in range(8):
        act_ev = [[None, None] for _ in range(11)]
        for fi in range(11):
            for part in range(2):
                fr = fi if part == 0 else 11 + fi
                slot = (fi * 2 + part) % 3
                src = guT[e_ * DM:(e_ + 1) * DM, fr * 128:(fr + 1) * 128].rearrange("(ec p) f -> p ec f", p=128) \
                    if gu_q == "sync" else guT[e_, :, fr * 128:(fr + 1) * 128].rearrange("(ec p) f -> p ec f", p=128)
                dst = gub[:, slot * NCH * 128:(slot + 1) * NCH * 128].rearrange("p (ec f) -> p ec f", f=128)
                lev = s.dma(gu_q, dst, src, deps=(gub_ev[slot],) + gu_d)
                mm = None
                for tg in range(2):
                    bi, bank, bev = ps.acquire()
                    for ec in range(NCH):
                        mm = s.emit("tensor", (lambda e, bank=bank, slot=slot, ec=ec, tg=tg,
                                               st=(ec == 0), sp=(ec == NCH - 1):
                                               e.matmul(bank[:, :],
                                                        gub[:, slot * NCH * 128 + ec * 128: slot * NCH * 128 + (ec + 1) * 128],
                                                        xn_t[ec][:, tg * 512:(tg + 1) * 512], start=st, stop=sp)),
                                      deps=[lev, xn_ev[ec]] + ([bev] if ec == 0 else []),
                                      sig=(True if ec == NCH - 1 else None))
                    if part == 0:
                        esl = s.emit("scalar", (lambda e, bank=bank, tg=tg:
                                                e.activation(sgb[tg][:, :], bank[:, :], AF.Silu)),
                                     deps=[mm, sg_prev[tg]], sig=True)
                        ps.release(bi, esl)
                        act_ev[fi][tg] = esl
                    else:
                        ea = s.emit("vector", (lambda e, bank=bank, fi=fi, tg=tg:
                                               e.tensor_mul(act_t[fi][:, tg * 512:(tg + 1) * 512], sgb[tg][:, :], bank[:, :])),
                                    deps=[mm, act_ev[fi][tg]], sig=True)
                        sg_prev[tg] = ea
                        eb2 = s.emit("vector", (lambda e, fi=fi, tg=tg, e_=e_:
                                                e.tensor_mul(act_t[fi][:, tg * 512:(tg + 1) * 512],
                                                             act_t[fi][:, tg * 512:(tg + 1) * 512],
                                                             comb_b[e_][:, tg * 512:(tg + 1) * 512])),
                                     deps=[ea, comb_b_ev[e_]], sig=True)
                        ps.release(bi, eb2)
                        act_ev[fi][tg] = eb2
                gub_ev[slot] = mm
        for oc in range(NCH):
            slot = oc % 3
            src = mdT[e_ * 1408:(e_ + 1) * 1408, oc * 128:(oc + 1) * 128].rearrange("(ec p) f -> p ec f", p=128) \
                if md_q == "sync" else mdT[e_, :, oc * 128:(oc + 1) * 128].rearrange("(ec p) f -> p ec f", p=128)
            dst = dnb[:, slot * 11 * 128:(slot + 1) * 11 * 128].rearrange("p (ec f) -> p ec f", f=128)
            lev = s.dma(md_q, dst, src, deps=(dnb_ev[slot],) + md_d)
            mm = None
            for tg in range(2):
                bi, bank, bev = ps.acquire()
                for fi in range(11):
                    mm = s.emit("tensor", (lambda e, bank=bank, slot=slot, fi=fi, tg=tg,
                                           st=(fi == 0), sp=(fi == 10):
                                           e.matmul(bank[:, :],
                                                    dnb[:, slot * 11 * 128 + fi * 128: slot * 11 * 128 + (fi + 1) * 128],
                                                    act_t[fi][:, tg * 512:(tg + 1) * 512], start=st, stop=sp)),
                                  deps=[lev, act_ev[fi][tg]] + ([bev] if fi == 0 else []),
                                  sig=(True if fi == 10 else None))
                if e_ == 0:
                    ey = s.emit("scalar", (lambda e, bank=bank, oc=oc, tg=tg:
                                           e.copy(y_t[oc][:, tg * 512:(tg + 1) * 512], bank[:, :])),
                                deps=[mm, y_ev[oc]], sig=True)
                else:
                    ey = s.emit("vector", (lambda e, bank=bank, oc=oc, tg=tg:
                                           e.tensor_add(y_t[oc][:, tg * 512:(tg + 1) * 512],
                                                        y_t[oc][:, tg * 512:(tg + 1) * 512], bank[:, :])),
                                deps=[mm, y_ev[oc]], sig=True)
                ps.release(bi, ey)
                y_ev[oc] = ey
            dnb_ev[slot] = mm

    sguT, sg_q, sg_d = l.wget("sguT")
    sdT, sd_q, sd_d = l.wget("sdT")
    sact = [mk_sbuf(nc, stack, f"msa{i}", [128, 512], BF16) for i in range(22)]
    sdb = mk_sbuf(nc, stack, "msdb", [128, 22 * 128 * 2], BF16)
    sdb_ev = [None, None]
    g1b = [mk_sbuf(nc, stack, f"mg1{i}", [128, 512], F32) for i in range(2)]
    g1b_dma = [None, None]
    it3 = 0
    for tg in range(2):
        sact_ev = [None] * 22
        for fi in range(22):
            for part in range(2):
                fr = fi if part == 0 else 22 + fi
                slot = (fi * 2 + part) % 3
                src = sguT[:, fr * 128:(fr + 1) * 128].rearrange("(ec p) f -> p ec f", p=128)
                dst = gub[:, slot * NCH * 128:(slot + 1) * NCH * 128].rearrange("p (ec f) -> p ec f", f=128)
                lev = s.dma(sg_q, dst, src, deps=(gub_ev[slot],) + sg_d)
                bi, bank, bev = ps.acquire()
                mm = None
                for ec in range(NCH):
                    mm = s.emit("tensor", (lambda e, bank=bank, slot=slot, ec=ec, tg=tg,
                                           st=(ec == 0), sp=(ec == NCH - 1):
                                           e.matmul(bank[:, :],
                                                    gub[:, slot * NCH * 128 + ec * 128: slot * NCH * 128 + (ec + 1) * 128],
                                                    xn_t[ec][:, tg * 512:(tg + 1) * 512], start=st, stop=sp)),
                                  deps=[lev] + ([bev] if ec == 0 else []),
                                  sig=(True if ec == NCH - 1 else None))
                gub_ev[slot] = mm
                if part == 0:
                    esl = s.emit("scalar", (lambda e, bank=bank, tg=tg: e.activation(sgb[tg][:, :], bank[:, :], AF.Silu)),
                                 deps=[mm, sg_prev[tg]], sig=True)
                    ps.release(bi, esl)
                    sact_ev[fi] = esl
                else:
                    ea = s.emit("vector", (lambda e, bank=bank, fi=fi, tg=tg:
                                           e.tensor_mul(sact[fi][:, :], sgb[tg][:, :], bank[:, :])),
                                deps=[mm, sact_ev[fi]], sig=True)
                    sg_prev[tg] = ea
                    ps.release(bi, ea)
                    sact_ev[fi] = ea
        for oc in range(NCH):
            slot = oc % 2
            src = sdT[:, oc * 128:(oc + 1) * 128].rearrange("(ec p) f -> p ec f", p=128)
            dst = sdb[:, slot * 22 * 128:(slot + 1) * 22 * 128].rearrange("p (ec f) -> p ec f", f=128)
            lev = s.dma(sd_q, dst, src, deps=(sdb_ev[slot],) + sd_d)
            bi, bank, bev = ps.acquire()
            mm = None
            for fi in range(22):
                mm = s.emit("tensor", (lambda e, bank=bank, slot=slot, fi=fi, st=(fi == 0), sp=(fi == 21):
                                       e.matmul(bank[:, :],
                                                sdb[:, slot * 22 * 128 + fi * 128: slot * 22 * 128 + (fi + 1) * 128],
                                                sact[fi][:, :], start=st, stop=sp)),
                              deps=[lev, sact_ev[fi]] + ([bev] if fi == 0 else []),
                              sig=(True if fi == 21 else None))
            sdb_ev[slot] = mm
            ey = s.emit("vector", (lambda e, bank=bank, oc=oc, tg=tg:
                                   e.tensor_add(y_t[oc][:, tg * 512:(tg + 1) * 512],
                                                y_t[oc][:, tg * 512:(tg + 1) * 512], bank[:, :])),
                        deps=[mm, y_ev[oc]], sig=True)
            ps.release(bi, ey)
            y_ev[oc] = ey
            sl = it3 % 2
            eg = s.emit("scalar", (lambda e, oc=oc, tg=tg, sl=sl:
                                   e.activation(g1b[sl][:, :], y_t[oc][:, tg * 512:(tg + 1) * 512], AF.Silu)),
                        deps=[ey, g1b_dma[sl]], sig=True)
            dev = s.dma("sync", g1[oc * 128:(oc + 1) * 128, tg * 512:(tg + 1) * 512], g1b[sl][:, :], deps=[eg])
            g1b_dma[sl] = dev
            l.store("g1", dev)
            it3 += 1
    close_stack(stack)


def emit_delta(l, upto="all"):
    s, nc, ps = l.s, l.nc, l.ps
    stack = []
    xn1 = l.din("xn1")
    s1 = l.dout("s1")
    identb, ev_idb = l.const("ident", BF16, stack)
    gnorm, ev_gn = l.const("gnorm", F32, stack)
    xn_t = [mk_sbuf(nc, stack, f"dx{i}", [128, COLS], BF16) for i in range(NCH)]
    xn_ev = [s.dma("sync", xn_t[i][:, :], xn1[i * 128:(i + 1) * 128, :], deps=l.ldeps("xn1")) for i in range(NCH)]

    conv = {nm: l.const(nm, F32, stack) for nm in ("convq", "convk", "convv")}
    qkv_t, qkv_ev = {}, {}
    tmp = mk_sbuf(nc, stack, "dtmp", [128, 512], BF16)
    dwring = (mk_sbuf(nc, stack, "dwring", [128, NCH * 128 * 2], BF16), [None, None])
    for nm, wnm, cnm in (("q", "wq1T", "convq"), ("k", "wk1T", "convk"), ("v", "wv1T", "convv")):
        tiles = [mk_sbuf(nc, stack, f"d{nm}{i}", [128, COLS], BF16) for i in range(NCH)]
        qkv_t[nm] = tiles
        evs = [[None, None] for _ in range(NCH)]

        def ep(fi, tg, bank, mm_ev, f0, fw, c0, cw, tiles=tiles, evs=evs):
            eev = s.emit("scalar", (lambda e, bank=bank, fi=fi, c0=c0: e.copy(tiles[fi][:, c0:c0 + 512], bank[:, :])),
                         deps=[mm_ev], sig=True)
            evs[fi][tg] = eev
            return eev

        emit_proj(l, wnm, [(h * 128, 128) for h in range(NH)], xn_t, xn_ev, ep, stack, f"d{nm}w", wring=dwring)
        csb, ev_c = conv[cnm]
        outevs = []
        tmp_prev = None
        for h in range(NH):
            t0 = tiles[h][:, 0:512]
            t1 = tiles[h][:, 512:1024]
            d0, d1 = evs[h]
            e1 = s.emit("vector", (lambda e, t1=t1, h=h, cs=csb:
                                   e.tensor_scalar(tmp[:, :], t1, cs[:, h * 4 + 3:h * 4 + 4], None, ALU.mult)),
                        deps=[d1, ev_c, tmp_prev], sig=True)
            e2 = s.emit("vector", (lambda e, t0=t0, t1=t1, h=h, cs=csb:
                                   e.scalar_tensor_tensor(t1, t0, cs[:, h * 4 + 2:h * 4 + 3], tmp[:, :],
                                                          ALU.mult, ALU.add)),
                        deps=[e1, d0], sig=True)
            tmp_prev = e2
            e3 = s.emit("vector", (lambda e, t0=t0, h=h, cs=csb:
                                   e.tensor_scalar(t0, t0, cs[:, h * 4 + 3:h * 4 + 4], None, ALU.mult)),
                        deps=[d0, e2], sig=True)
            e4 = s.emit("scalar", (lambda e, t=tiles[h]: e.activation(t[:, :], t[:, :], AF.Silu)),
                        deps=[e2, e3], sig=True)
            outevs.append(e4)
        qkv_ev[nm] = outevs

    if upto == "conv":
        for i in range(NCH):
            dev = s.dma("gpsimd", s1[i * 128:(i + 1) * 128, :], qkv_t["q"][i][:, :], deps=qkv_ev["q"])
            l.store("s1", dev)
        close_stack(stack)
        return
    wb = l.din("wbT")
    wbsb = mk_sbuf(nc, stack, "dwb", [128, NCH * 16], BF16)
    ev_wb = s.dma("gpsimd", wbsb[:, :].rearrange("p (ec f) -> p ec f", f=16),
                  wb[:, :].rearrange("(ec p) f -> p ec f", p=128))
    btf = mk_sbuf(nc, stack, "dbtf", [16, COLS], F32)
    bt_ev = []
    for tg in range(2):
        bi, bank, bev = ps.acquire()
        mm = None
        for ec in range(NCH):
            mm = s.emit("tensor", (lambda e, bank=bank, ec=ec, tg=tg, st=(ec == 0), sp=(ec == NCH - 1):
                                   e.matmul(bank[0:16, :], wbsb[:, ec * 16:(ec + 1) * 16],
                                            xn_t[ec][:, tg * 512:(tg + 1) * 512], start=st, stop=sp)),
                        deps=[ev_wb, xn_ev[ec]] + ([bev] if ec == 0 else []), sig=(True if ec == NCH - 1 else None))
        eev = s.emit("scalar", (lambda e, bank=bank, tg=tg: e.copy(btf[:, tg * 512:(tg + 1) * 512], bank[0:16, :])),
                     deps=[mm], sig=True)
        ps.release(bi, eev)
        bt_ev.append(eev)
    esg = s.emit("scalar", lambda e: e.activation(btf[:, :], btf[:, :], AF.Sigmoid), deps=bt_ev, sig=True)
    beta_tm = mk_sbuf(nc, stack, "dbtm", [128, 8 * 16], F32)
    beta_ev = []
    btmp = mk_sbuf(nc, stack, "dbtmp", [16, 128], BF16)
    prev_bt = None
    for cb_i in range(8):
        ecp0 = s.emit("vector", (lambda e, cb_i=cb_i: e.tensor_copy(btmp[0:16, 0:128], btf[0:16, cb_i * 128:(cb_i + 1) * 128])),
                      deps=[esg, prev_bt], sig=True)
        bi, bank, bev = ps.acquire()
        bkb = bank.bitcast(BF16)
        et = s.emit("tensor", (lambda e, bkb=bkb: e.transpose(bkb[0:128, 0:16], btmp[0:16, 0:128], identb[0:16, 0:16])),
                    deps=[ecp0, ev_idb, bev], sig=True)
        prev_bt = et
        ee = s.emit("vector", (lambda e, bkb=bkb, cb_i=cb_i: e.tensor_copy(beta_tm[:, cb_i * 16:(cb_i + 1) * 16],
                                                                           bkb[0:128, 0:16])),
                    deps=[et], sig=True)
        ps.release(bi, ee)
        beta_ev.append(ee)

    tm = {nm: [mk_sbuf(nc, stack, f"dtm{nm}{sb_}", [128, NH * 128], BF16) for sb_ in range(2)]
          for nm in ("q", "k", "v")}
    prod = mk_sbuf(nc, stack, "dprod", [128, 2048], BF16)
    dots = mk_sbuf(nc, stack, "ddots", [128, 8 * 16], F32)
    scal = mk_sbuf(nc, stack, "dscal", [128, 16 * 16], F32)
    scbf = mk_sbuf(nc, stack, "dscbf", [128, 8 * 16], BF16)
    u1 = mk_sbuf(nc, stack, "du1", [128, 2048], BF16)
    u2 = mk_sbuf(nc, stack, "du2", [128, 2048], BF16)
    o0b = [mk_sbuf(nc, stack, f"do0{i}", [128, 2048], BF16) for i in range(2)]
    o1b = [mk_sbuf(nc, stack, f"do1{i}", [128, 2048], BF16) for i in range(2)]
    oT = xn_t
    oT_evs = []


    prev_tm_reads = [None, None]
    o_reads = [None, None]
    pend_oT = []

    def flush_oT(item):
        jj, par, oo0, oo1, of0, of1 = item
        reads = []
        for sb_, (oo, oev, colblk) in enumerate(((oo0, of0, jj), (oo1, of1, jj + 4))):
            for h in range(NH):
                bi, bank, bev = ps.acquire()
                bkb = bank.bitcast(BF16)
                et = s.emit("tensor", (lambda e, bkb=bkb, oo=oo, h=h:
                                       e.transpose(bkb[0:128, 0:128], oo[:, h * 128:(h + 1) * 128], identb[:, :])),
                            deps=[oev, ev_idb, bev], sig=True)
                ee = s.emit("vector", (lambda e, bkb=bkb, h=h, colblk=colblk:
                                       e.tensor_scalar(oT[h][:, colblk * 128:(colblk + 1) * 128], bkb[0:128, 0:128],
                                                       gnorm[:, 0:1], None, ALU.mult)),
                            deps=[et, ev_gn], sig=True)
                ps.release(bi, ee)
                reads.append(ee)
                oT_evs.append(ee)
        o_reads[par] = reads[-1]

    for j in range(4):
        tm_ev = []
        for sb_, colblk in ((0, j), (1, j + 4)):
            for nm in ("q", "k", "v"):
                for h in range(NH):
                    bi, bank, bev = ps.acquire()
                    bkb = bank.bitcast(BF16)
                    et = s.emit("tensor", (lambda e, bkb=bkb, nm=nm, h=h, colblk=colblk:
                                           e.transpose(bkb[0:128, 0:128],
                                                       qkv_t[nm][h][:, colblk * 128:(colblk + 1) * 128], identb[:, :])),
                                deps=[qkv_ev[nm][h], ev_idb, bev], sig=True)
                    ee = s.emit("vector", (lambda e, bkb=bkb, nm=nm, sb_=sb_, h=h:
                                           e.tensor_copy(tm[nm][sb_][:, h * 128:(h + 1) * 128], bkb[0:128, 0:128])),
                                deps=[et, prev_tm_reads[sb_]], sig=True)
                    ps.release(bi, ee)
                    tm_ev.append(ee)
        q0_, q1_ = tm["q"][0], tm["q"][1]
        k0_, k1_ = tm["k"][0], tm["k"][1]
        v0_, v1_ = tm["v"][0], tm["v"][1]
        pr_specs = [(q0_, q0_), (q1_, q1_), (k0_, k0_), (k1_, k1_),
                    (q0_, k0_), (k1_, k0_), (q1_, k0_), (q1_, k1_)]
        dot_evs = []
        red_prev = None
        for idx, (a_, b_) in enumerate(pr_specs):
            ep1 = s.emit("vector", (lambda e, a_=a_, b_=b_: e.tensor_mul(prod[:, :], a_[:, :], b_[:, :])),
                         deps=tm_ev + [red_prev], sig=True)
            red_prev = s.emit("vector", (lambda e, idx=idx:
                                         e.reduce_sum(dots[:, idx * 16:(idx + 1) * 16],
                                                      prod[:, :].rearrange("p (h d) -> p h d", d=128), axis=AX)),
                              deps=[ep1], sig=True)
            dot_evs.append(red_prev)
        sc_evs = {}
        for idx in range(4):
            eps12, ev_eps12 = l.epsap(1.0e-12)
            eL = s.emit("scalar", (lambda e, idx=idx:
                                   e.activation(scal[:, idx * 16:(idx + 1) * 16], dots[:, idx * 16:(idx + 1) * 16],
                                                AF.Ln, bias=eps12[:, 0:1])), deps=[dot_evs[idx], ev_eps12], sig=True)
            sc_evs[idx] = s.emit("scalar", (lambda e, idx=idx:
                                            e.activation(scal[:, idx * 16:(idx + 1) * 16], scal[:, idx * 16:(idx + 1) * 16],
                                                         AF.Exp, scale=-0.5)), deps=[eL], sig=True)

        def dotscale(dst_off, pidx, r1o, r2o, const, deps_):
            ea = s.emit("vector", (lambda e, dst_off=dst_off, pidx=pidx, r1o=r1o, const=const:
                                   e.scalar_tensor_tensor(scal[:, dst_off * 16:(dst_off + 1) * 16],
                                                          dots[:, pidx * 16:(pidx + 1) * 16], const,
                                                          scal[:, r1o * 16:(r1o + 1) * 16], ALU.mult, ALU.mult)),
                        deps=deps_, sig=True)
            return s.emit("vector", (lambda e, dst_off=dst_off, r2o=r2o:
                                     e.tensor_mul(scal[:, dst_off * 16:(dst_off + 1) * 16],
                                                  scal[:, dst_off * 16:(dst_off + 1) * 16],
                                                  scal[:, r2o * 16:(r2o + 1) * 16])),
                          deps=[ea], sig=True)

        e_c0 = dotscale(4, 4, 0, 2, RSD, [dot_evs[4], sc_evs[0], sc_evs[2]])
        e_a = dotscale(5, 5, 3, 2, 1.0, [dot_evs[5], sc_evs[3], sc_evs[2]])
        e_b1 = dotscale(6, 6, 1, 2, RSD, [dot_evs[6], sc_evs[1], sc_evs[2]])
        e_c1 = dotscale(7, 7, 1, 3, RSD, [dot_evs[7], sc_evs[1], sc_evs[3]])
        par = j % 2
        o0 = o0b[par]
        o1 = o1b[par]
        b0sl = beta_tm[:, j * 16:(j + 1) * 16]
        b1sl = beta_tm[:, (j + 4) * 16:(j + 5) * 16]

        def perhead(out_t, in_t, col_ap, deps_):
            evs_ = []
            for h in range(NH):
                evs_.append(s.emit("vector", (lambda e, h=h, out_t=out_t, in_t=in_t, col_ap=col_ap:
                                              e.tensor_scalar(out_t[:, h * 128:(h + 1) * 128],
                                                              in_t[:, h * 128:(h + 1) * 128],
                                                              col_ap[:, h:h + 1], None, ALU.mult)),
                                   deps=deps_ if h == 0 else [evs_[-1]], sig=True))
            return evs_[-1]

        r1_ = perhead(u1, v0_, b0sl, tm_ev + beta_ev)
        r2_ = perhead(o0, u1, scal[:, 64:80], [r1_, e_c0, o_reads[par]])
        r3_ = perhead(u2, u1, scal[:, 80:96], [r1_, e_a])
        r4_ = s.emit("vector", (lambda e: e.tensor_sub(u2[:, :], v1_[:, :], u2[:, :])), deps=[r3_] + tm_ev, sig=True)
        r5_ = perhead(u2, u2, b1sl, [r4_])
        r6_ = perhead(o1, u1, scal[:, 96:112], [r1_, e_b1, r3_, o_reads[par]])
        r7_ = perhead(u1, u2, scal[:, 112:128], [r5_, e_c1, r2_, r6_])
        r8_ = s.emit("vector", (lambda e, o1=o1: e.tensor_add(o1[:, :], o1[:, :], u1[:, :])), deps=[r6_, r7_], sig=True)
        o_fin = []
        for oo, oev, soff in ((o0, r2_, 8), (o1, r8_, 9)):
            ep1 = s.emit("vector", (lambda e, oo=oo: e.tensor_mul(prod[:, :], oo[:, :], oo[:, :])),
                         deps=[oev, red_prev], sig=True)
            ep2 = s.emit("vector", (lambda e, soff=soff:
                                    e.reduce_sum(scal[:, soff * 16:(soff + 1) * 16],
                                                 prod[:, :].rearrange("p (h d) -> p h d", d=128), axis=AX)),
                         deps=[ep1], sig=True)
            red_prev = ep2
            eps5b, ev_eps5b = l.epsap(1.0e-5)
            eL = s.emit("scalar", (lambda e, soff=soff:
                                   e.activation(scal[:, soff * 16:(soff + 1) * 16], scal[:, soff * 16:(soff + 1) * 16],
                                                AF.Ln, scale=1.0 / 128.0, bias=eps5b[:, 0:1])), deps=[ep2, ev_eps5b], sig=True)
            eE = s.emit("scalar", (lambda e, soff=soff:
                                   e.activation(scal[:, soff * 16:(soff + 1) * 16], scal[:, soff * 16:(soff + 1) * 16],
                                                AF.Exp, scale=-0.5)), deps=[eL], sig=True)
            eo = perhead(oo, oo, scal[:, soff * 16:(soff + 1) * 16], [eE, oev])
            o_fin.append(eo)
        pend_oT.append((j, par, o0, o1, o_fin[0], o_fin[1]))
        if len(pend_oT) > 1:
            flush_oT(pend_oT.pop(0))
        prev_tm_reads = [dot_evs[-1], r4_]

    while pend_oT:
        flush_oT(pend_oT.pop(0))
    sbuf_o = [mk_sbuf(nc, stack, f"dsb{i}", [128, 512], F32) for i in range(2)]
    sb_dma = [None, None]
    st = {"it": 0}

    def ep2(fi, tg, bank, mm_ev, f0, fw, c0, cw):
        sl = st["it"] % 2
        eev = s.emit("scalar", lambda e, bank=bank, sl=sl: e.copy(sbuf_o[sl][:, :], bank[:, :]),
                     deps=[mm_ev, sb_dma[sl]], sig=True)
        dev = s.dma("sync", s1[f0:f0 + 128, c0:c0 + cw], sbuf_o[sl][:, :], deps=[eev])
        sb_dma[sl] = dev
        l.store("s1", dev)
        st["it"] += 1
        return eev

    emit_proj(l, "woT", [(h * 128, 128) for h in range(NH)], oT, oT_evs, ep2, stack, "dwo", wring=dwring)
    close_stack(stack)


LAUNCHES = [
    ("ln0", emit_ln0, ["xT", "onesf", "ones1"], ["xn0"],
     [("onesf", F32), ("ones1", F32)]),
    ("ffn0", emit_ffn0,
     ["xn0", "h128", "a1", "a2", "su_ug", "su_dp", "ws_up", "ws_gate", "ws_dp",
      "wupT", "wgateT", "wdownT"], ["g0"],
     [("h128", BF16), ("a1", BF16), ("a2", BF16), ("su_ug", F32), ("su_dp", F32),
      ("ws_up", F32), ("ws_gate", F32), ("ws_dp", F32)]),
    ("attq", emit_attq,
     ["xng", "wqhT", "wkhT", "wvhT", "wprojhT", "cs2", "ss2", "tri", "onesf", "ones1"], ["pattn", "ps0"],
     [("cs2", F32), ("ss2", F32), ("tri", BF16), ("ones1", F32)]),
    ("cmb0", emit_cmb0, ["xT", "s0", "g0", "onesf", "ones1"], ["x1", "xn1"],
     [("onesf", F32), ("ones1", F32)]),
    ("moe", emit_moe, ["xn1", "gateT", "guT", "moedT", "sguT", "sdT", "ident", "ones1"], ["g1"],
     [("ident", F32), ("ones1", F32)]),
    ("delta", emit_delta,
     ["xn1", "wq1T", "wk1T", "wv1T", "wbT", "convq", "convk", "convv", "gnorm", "woT", "ident"], ["s1"],
     [("ident", BF16), ("gnorm", F32), ("convq", F32), ("convk", F32), ("convv", F32)]),
    ("cmb1", emit_cmb1, ["x1", "s1", "g1"], ["outT"], []),
]


def _run_single(name, emitf, ins, outs, consts, sh, pc, hand):
    nc = bass.Bass("TRN2", target_bir_lowering=False)
    s = Sched(nc)
    ps = PsumMgr(nc)
    lctx = L(nc, s, ps, "unfused")
    with nc.Block() as blk:
        lctx.const("onesf", F32)
        lctx.const("ones1", F32)
        lctx.epsap(1.0e-5)
        lctx.epsap(1.0e-12)
        emitf(lctx)
        s.finalize()
        s.replay(blk)
    close_stack(lctx.cstack)
    ps.close()
    s.close()
    in_maps = []
    for c in range(NCORE):
        m = {}
        for nm in list(ins) + ["onesf", "ones1"]:
            if nm in sh:
                m[nm] = sh[nm]
            elif nm in pc[c]:
                m[nm] = pc[c][nm]
            elif nm in hand[c]:
                m[nm] = hand[c][nm]
            else:
                raise KeyError(nm)
        in_maps.append(m)
    res = run_bass_kernel_spmd(nc, in_maps, core_ids=list(range(NCORE)))
    return res.results


def run_unfused(sh, pc, only=None, hand=None):
    if hand is None:
        hand = [dict() for _ in range(NCORE)]
    for name, emitf, ins, outs, consts in LAUNCHES:
        if only is not None and name not in only:
            continue
        results = _run_single(name, emitf, ins, outs, consts, sh, pc, hand)
        for c in range(NCORE):
            for o in outs:
                hand[c][o] = np.asarray(results[c][o])
        if name == "ln0":
            xng = np.concatenate([hand[c]["xn0"] for c in range(NCORE)], axis=0)
            for c in range(NCORE):
                hand[c]["xng"] = xng
        if name == "attq":
            for c in range(NCORE):
                hand[c]["s0"] = np.sum(
                    [np.asarray(hand[i]["ps0"][DM * c:DM * (c + 1)], dtype=np.float32) for i in range(NCORE)],
                    axis=0).astype(np.float32)
    return hand


ALL_CONSTS = [("onesf", F32), ("ones1", F32)]

FUSED_INPUTS = ["xT", "wqhT", "wkhT", "wvhT", "wprojhT", "cs2", "ss2", "tri", "ident",
                "onesf", "ones1", "h128", "a1", "a2", "su_ug", "su_dp", "ws_up",
                "ws_gate", "ws_dp", "wupT", "wgateT", "wdownT", "gateT", "guT",
                "moedT", "sguT", "sdT", "wq1T", "wk1T", "wv1T", "wbT", "convq",
                "convk", "convv", "gnorm", "woT"]


def run_fused(sh, pc):
    nc = bass.Bass("TRN2", target_bir_lowering=False)
    s = Sched(nc)
    ps = PsumMgr(nc)
    lctx = L(nc, s, ps, "fused")
    rg = [list(range(NCORE))]
    with nc.Block() as blk:
        for cn, cd in ALL_CONSTS:
            lctx.const(cn, cd)
        lctx.epsap(1.0e-5)
        lctx.epsap(1.0e-12)
        emit_ln0(lctx)
        xn0_ap = lctx.dram["xn0"]
        xng_ap = lctx.din("xng")
        ev = s.emit("gpsimd",
                    (lambda e: e.collective_compute("AllGather", ALU.bypass, rg,
                                                    [xn0_ap.opt()], [xng_ap.opt()])),
                    deps=lctx.ldeps("xn0"), sig="coll")
        lctx.store("xng", ev)
        ffn_ostack = []
        emit_ffn0(lctx, phase="head", outer_stack=ffn_ostack)
        if USE_WBF:
            for wn in ("wqhT", "wkhT", "wvhT", "wprojhT", "tri"):
                lctx.wbf(wn)
        s.dma_guard = (lctx.ffn_state[1][-1],)

        def _rs_hook(b):
            nm_in = "ps0a" if b == 0 else "ps0b"
            nm_out = "s0a" if b == 0 else "s0b"
            pin = lctx.dout(nm_in)
            pout_ = lctx.din(nm_out)
            evc = s.emit("gpsimd",
                         (lambda e, pin=pin, pout_=pout_: e.collective_compute(
                             "ReduceScatter", ALU.add, rg, [pin.opt()], [pout_.opt()])),
                         deps=lctx.ldeps(nm_in), sig="coll")
            lctx.store(nm_out, evc)
        lctx.rs_hook = _rs_hook
        emit_attq(lctx)
        lctx.rs_hook = None
        _wupa, _wu_q, _wu_d = lctx.wget("wupT")
        _wgta, _wg_q, _wg_d = lctx.wget("wgateT")
        for _fi in (0, 1):
            _slot = _fi % 2
            lctx.ffn_pref[("u", _fi)] = s.dma(
                _wu_q,
                lctx.ffn_ub[:, _slot * NCH * 128:(_slot + 1) * NCH * 128].rearrange("p (ec f) -> p ec f", f=128),
                _wupa[:, _fi * 128:(_fi + 1) * 128].rearrange("(ec p) f -> p ec f", p=128), deps=_wu_d)
            lctx.ffn_pref[("g", _fi)] = s.dma(
                _wg_q,
                lctx.ffn_gb[:, _slot * NCH * 128:(_slot + 1) * NCH * 128].rearrange("p (ec f) -> p ec f", f=128),
                _wgta[:, _fi * 128:(_fi + 1) * 128].rearrange("(ec p) f -> p ec f", p=128), deps=_wg_d)
        s0_ap = lctx.dout("s0")
        eva = s.dma("sync", s0_ap[:, 0:512], lctx.din("s0a"), deps=lctx.ldeps("s0a"))
        evb = s.dma("sync", s0_ap[:, 512:1024], lctx.din("s0b"), deps=lctx.ldeps("s0b"))
        lctx.store("s0", eva)
        lctx.store("s0", evb)
        s.phase_fence()
        emit_ffn0_tail(lctx)
        close_stack(ffn_ostack)
        s.phase_fence()
        emit_cmb0(lctx)
        s.phase_fence()
        emit_moe(lctx)
        s.phase_fence()
        emit_delta(lctx)
        s.phase_fence()
        emit_cmb1(lctx)
        s.finalize()
        s.replay(blk)
    close_stack(lctx.cstack)
    ps.close()
    s.close()
    in_maps = []
    for c in range(NCORE):
        m = {}
        for nm in FUSED_INPUTS:
            m[nm] = sh[nm] if nm in sh else pc[c][nm]
        in_maps.append(m)
    res = run_bass_kernel_spmd(nc, in_maps, core_ids=list(range(NCORE)))
    return [{"outT": np.asarray(r["outT"])} for r in res.results]


FUSED = True


def kernel(**inputs):
    sh, pc = host_prep(inputs)
    if FUSED:
        hand = run_fused(sh, pc)
    else:
        hand = run_unfused(sh, pc)
    out = np.empty((2, 4096, 2048), np.float32)
    for c in range(NCORE):
        arr = np.asarray(hand[c]["outT"], dtype=np.float32)
        out[0, 512 * c:512 * (c + 1), :] = arr[:, 0:512].T
        out[1, 512 * c:512 * (c + 1), :] = arr[:, 512:1024].T
    return out


# version 32 (final)
# speedup vs baseline: 1.0422x; 1.0422x over previous
import numpy as np
import ml_dtypes

import concourse.bass as bass
import concourse.mybir as mybir
from concourse.bass_utils import run_bass_kernel_spmd

F32 = mybir.dt.float32
BF16 = mybir.dt.bfloat16
AF = mybir.ActivationFunctionType
ALU = mybir.AluOpType
AX = mybir.AxisListType.X
BF = ml_dtypes.bfloat16

NCORE = 8
DM = 2048
NCH = 16
COLS = 1024
NH, HD = 16, 128
INTER = 8192
TT = 4096
RSD = float(1.0 / np.sqrt(np.float32(128.0)))
FW1 = float(1.0 / (np.sqrt(np.float32(2048.0)) * np.float32(64.0)))
FW2 = float(1.0 / (np.sqrt(np.float32(8192.0)) * np.float32(64.0)))
ENGS = ("tensor", "vector", "scalar", "sync", "gpsimd")
USE_WBF = True
WBF_NAMES = {"wqhT", "wkhT", "wvhT", "wprojhT", "tri"}


def _hadamard(n):
    h = np.array([[1.0]])
    while h.shape[0] < n:
        h = np.block([[h, h], [h, -h]])
    return h.astype(np.float32)


def _perm_fwht(nchunk):
    perm = np.empty(nchunk * 128, dtype=np.int64)
    for jp in range(nchunk):
        for pi in range(128):
            jr, g = pi // nchunk, pi % nchunk
            perm[jp * 128 + pi] = g * 128 + (jp + nchunk * jr)
    return perm


PERM1 = _perm_fwht(16)
PERM2 = _perm_fwht(64)


def _chunk_major(vec):
    n = vec.shape[0] // 128
    return np.ascontiguousarray(vec.reshape(n, 128).T.astype(np.float32))


def host_prep(inp):
    sh, pc = {}, [dict() for _ in range(NCORE)]
    x = inp["L0_attn_x"]
    for c in range(NCORE):
        xs = np.concatenate([x[0, 512 * c:512 * (c + 1), :], x[1, 512 * c:512 * (c + 1), :]], axis=0)
        pc[c]["xT"] = np.ascontiguousarray(xs.T.astype(np.float32))

    Wqkv = inp["L0_attn_W_qkv"]
    rp = np.array(list(range(0, 128, 2)) + list(range(1, 128, 2)), dtype=np.int64)
    qrows = np.concatenate([h * 128 + rp for h in range(NH)])
    wqT = np.ascontiguousarray(Wqkv[qrows, :].T)
    wkT = np.ascontiguousarray(Wqkv[2048 + qrows, :].T)
    wvT = np.ascontiguousarray(Wqkv[4096:6144, :].T)
    for c in range(NCORE):
        pc[c]["wqhT"] = np.ascontiguousarray(wqT[:, 256 * c:256 * (c + 1)])
        pc[c]["wkhT"] = np.ascontiguousarray(wkT[:, 256 * c:256 * (c + 1)])
        pc[c]["wvhT"] = np.ascontiguousarray(wvT[:, 256 * c:256 * (c + 1)])
    wprojT = np.ascontiguousarray(inp["L0_attn_W_proj"].T)
    for c in range(NCORE):
        pc[c]["wprojhT"] = np.ascontiguousarray(wprojT[256 * c:256 * (c + 1), :])

    inv_freq = (1.0 / 10000.0 ** (np.arange(0, 128, 2, dtype=np.float32) / np.float32(128.0))).astype(np.float32)
    pos = np.arange(TT, dtype=np.float32)
    ang = pos[None, :] * inv_freq[:, None]
    cos, sin = np.cos(ang).astype(np.float32), np.sin(ang).astype(np.float32)
    sh["cs2"] = np.ascontiguousarray(np.concatenate([cos, cos], axis=0))
    sh["ss2"] = np.ascontiguousarray(np.concatenate([-sin, sin], axis=0))
    sh["tri"] = np.ascontiguousarray(np.triu(np.ones((128, 128), dtype=np.float32)))
    sh["ident"] = np.ascontiguousarray(np.eye(128, dtype=np.float32))
    sh["onesf"] = np.ones((128, 1), dtype=np.float32)
    sh["ones1"] = np.ones((1, 128), dtype=np.float32)

    sh["h128"] = _hadamard(128)
    sh["a1"] = np.ascontiguousarray(np.kron(np.eye(8, dtype=np.float32), _hadamard(16)))
    sh["a2"] = np.ascontiguousarray(np.kron(np.eye(2, dtype=np.float32), _hadamard(64)))
    sh["wupT"] = np.ascontiguousarray(inp["L0_ffn_W_up"].T[PERM1, :])
    sh["wgateT"] = np.ascontiguousarray(inp["L0_ffn_W_gate"].T[PERM1, :])
    sh["wdownT"] = np.ascontiguousarray(inp["L0_ffn_W_down"].T[PERM2, :])
    sh["su_ug"] = _chunk_major(inp["L0_ffn_SU_ug"])
    sh["su_dp"] = _chunk_major(inp["L0_ffn_SU_dp"])
    sh["ws_up"] = _chunk_major(inp["L0_ffn_Wscale_ug"][:INTER])
    sh["ws_gate"] = _chunk_major(inp["L0_ffn_Wscale_ug"][INTER:])
    sh["ws_dp"] = _chunk_major(inp["L0_ffn_Wscale_dp"])

    sh["wq1T"] = np.ascontiguousarray(inp["L1_attn_Wq"])
    sh["wk1T"] = np.ascontiguousarray(inp["L1_attn_Wk"])
    sh["wv1T"] = np.ascontiguousarray(inp["L1_attn_Wv"])
    sh["wbT"] = np.ascontiguousarray(inp["L1_attn_Wb"])
    sh["woT"] = np.ascontiguousarray(inp["L1_attn_Wo"])
    for nm, key in (("convq", "L1_attn_conv_q"), ("convk", "L1_attn_conv_k"), ("convv", "L1_attn_conv_v")):
        cv = inp[key]
        sh[nm] = np.ascontiguousarray(cv.reshape(NCH, 128, 4).transpose(1, 0, 2).reshape(128, NCH * 4).astype(np.float32))
    sh["gnorm"] = np.ascontiguousarray(inp["L1_attn_g_norm"].reshape(128, 1).astype(np.float32))

    sh["gateT"] = np.ascontiguousarray(inp["L1_ffn_gate_weight"].T)
    sh["guT"] = np.ascontiguousarray(np.stack([np.ascontiguousarray(inp["L1_ffn_gate_up_weights"][e].T) for e in range(8)]))
    sh["moedT"] = np.ascontiguousarray(np.stack([np.ascontiguousarray(inp["L1_ffn_down_weights"][e].T) for e in range(8)]))
    sh["sguT"] = np.ascontiguousarray(inp["L1_ffn_shared_gate_up"].T)
    sh["sdT"] = np.ascontiguousarray(inp["L1_ffn_shared_down"].T)
    return sh, pc


class Ev:
    __slots__ = ("sem", "val")

    def __init__(self, sem, val):
        self.sem, self.val = sem, val


class Op:
    __slots__ = ("eng", "fn", "deps", "sig", "amt", "sigval")

    def __init__(self, eng, fn, deps, sig, amt, sigval=0):
        self.eng, self.fn, self.deps, self.sig, self.amt, self.sigval = eng, fn, deps, sig, amt, sigval


class Sched:
    SIG = {"tensor": "pe", "vector": "dve", "scalar": "act", "sync": "spq", "gpsimd": "pool"}

    def __init__(self, nc):
        self.nc = nc
        self.ops = []
        self.cnt = {}
        self.handles = {}
        self._sem_ctxs = []
        self.base_deps = ()
        self.dma_guard = ()

    def _sem(self, name):
        if name not in self.handles:
            cm = self.nc.semaphore(name)
            self.handles[name] = cm.__enter__()
            self._sem_ctxs.append(cm)
            self.cnt[name] = 0
        return self.handles[name]

    DMA_POOL = 8

    def emit(self, eng, fn, deps=(), sig=None, dma=False):
        semname, amt = None, 0
        if dma:
            q = "dms" if eng == "sync" else "dmg"
            if not hasattr(self, "_dma_idx"):
                self._dma_idx = {}
                self._dma_last = {}
            idx = self._dma_idx.get(q, 0)
            self._dma_idx[q] = idx + 1
            k = idx % self.DMA_POOL
            semname, amt = f"{q}{k}", 16
            prev = self._dma_last.get(semname)
            if prev is not None:
                deps = tuple(deps) + (prev,)
        elif sig is not None:
            semname, amt = (self.SIG[eng] if sig is True else sig), 1
        ev = None
        if semname is not None:
            self._sem(semname)
            self.cnt[semname] += amt
            ev = Ev(semname, self.cnt[semname])
        if dma:
            self._dma_last[semname] = ev
        deps = tuple(d for d in deps if d is not None) + self.base_deps
        self.ops.append(Op(eng, fn, deps, semname, amt, ev.val if ev is not None else 0))
        return ev

    def dma(self, eng, out_ap, in_ap, deps=(), **kw):
        deps = tuple(deps) + self.dma_guard
        return self.emit(eng, lambda e: e.dma_start(out=out_ap, in_=in_ap, **kw), deps=deps, dma=True)

    def phase_fence(self):
        self.dma_guard = ()
        evs = []
        for e in ENGS:
            if e in ("sync",):
                ev = self.emit(e, lambda eng: eng.nop(), sig=True)
            else:
                ev = self.emit(e, lambda eng: eng.drain(), sig=True)
            evs.append(ev)
        self.base_deps = tuple(evs)

    def finalize(self):
        deps = []
        for nm in list(self.cnt):
            if nm.startswith("dms") or nm.startswith("dmg"):
                deps.append(Ev(nm, self.cnt[nm]))
        self.emit("sync", lambda e: e.nop(), deps=deps)

    def replay(self, blk):
        attr = {"tensor": blk.tensor, "vector": blk.vector, "scalar": blk.scalar,
                "sync": blk.sync, "gpsimd": blk.gpsimd}
        sched = self
        for eng in ENGS:
            myops = [op for op in self.ops if op.eng == eng]
            if not myops:
                continue

            def body(e, myops=myops, eng=eng):
                waited = {}
                own = sched.SIG[eng]
                owncnt = 0
                for op in myops:
                    for d in op.deps:
                        if d.sem == own and d.val > owncnt:
                            continue
                        if waited.get(d.sem, -1) >= d.val:
                            continue
                        e.wait_ge(sched.handles[d.sem], d.val)
                        waited[d.sem] = d.val
                    ins = op.fn(e)
                    if op.sig is not None:
                        ins.then_inc(sched.handles[op.sig], op.amt)
                        if op.sig == own:
                            owncnt = max(owncnt, op.sigval)
            attr[eng](body)
        for k, v in self.cnt.items():
            assert v < 64000, f"sem {k} overflow risk: {v}"

    def close(self):
        for cm in reversed(self._sem_ctxs):
            cm.__exit__(None, None, None)


_SBUF_UID = [0]


def mk_sbuf(nc, stack, name, shape, dtype):
    _SBUF_UID[0] += 1
    cm = nc.sbuf_tensor(f"{name}_u{_SBUF_UID[0]}", list(shape), dtype)
    h = cm.__enter__()
    stack.append(cm)
    return h


def close_stack(stack):
    for cm in reversed(stack):
        cm.__exit__(None, None, None)
    stack.clear()


class PsumMgr:
    def __init__(self, nc):
        self.banks, self._ctx, self.free_ev = [], [], [None] * 8
        for i in range(8):
            cm = nc.psum_tensor(f"pb{i}", [128, 512], F32)
            self.banks.append(cm.__enter__())
            self._ctx.append(cm)
        self.rr = 0

    def acquire(self):
        i = self.rr
        self.rr = (self.rr + 1) % 8
        return i, self.banks[i], self.free_ev[i]

    def release(self, i, ev):
        self.free_ev[i] = ev

    def close(self):
        for cm in reversed(self._ctx):
            cm.__exit__(None, None, None)


def bc(ap2d, reps):
    return ap2d.unsqueeze(2).broadcast_to([ap2d.shape[0], ap2d.shape[1], reps])


HANDOFF_SHAPES = {
    "xn0": ((DM, COLS), BF16), "xng": ((8 * DM, COLS), BF16),
    "g0": ((DM, COLS), F32), "pattn": ((256, 2 * TT), BF16), "ps0": ((8 * DM, COLS), F32),
    "ps0a": ((8 * DM, 512), F32), "ps0b": ((8 * DM, 512), F32),
    "s0a": ((DM, 512), F32), "s0b": ((DM, 512), F32),
    "s0": ((DM, COLS), F32), "x1": ((DM, COLS), F32), "xn1": ((DM, COLS), BF16),
    "g1": ((DM, COLS), F32), "s1": ((DM, COLS), F32),
}

INPUT_SHAPES = {
    "xT": ((DM, COLS), F32), "wqhT": ((DM, 256), F32), "wkhT": ((DM, 256), F32),
    "wvhT": ((DM, 256), F32), "wprojhT": ((256, DM), F32), "cs2": ((128, TT), F32), "ss2": ((128, TT), F32),
    "onesf": ((128, 1), F32), "ones1": ((1, 128), F32),
    "h128": ((128, 128), F32), "a1": ((128, 128), F32), "a2": ((128, 128), F32),
    "su_ug": ((128, 16), F32), "su_dp": ((128, 64), F32), "ws_up": ((128, 64), F32),
    "ws_gate": ((128, 64), F32), "ws_dp": ((128, 16), F32),
    "wupT": ((DM, INTER), F32), "wgateT": ((DM, INTER), F32), "wdownT": ((INTER, DM), F32),
    "tri": ((128, 128), F32), "ident": ((128, 128), F32),
    "gateT": ((DM, 8), F32), "guT": ((8, DM, 2816), F32), "moedT": ((8, 1408, DM), F32),
    "sguT": ((DM, 5632), F32), "sdT": ((2816, DM), F32),
    "wq1T": ((DM, DM), F32), "wk1T": ((DM, DM), F32), "wv1T": ((DM, DM), F32), "woT": ((DM, DM), F32),
    "wbT": ((DM, 16), F32), "convq": ((128, 64), F32), "convk": ((128, 64), F32),
    "convv": ((128, 64), F32), "gnorm": ((128, 1), F32),
}


class L:
    def __init__(self, nc, s, ps, mode):
        self.nc, self.s, self.ps, self.mode = nc, s, ps, mode
        self.dram = {}
        self.store_evs = {}
        self.consts = {}
        self.cstack = []

    def _shape(self, name):
        if name in HANDOFF_SHAPES:
            return HANDOFF_SHAPES[name]
        return INPUT_SHAPES[name]

    def din(self, name):
        if name not in self.dram:
            shp, dt = self._shape(name)
            if self.mode == "fused" and name in HANDOFF_SHAPES:
                kind = "Internal"
            else:
                kind = "ExternalInput"
            self.dram[name] = self.nc.dram_tensor(name, list(shp), dt, kind=kind).ap()
        return self.dram[name]

    def dout(self, name):
        if name not in self.dram:
            if name == "outT":
                shp, dt = (DM, COLS), F32
                kind = "ExternalOutput"
            else:
                shp, dt = self._shape(name)
                kind = "Internal" if self.mode == "fused" else "ExternalOutput"
            self.dram[name] = self.nc.dram_tensor(name, list(shp), dt, kind=kind).ap()
        return self.dram[name]

    def dscratch(self, name, shape, dtype):
        if name not in self.dram:
            self.dram[name] = self.nc.dram_tensor(name, list(shape), dtype, kind="Internal").ap()
        return self.dram[name]

    def store(self, name, ev):
        self.store_evs.setdefault(name, []).append(ev)

    def ldeps(self, name):
        return tuple(self.store_evs.get(name, ()))

    def wbf(self, name):
        if not hasattr(self, "_wbf"):
            self._wbf = {}
        if name not in self._wbf:
            shp, _ = self._shape(name)
            ap = self.din(name)
            t = self.nc.dram_tensor(f"bf_{name}", list(shp), BF16, kind="Internal").ap()
            n = 1
            for d in shp:
                n *= d
            rows = n // shp[-1]
            chunk = max(1, (4 * 1024 * 1024) // shp[-1])
            evs = []
            if len(shp) == 3:
                a2 = ap.rearrange("a b f -> (a b) f")
                t2 = t.rearrange("a b f -> (a b) f")
            else:
                a2, t2 = ap, t
            for r0 in range(0, rows, chunk):
                r1 = min(rows, r0 + chunk)
                evs.append(self.s.dma("gpsimd", t2[r0:r1, :], a2[r0:r1, :]))
            self._wbf[name] = (t2 if len(shp) > 2 else t, tuple(evs))
        t2, evs = self._wbf[name]
        return t2, evs

    def wget(self, name):
        if self.mode == "fused" and USE_WBF and name in WBF_NAMES:
            t, evs = self.wbf(name)
            return t, "sync", tuple(evs)
        return self.din(name), "gpsimd", ()

    def const(self, name, dst_dtype, stack=None):
        key = (name, dst_dtype)
        if stack is not None:
            shp, src_dtype = self._shape(name)
            t = mk_sbuf(self.nc, stack, f"p_{name}_{1 if dst_dtype == F32 else 2}", list(shp), dst_dtype)
            if src_dtype != dst_dtype and self.mode == "fused" and USE_WBF and name in WBF_NAMES:
                bap, evs = self.wbf(name)
                ev = self.s.dma("sync", t[:, :], bap, deps=tuple(evs) + self.s.base_deps)
            else:
                ap = self.din(name)
                eng = "gpsimd" if src_dtype != dst_dtype else "sync"
                ev = self.s.dma(eng, t[:, :], ap, deps=self.s.base_deps)
            return (t, ev)
        if key not in self.consts:
            shp, src_dtype = self._shape(name)
            ap = self.din(name)
            t = mk_sbuf(self.nc, self.cstack, f"c_{name}_{1 if dst_dtype == F32 else 2}", list(shp), dst_dtype)
            eng = "gpsimd" if src_dtype != dst_dtype else "sync"
            ev = self.s.dma(eng, t[:, :], ap)
            self.consts[key] = (t, ev)
        return self.consts[key]

    def epsap(self, val):
        key = ("eps", val)
        if key not in self.consts:
            t = mk_sbuf(self.nc, self.cstack, f"c_eps{len(self.consts)}", [128, 1], F32)
            ev = self.s.emit("vector", lambda e, t=t, val=val: e.memset(t[:, :], val), sig=True)
            self.consts[key] = (t, ev)
        return self.consts[key]


def emit_proj(l, wname, fcols, xtiles, xdeps, epilogue, stack, tag, nslab=2,
              tgs=((0, 512), (512, 1024)), wring=None):
    s, nc, ps = l.s, l.nc, l.ps
    wsrc, w_q, w_deps = l.wget(wname)
    wshape = INPUT_SHAPES[wname][0]
    nech = wshape[0] // 128
    fwmax = max(fw for _, fw in fcols)
    slab_w = nech * fwmax
    if wring is None:
        wbuf = mk_sbuf(nc, stack, f"{tag}_wb", [128, slab_w * nslab], BF16)
        slot_ev = [None] * nslab
    else:
        wbuf, slot_ev = wring
        assert wbuf.shape[1] >= slab_w * nslab
    for fi, (f0, fw) in enumerate(fcols):
        slot = fi % nslab
        src = wsrc[:, f0:f0 + fw].rearrange("(ec p) f -> p ec f", p=128)
        dst = wbuf[:, slot * slab_w: slot * slab_w + nech * fw].rearrange("p (ec f) -> p ec f", f=fw)
        lev = s.dma(w_q, dst, src, deps=(slot_ev[slot],) + w_deps)
        last_mm = None
        for tg_idx, (c0, c1) in enumerate(tgs):
            cw = c1 - c0
            bi, bank, bev = ps.acquire()
            mm_ev = None
            for ec in range(nech):
                lhsT = wbuf[:, slot * slab_w + ec * fw: slot * slab_w + (ec + 1) * fw]
                rhs = xtiles[ec][:, c0:c1]
                deps = [lev] + (list(xdeps) + [bev] if ec == 0 else [])
                mm_ev = s.emit("tensor",
                               (lambda e, o=bank[0:fw, 0:cw], lh=lhsT, r=rhs, st=(ec == 0), sp=(ec == nech - 1):
                                e.matmul(o, lh, r, start=st, stop=sp)),
                               deps=deps, sig=(True if ec == nech - 1 else None))
            last_mm = mm_ev
            free_ev = epilogue(fi, tg_idx, bank, mm_ev, f0, fw, c0, cw)
            ps.release(bi, free_ev)
        slot_ev[slot] = last_mm


def emit_ln(l, xsrc_tiles, xsrc_evs, xn_dram_name, xn_tiles, stack, tag):
    s, nc, ps = l.s, l.nc, l.ps
    xn_dram = l.dout(xn_dram_name)
    onesf, ev_of = l.const("onesf", F32)
    ones1, ev_o1 = l.const("ones1", F32)
    sq = mk_sbuf(nc, stack, f"{tag}sq", [128, 1024], F32)
    rows = mk_sbuf(nc, stack, f"{tag}rows", [1, 2048], F32)
    rb = mk_sbuf(nc, stack, f"{tag}rb", [128, 2048], F32)
    ready = []
    sq_mm_prev = None
    prev_apply = []
    for tg in range(2):
        c0 = tg * 512
        bi_s, bank_s, bev_s = ps.acquire()
        bi_q, bank_q, bev_q = ps.acquire()
        sum_ev = None
        sq_evs = [None, None]
        sq_mm = [sq_mm_prev, sq_mm_prev]
        for i in range(NCH):
            ssl = i % 2
            dep = [xsrc_evs[i], ev_of]
            sq_evs[ssl] = s.emit("scalar", (lambda e, x=xsrc_tiles[i], c=c0, ssl=ssl:
                                            e.activation(sq[:, ssl * 512:(ssl + 1) * 512], x[:, c:c + 512], AF.Square)),
                                 deps=dep + [sq_mm[ssl]], sig=True)
            sum_ev = s.emit("tensor", (lambda e, o=bank_s[0:1, :], x=xsrc_tiles[i], c=c0,
                                       st=(i == 0), sp=(i == NCH - 1):
                                       e.matmul(o, onesf[:, :], x[:, c:c + 512], start=st, stop=sp)),
                            deps=dep + ([bev_s] if i == 0 else []),
                            sig=(True if i == NCH - 1 else None))
            sq_mm[ssl] = s.emit("tensor", (lambda e, o=bank_q[0:1, :], ssl=ssl, st=(i == 0), sp=(i == NCH - 1):
                                           e.matmul(o, onesf[:, :], sq[:, ssl * 512:(ssl + 1) * 512], start=st, stop=sp)),
                               deps=[sq_evs[ssl]] + ([bev_q] if i == 0 else []),
                               sig=True)
            sq_mm_prev = sq_mm[ssl]
        mean = rows[0:1, c0:c0 + 512]
        var = rows[0:1, 1024 + c0:1024 + c0 + 512]
        e1 = s.emit("vector", lambda e, m=mean, b=bank_s: e.tensor_scalar_mul(m, b[0:1, :], 1.0 / 2048.0),
                    deps=[sum_ev], sig=True)
        e2 = s.emit("vector", lambda e, v=var, b=bank_q: e.tensor_scalar_mul(v, b[0:1, :], 1.0 / 2048.0),
                    deps=[sq_mm_prev], sig=True)
        ps.release(bi_s, e1)
        ps.release(bi_q, e2)
        e5 = s.emit("vector", lambda e, m=mean: e.tensor_mul(rb[0:1, 0:512], m, m), deps=[e1] + prev_apply, sig=True)
        e6 = s.emit("vector", lambda e, v=var: e.tensor_sub(v, v, rb[0:1, 0:512]), deps=[e2, e5], sig=True)
        eps5, ev_eps5 = l.epsap(1.0e-5)
        e7 = s.emit("scalar", lambda e, v=var: e.activation(v, v, AF.Ln, bias=eps5[0:1, 0:1]), deps=[e6, ev_eps5], sig=True)
        e8 = s.emit("scalar", lambda e, v=var: e.activation(v, v, AF.Exp, scale=-0.5), deps=[e7], sig=True)
        e9 = s.emit("vector", lambda e, m=mean, v=var: e.tensor_mul(m, m, v), deps=[e8], sig=True)
        bi_b, bank_b, bev_b = ps.acquire()
        eb1 = s.emit("tensor", lambda e, o=bank_b, v=var: e.matmul(o[:, :], ones1[:, :], v, start=True, stop=True),
                     deps=[e8, bev_b, ev_o1], sig=True)
        ec1 = s.emit("scalar", lambda e, b=bank_b, c=c0: e.copy(rb[:, c:c + 512], b[:, :]), deps=[eb1], sig=True)
        ps.release(bi_b, ec1)
        bi_b2, bank_b2, bev_b2 = ps.acquire()
        eb2 = s.emit("tensor", lambda e, o=bank_b2, m=mean: e.matmul(o[:, :], ones1[:, :], m, start=True, stop=True),
                     deps=[e9, bev_b2], sig=True)
        ec2 = s.emit("scalar", lambda e, b=bank_b2, c=c0: e.copy(rb[:, 1024 + c:1024 + c + 512], b[:, :]), deps=[eb2], sig=True)
        ps.release(bi_b2, ec2)
        for i in range(NCH):
            ea = s.emit("vector", (lambda e, o=xn_tiles[i], x=xsrc_tiles[i], c=c0:
                                   e.tensor_mul(o[:, c:c + 512], x[:, c:c + 512], rb[:, c:c + 512])),
                        deps=[xsrc_evs[i], ec1], sig=True)
            eb = s.emit("vector", (lambda e, o=xn_tiles[i], c=c0:
                                   e.tensor_sub(o[:, c:c + 512], o[:, c:c + 512], rb[:, 1024 + c:1024 + c + 512])),
                        deps=[ea, ec2], sig=True)
            dev = s.dma("sync", xn_dram[i * 128:(i + 1) * 128, c0:c0 + 512], xn_tiles[i][:, c0:c0 + 512], deps=[eb])
            l.store(xn_dram_name, dev)
            ready.append(eb)
        prev_apply = ready[-NCH:]
    return ready


def emit_ln0(l):
    s, nc = l.s, l.nc
    stack = []
    xT = l.din("xT")
    x_t = [mk_sbuf(nc, stack, f"lx{i}", [128, COLS], F32) for i in range(NCH)]
    x_ev = [s.dma("sync", x_t[i][:, :], xT[i * 128:(i + 1) * 128, :]) for i in range(NCH)]
    xn_t = [mk_sbuf(nc, stack, f"lxn{i}", [128, COLS], BF16) for i in range(NCH)]
    emit_ln(l, x_t, x_ev, "xn0", xn_t, stack, "ln0")
    close_stack(stack)


def emit_attq(l, upto="all"):
    s, nc, ps = l.s, l.nc, l.ps
    stack = []
    xng = l.din("xng")
    a2ai = l.dout("pattn")
    ps0 = l.dout("ps0") if l.mode == "unfused" else None
    cs2d = l.din("cs2")
    ss2d = l.din("ss2")
    cs2 = mk_sbuf(nc, stack, "acs2", [128, 512], F32)
    ss2 = mk_sbuf(nc, stack, "ass2", [128, 512], F32)
    tri, ev_tri = l.const("tri", BF16, stack)
    ones1, ev_o1 = l.const("ones1", F32)
    onesb = mk_sbuf(nc, stack, "aob", [128, 1], BF16)
    ev_ob = s.emit("vector", lambda e: e.memset(onesb[:, :], 1.0), deps=s.dma_guard, sig=True)

    wq = mk_sbuf(nc, stack, "awq", [128, NCH * 256], BF16)
    wk = mk_sbuf(nc, stack, "awk", [128, NCH * 256], BF16)
    wv = mk_sbuf(nc, stack, "awv", [128, NCH * 256], BF16)
    wpj = mk_sbuf(nc, stack, "awpj", [128, 2 * NCH * 128], BF16)
    _wqs, _wq_q, _wq_d = l.wget("wqhT")
    _wks, _wk_q, _wk_d = l.wget("wkhT")
    _wvs, _wv_q, _wv_d = l.wget("wvhT")
    _wps, _wp_q, _wp_d = l.wget("wprojhT")
    ev_wq = s.dma(_wq_q, wq[:, :].rearrange("p (ec f) -> p ec f", f=256),
                  _wqs[:, :].rearrange("(ec p) f -> p ec f", p=128), deps=_wq_d)
    ev_wk = s.dma(_wk_q, wk[:, :].rearrange("p (ec f) -> p ec f", f=256),
                  _wks[:, :].rearrange("(ec p) f -> p ec f", p=128), deps=_wk_d)
    ev_wv = s.dma(_wv_q, wv[:, :].rearrange("p (ec f) -> p ec f", f=256),
                  _wvs[:, :].rearrange("(ec p) f -> p ec f", p=128), deps=_wv_d)
    ev_wpj = s.dma(_wp_q, wpj[:, :].rearrange("p (ec f) -> p ec f", f=DM),
                   _wps[:, :].rearrange("(ec p) f -> p ec f", p=128), deps=_wp_d)

    Q = [mk_sbuf(nc, stack, f"aQ{i}", [128, TT], BF16) for i in range(2)]
    K = [mk_sbuf(nc, stack, f"aK{i}", [128, TT], BF16) for i in range(2)]
    V = [mk_sbuf(nc, stack, f"aV{i}", [128, TT], BF16) for i in range(2)]
    xg = [[mk_sbuf(nc, stack, f"axg{sb_}_{i}", [128, 512], BF16) for i in range(NCH)] for sb_ in range(2)]
    qf = mk_sbuf(nc, stack, "aqf", [128, 512], F32)
    qsw = mk_sbuf(nc, stack, "aqsw", [128, 512], F32)
    r1 = mk_sbuf(nc, stack, "ar1", [128, 512], F32)
    r2 = mk_sbuf(nc, stack, "ar2", [128, 512], F32)
    P = [mk_sbuf(nc, stack, f"apb{i}", [128, 512], BF16) for i in range(2)]
    lrow = mk_sbuf(nc, stack, "alr", [1, 512], F32)
    lb = mk_sbuf(nc, stack, "alb", [128, 512], F32)
    osb = [mk_sbuf(nc, stack, f"aosb{i}", [128, TT], BF16) for i in range(2)]
    pout = [mk_sbuf(nc, stack, f"apo{i}", [128, 512], F32) for i in range(4)]
    pout_dma = [None, None, None, None]

    xg_reads = [[None] * NCH, [None] * NCH]
    osb_reads = [None, None]
    rope_last = None
    cs_loaded = [None, None]
    lrow_read = None
    it = 0
    itp = 0
    sidx = 0
    sfree = [ps.free_ev[i] for i in range(6)]
    o_free = ps.free_ev[6]
    l_free = ps.free_ev[7]
    bank_o = ps.banks[6]
    bank_l = ps.banks[7]
    pat_dep = []

    for b in range(2):
        qkv_done = []
        for r in range(8):
            sl = r % 2
            ev_cs = s.dma("sync", cs2[:, :], cs2d[:, r * 512:(r + 1) * 512], deps=(rope_last,))
            ev_ss = s.dma("sync", ss2[:, :], ss2d[:, r * 512:(r + 1) * 512], deps=(rope_last,))
            xg_ev = [None] * NCH
            for ec in range(NCH):
                xg_ev[ec] = s.dma("sync", xg[sl][ec][:, :],
                                  xng[(r * NCH + ec) * 128:(r * NCH + ec + 1) * 128, b * 512:(b + 1) * 512],
                                  deps=list(l.ldeps("xng")) + [xg_reads[sl][ec]])
            last_mm = None
            for hl in range(2):
                for which, wbuf_, wev, dst in (("q", wq, ev_wq, Q[hl]), ("k", wk, ev_wk, K[hl])):
                    bi, bank, bev = ps.acquire()
                    mm = None
                    for ec in range(NCH):
                        mm = s.emit("tensor", (lambda e, bank=bank, w=wbuf_, ec=ec, hl=hl, sl=sl,
                                               st=(ec == 0), sp=(ec == NCH - 1):
                                               e.matmul(bank[:, :], w[:, ec * 256 + hl * 128: ec * 256 + (hl + 1) * 128],
                                                        xg[sl][ec][:, :], start=st, stop=sp)),
                                      deps=[xg_ev[ec], wev] + ([bev] if ec == 0 else []),
                                      sig=(True if ec == NCH - 1 else None))
                    last_mm = mm
                    c_dst = r * 512
                    ev0 = s.emit("scalar", lambda e, bank=bank: e.copy(qf[:, :], bank[:, :]),
                                 deps=[mm, rope_last], sig=True)
                    ps.release(bi, ev0)
                    ev1 = s.emit("vector", lambda e: e.tensor_copy(qsw[0:64, :], qf[64:128, :]), deps=[ev0], sig=True)
                    ev2 = s.emit("vector", lambda e: e.tensor_copy(qsw[64:128, :], qf[0:64, :]), deps=[ev0], sig=True)
                    ev3 = s.emit("vector", (lambda e: e.tensor_mul(r1[:, :], qf[:, :], cs2[:, :])),
                                 deps=[ev0, ev_cs], sig=True)
                    ev4 = s.emit("vector", (lambda e: e.tensor_mul(r2[:, :], qsw[:, :], ss2[:, :])),
                                 deps=[ev1, ev2, ev_ss], sig=True)
                    evr = s.emit("vector", (lambda e, dst=dst, c_dst=c_dst:
                                            e.tensor_add(dst[:, c_dst:c_dst + 512], r1[:, :], r2[:, :])),
                                 deps=[ev3, ev4], sig=True)
                    rope_last = evr
                    qkv_done.append(evr)
            for ts in range(4):
                bi, bank, bev = ps.acquire()
                mm = None
                for ec in range(NCH):
                    mm = s.emit("tensor", (lambda e, bank=bank, ec=ec, ts=ts, sl=sl,
                                           st=(ec == 0), sp=(ec == NCH - 1):
                                           e.matmul(bank[:, 0:256], xg[sl][ec][:, ts * 128:(ts + 1) * 128],
                                                    wv[:, ec * 256:(ec + 1) * 256], start=st, stop=sp)),
                              deps=[xg_ev[ec], ev_wv] + ([bev] if ec == 0 else []),
                              sig=(True if ec == NCH - 1 else None))
                last_mm = mm
                tt_i = r * 4 + ts
                ev_v0 = s.emit("vector", (lambda e, bank=bank, tt_i=tt_i:
                                          e.tensor_copy(V[0][:, tt_i * 128:(tt_i + 1) * 128], bank[:, 0:128])),
                               deps=[mm], sig=True)
                ev_v1 = s.emit("vector", (lambda e, bank=bank, tt_i=tt_i:
                                          e.tensor_copy(V[1][:, tt_i * 128:(tt_i + 1) * 128], bank[:, 128:256])),
                               deps=[mm], sig=True)
                ps.release(bi, ev_v1)
                qkv_done.append(ev_v0)
                qkv_done.append(ev_v1)
            for ec in range(NCH):
                xg_reads[sl][ec] = last_mm

        for hl in range(2):
            for qg in range(8):
                pdep = [None, None]
                l_ev = av_ev = None
                ktmax = 4 * qg + 3

                def emit_qk(kt, hl=hl, qg=qg, first=False):
                    nonlocal_sidx = state_sidx[0]
                    bi_s = nonlocal_sidx % 6
                    state_sidx[0] = nonlocal_sidx + 1
                    bank_s = ps.banks[bi_s]
                    bev_s = sfree[bi_s]
                    w0 = max(0, (kt - 4 * qg) * 128)
                    mm = s.emit("tensor", (lambda e, bank_s=bank_s, hl=hl, kt=kt, qg=qg, w0=w0:
                                           e.matmul(bank_s[:, w0:512], K[hl][:, kt * 128:(kt + 1) * 128],
                                                    Q[hl][:, qg * 512 + w0:(qg + 1) * 512],
                                                    start=True, stop=True)),
                                deps=(qkv_done + [bev_s]) if first else [bev_s], sig=True)
                    return (mm, bi_s, bank_s, w0)

                state_sidx = [sidx]
                pend = [emit_qk(0, first=(qg == 0 and hl == 0))]
                if ktmax >= 1:
                    pend.append(emit_qk(1))
                for kt in range(ktmax + 1):
                    mm_s, bi_s, bank_s, w0 = pend.pop(0)
                    if kt + 2 <= ktmax:
                        pend.append(emit_qk(kt + 2))
                    psl = kt % 2
                    e_exp = s.emit("scalar", (lambda e, bank_s=bank_s, psl=psl, w0=w0:
                                              e.activation(P[psl][:, w0:512], bank_s[:, w0:512], AF.Exp, scale=RSD)),
                                   deps=[mm_s, pdep[psl]], sig=True)
                    sfree[bi_s] = e_exp
                    p_ready = e_exp
                    if kt >= 4 * qg:
                        qi = kt - 4 * qg
                        p_ready = s.emit("vector", (lambda e, psl=psl, qi=qi:
                                                    e.tensor_mul(P[psl][:, qi * 128:(qi + 1) * 128],
                                                                 P[psl][:, qi * 128:(qi + 1) * 128], tri[:, :])),
                                         deps=[e_exp, ev_tri], sig=True)
                    l_ev = s.emit("tensor", (lambda e, psl=psl, w0=w0,
                                             st=(kt == 0), sp=(kt == ktmax):
                                             e.matmul(bank_l[0:1, w0:512], onesb[:, :], P[psl][:, w0:512],
                                                      start=st, stop=sp)),
                                  deps=[p_ready, ev_ob] + ([l_free] if kt == 0 else []), sig=True)
                    av_ev = s.emit("tensor", (lambda e, hl=hl, kt=kt, psl=psl, w0=w0,
                                              st=(kt == 0), sp=(kt == ktmax):
                                              e.matmul(bank_o[:, w0:512], V[hl][:, kt * 128:(kt + 1) * 128],
                                                       P[psl][:, w0:512], start=st, stop=sp)),
                                   deps=[p_ready] + ([o_free] if kt == 0 else []), sig=True)
                    pdep[psl] = av_ev
                sidx = state_sidx[0]
                e_li = s.emit("vector", lambda e: e.reciprocal(lrow[0:1, :], bank_l[0:1, :]),
                              deps=[l_ev, lrow_read], sig=True)
                l_free = e_li
                bi_b = sidx % 6
                sidx += 1
                bank_b = ps.banks[bi_b]
                eb = s.emit("tensor", lambda e, bank_b=bank_b: e.matmul(bank_b[:, :], ones1[:, :], lrow[0:1, :],
                                                                        start=True, stop=True),
                            deps=[e_li, sfree[bi_b], ev_o1], sig=True)
                lrow_read = eb
                ec_ = s.emit("scalar", lambda e, bank_b=bank_b: e.copy(lb[:, :], bank_b[:, :]),
                             deps=[eb, o_free], sig=True)
                sfree[bi_b] = ec_
                eo = s.emit("vector", (lambda e, hl=hl, qg=qg:
                                       e.tensor_mul(osb[hl][:, qg * 512:(qg + 1) * 512], bank_o[:, :], lb[:, :])),
                            deps=[av_ev, ec_, osb_reads[hl]], sig=True)
                o_free = eo
                if l.mode == "unfused":
                    dev = s.dma("sync", a2ai[hl * 128:(hl + 1) * 128, b * TT + qg * 512: b * TT + (qg + 1) * 512],
                                osb[hl][:, qg * 512:(qg + 1) * 512], deps=[eo])
                    l.store("pattn", dev)
                pat_dep.append(eo)
                it += 1

        proj_reads = []
        for qg in range(8):
            for oc in range(NCH):
                bi, bank, bev = ps.acquire()
                mm = None
                for ec in range(2):
                    mm = s.emit("tensor", (lambda e, bank=bank, oc=oc, ec=ec, qg=qg,
                                           st=(ec == 0), sp=(ec == 1):
                                           e.matmul(bank[:, :], wpj[:, ec * NCH * 128 + oc * 128: ec * NCH * 128 + (oc + 1) * 128],
                                                    osb[ec][:, qg * 512:(qg + 1) * 512], start=st, stop=sp)),
                              deps=[ev_wpj, pat_dep[-1]] + ([bev] if ec == 0 else []),
                              sig=(True if ec == 1 else None))
                slp = itp % 4
                eev = s.emit("scalar", (lambda e, bank=bank, slp=slp: e.copy(pout[slp][:, :], bank[:, :])),
                             deps=[mm, pout_dma[slp]], sig=True)
                ps.release(bi, eev)
                if l.mode == "fused":
                    psb = l.dout("ps0a") if b == 0 else l.dout("ps0b")
                    dev = s.dma("sync", psb[qg * DM + oc * 128: qg * DM + (oc + 1) * 128, :],
                                pout[slp][:, :], deps=[eev])
                    l.store("ps0a" if b == 0 else "ps0b", dev)
                else:
                    dev = s.dma("sync", ps0[qg * DM + oc * 128: qg * DM + (oc + 1) * 128, b * 512:(b + 1) * 512],
                                pout[slp][:, :], deps=[eev])
                    l.store("ps0", dev)
                pout_dma[slp] = dev
                itp += 1
                proj_reads.append(mm)
        osb_reads = [proj_reads[-1], proj_reads[-1]]
        if getattr(l, "rs_hook", None) is not None:
            l.rs_hook(b)
    for i in range(6):
        ps.free_ev[i] = sfree[i]
    ps.free_ev[6] = o_free
    ps.free_ev[7] = l_free
    ps.rr = 0
    for d in pat_dep:
        l.store("pattn", d)
    close_stack(stack)


def emit_cmb0(l):
    stack = []
    emit_cmb(l, "xT", "s0", "g0", "x1", "xn1", stack)
    close_stack(stack)


def emit_cmb(l, x_name, s_name, g_name, out_x, out_xn, stack=None):
    s, nc = l.s, l.nc
    own_stack = stack is None
    if own_stack:
        stack = []
    xT = l.din(x_name)
    sT = l.din(s_name)
    gT = l.din(g_name)
    x_t = [mk_sbuf(nc, stack, f"cx{i}", [128, COLS], F32) for i in range(NCH)]
    tb = [mk_sbuf(nc, stack, f"ct{i}", [128, COLS], F32) for i in range(2)]
    gb = [mk_sbuf(nc, stack, f"cg{i}", [128, COLS], F32) for i in range(2)]
    x_ev = []
    prev = [None, None]
    out_dram = l.dout(out_x) if out_x else None
    for i in range(NCH):
        sl = i % 2
        e1 = s.dma("sync", x_t[i][:, :], xT[i * 128:(i + 1) * 128, :], deps=l.ldeps(x_name))
        e2 = s.dma("sync", tb[sl][:, :], sT[i * 128:(i + 1) * 128, :], deps=list(l.ldeps(s_name)) + [prev[sl]])
        e3 = s.dma("sync", gb[sl][:, :], gT[i * 128:(i + 1) * 128, :], deps=list(l.ldeps(g_name)) + [prev[sl]])
        e4 = s.emit("vector", (lambda e, sl=sl: e.tensor_mul(tb[sl][:, :], tb[sl][:, :], gb[sl][:, :])),
                    deps=[e2, e3], sig=True)
        e5 = s.emit("vector", (lambda e, i=i, sl=sl: e.tensor_add(x_t[i][:, :], x_t[i][:, :], tb[sl][:, :])),
                    deps=[e1, e4], sig=True)
        prev[sl] = e5
        x_ev.append(e5)
        if out_dram is not None:
            dev = s.dma("sync", out_dram[i * 128:(i + 1) * 128, :], x_t[i][:, :], deps=[e5])
            l.store(out_x, dev)
    if out_xn:
        xn_t = [mk_sbuf(nc, stack, f"cxn{i}", [128, COLS], BF16) for i in range(NCH)]
        emit_ln(l, x_t, x_ev, out_xn, xn_t, stack, "cln")
    if own_stack:
        close_stack(stack)


def emit_cmb1(l):
    stack = []
    emit_cmb(l, "x1", "s1", "g1", "outT", None, stack)
    close_stack(stack)


def emit_ffn0(l, upto="all", phase="all", outer_stack=None):
    s, nc, ps = l.s, l.nc, l.ps
    stack = []
    ostack = outer_stack if outer_stack is not None else stack
    xn0 = l.din("xn0")
    g0 = l.dout("g0")
    y1 = l.dscratch("y1d", (128, 16 * COLS), BF16)
    y2 = l.dscratch("y2d", (128, 64 * COLS), BF16)
    h128, ev_h = l.const("h128", BF16, ostack)
    a2, ev_a2 = l.const("a2", BF16, ostack)
    scal = mk_sbuf(nc, ostack, "fscal", [128, 208], F32)
    z1 = [mk_sbuf(nc, ostack, f"fz{i}", [128, COLS], BF16) for i in range(NCH)]
    l.ffn_ub = mk_sbuf(nc, ostack, "fub", [128, NCH * 128 * 2], BF16)
    l.ffn_gb = mk_sbuf(nc, ostack, "fgb", [128, NCH * 128 * 2], BF16)
    l.ffn_pref = {}
    l.ffn_h128 = h128
    l.ffn_a2 = (a2, ev_a2)
    a1, ev_a1 = l.const("a1", BF16, stack)
    su_ug, ev_s1 = l.const("su_ug", F32, stack)
    su_dp, ev_s2 = l.const("su_dp", F32, stack)
    ws_up, ev_s3 = l.const("ws_up", F32, stack)
    ws_gate, ev_s4 = l.const("ws_gate", F32, stack)
    ws_dp, ev_s5 = l.const("ws_dp", F32, stack)
    es1 = s.emit("vector", lambda e: e.tensor_scalar_mul(scal[:, 0:64], su_dp[:, :], FW2), deps=[ev_s2], sig=True)
    es2 = s.emit("vector", lambda e: e.tensor_scalar_mul(scal[:, 64:128], ws_up[:, :], 64.0), deps=[ev_s3], sig=True)
    es3 = s.emit("vector", lambda e: e.tensor_scalar_mul(scal[:, 128:192], ws_gate[:, :], 64.0), deps=[ev_s4], sig=True)
    es4 = s.emit("vector", lambda e: e.tensor_scalar_mul(scal[:, 192:208], ws_dp[:, :], 64.0), deps=[ev_s5], sig=True)

    xn_t = [mk_sbuf(nc, stack, f"fx{i}", [128, COLS], BF16) for i in range(NCH)]
    xn_ev = [s.dma("sync", xn_t[i][:, :], xn0[i * 128:(i + 1) * 128, :], deps=l.ldeps("xn0")) for i in range(NCH)]
    xs_t = xn_t
    xs_ev = [s.emit("vector", (lambda e, i=i: e.tensor_scalar(xs_t[i][:, :], xn_t[i][:, :],
                                                              su_ug[:, i:i + 1], FW1, ALU.mult, ALU.mult)),
                    deps=[xn_ev[i], ev_s1], sig=True) for i in range(NCH)]

    if upto == "xs":
        for jp in range(16):
            dev = s.dma("gpsimd", g0[jp * 128:(jp + 1) * 128, :], xs_t[jp][:, :], deps=xs_ev)
            l.store("g0", dev)
        close_stack(stack)
        return
    ybf = [mk_sbuf(nc, stack, f"fyb{i}", [128, 512], BF16) for i in range(2)]
    ybf_dma = [None, None]
    it = 0
    y1_evs = []
    for g in range(NCH):
        for tg in range(2):
            bi, bank, bev = ps.acquire()
            mm = s.emit("tensor", (lambda e, b_=bank, g=g, tg=tg:
                                   e.matmul(b_[:, :], h128[:, :], xs_t[g][:, tg * 512:(tg + 1) * 512],
                                            start=True, stop=True)),
                        deps=[xs_ev[g], ev_h, bev], sig=True)
            sl = it % 2
            eev = s.emit("scalar", lambda e, b_=bank, sl=sl: e.copy(ybf[sl][:, :], b_[:, :]),
                         deps=[mm, ybf_dma[sl]], sig=True)
            ps.release(bi, eev)
            dev = s.dma("sync", y1[:, g * COLS + tg * 512: g * COLS + tg * 512 + 512], ybf[sl][:, :], deps=[eev])
            ybf_dma[sl] = dev
            y1_evs.append(dev)
            it += 1

    if upto == "y1":
        for g in range(16):
            dev = s.dma("gpsimd", g0[g * 128:(g + 1) * 128, :].rearrange("p t -> p t"),
                        y1[:, g * COLS:(g + 1) * COLS], deps=y1_evs)
            l.store("g0", dev)
        close_stack(stack)
        return
    y1f = y1.rearrange("r c -> (r c)")
    x2_t = z1
    x2_ev = []
    for jp in range(16):
        src = bass.AP(y1f.tensor, jp * (16 * COLS), [[16 * (16 * COLS), 8], [COLS, 16], [1, COLS]])
        dst = z1[jp][:, :]
        zev = s.dma("sync", dst, src, deps=y1_evs)
        if upto == "z1":
            dev = s.dma("gpsimd", g0[jp * 128:(jp + 1) * 128, :], z1[jp][:, :], deps=[zev])
            l.store("g0", dev)
            continue
        for tg in range(2):
            bi, bank, bev = ps.acquire()
            mm = s.emit("tensor", (lambda e, b_=bank, jp=jp, tg=tg:
                                   e.matmul(b_[:, :], a1[:, :], z1[jp][:, tg * 512:(tg + 1) * 512],
                                            start=True, stop=True)),
                        deps=[zev, ev_a1, bev], sig=True)
            eev = s.emit("scalar", (lambda e, b_=bank, jp=jp, tg=tg:
                                    e.copy(x2_t[jp][:, tg * 512:(tg + 1) * 512], b_[:, :])), deps=[mm], sig=True)
            ps.release(bi, eev)
            x2_ev.append(eev)
    if upto == "x2":
        for jp in range(16):
            dev = s.dma("gpsimd", g0[jp * 128:(jp + 1) * 128, :], x2_t[jp][:, :], deps=x2_ev)
            l.store("g0", dev)
        close_stack(stack)
        return

    if upto == "z1":
        close_stack(stack)
        return
    if phase == "head":
        close_stack(stack)
        l.ffn_state = (x2_t, x2_ev, scal, es1, es2, es3, es4)
        return
    _ffn0_tail(l, x2_t, x2_ev, scal, es1, es2, es3, es4, stack)


def emit_ffn0_tail(l):
    (x2_t, x2_ev, scal, es1, es2, es3, es4) = l.ffn_state
    _ffn0_tail(l, x2_t, x2_ev, scal, es1, es2, es3, es4, [])
    l.ffn_state = None


def _ffn0_tail(l, x2_t, x2_ev, scal, es1, es2, es3, es4, stack, upto="all"):
    s, nc, ps = l.s, l.nc, l.ps
    g0 = l.dout("g0")
    y2 = l.dscratch("y2d", (128, 64 * COLS), BF16)
    h128 = l.ffn_h128
    a2, ev_a2 = l.ffn_a2
    ybf = [mk_sbuf(nc, stack, f"fybT{i}", [128, 512], BF16) for i in range(2)]
    ybf_dma = [None, None]
    xhb = [mk_sbuf(nc, stack, f"fxh2{i}", [128, 512], BF16) for i in range(2)]
    pend_xh = []

    def _flush_xh(item):
        fi, tg, xsl, e_xh = item
        bi2, bank2, bev2 = ps.acquire()
        mm2 = s.emit("tensor", (lambda e, b_=bank2, xsl=xsl:
                                e.matmul(b_[:, :], h128[:, :], xhb[xsl][:, :], start=True, stop=True)),
                     deps=[e_xh, bev2], sig=True)
        st2["xh"][xsl] = mm2
        sl2 = (fi * 2 + tg) % 2
        eev = s.emit("scalar", (lambda e, b_=bank2, sl2=sl2: e.copy(ybf[sl2][:, :], b_[:, :])),
                     deps=[mm2, ybf_dma[sl2]], sig=True)
        ps.release(bi2, eev)
        dev = s.dma("sync", y2[:, fi * COLS + tg * 512: fi * COLS + tg * 512 + 512], ybf[sl2][:, :], deps=[eev])
        ybf_dma[sl2] = dev
        y2_evs[tg].append(dev)

    wup, wq_q, wup_d = l.wget("wupT")
    wgt, wg_q, wgt_d = l.wget("wgateT")
    NS = 2
    ub = l.ffn_ub
    gb = l.ffn_gb
    ub_ev, gb_ev = [None] * NS, [None] * NS
    sgb = mk_sbuf(nc, stack, "fsgb", [128, 512], BF16)
    xub = mk_sbuf(nc, stack, "fxub", [128, 512], BF16)
    y2_evs = [[], []]
    st2 = {"xh": [None, None]}
    for fi in range(64):
        slot = fi % NS
        if ("u", fi) in l.ffn_pref:
            ulev = l.ffn_pref[("u", fi)]
            glev = l.ffn_pref[("g", fi)]
        else:
            usrc = wup[:, fi * 128:(fi + 1) * 128].rearrange("(ec p) f -> p ec f", p=128)
            udst = ub[:, slot * NCH * 128:(slot + 1) * NCH * 128].rearrange("p (ec f) -> p ec f", f=128)
            ulev = s.dma(wq_q, udst, usrc, deps=(ub_ev[slot],) + wup_d)
            gsrc = wgt[:, fi * 128:(fi + 1) * 128].rearrange("(ec p) f -> p ec f", p=128)
            gdst = gb[:, slot * NCH * 128:(slot + 1) * NCH * 128].rearrange("p (ec f) -> p ec f", f=128)
            glev = s.dma(wg_q, gdst, gsrc, deps=(gb_ev[slot],) + wgt_d)
        mm_u = mm_g = None
        for tg in range(2):
            bi_u, bank_u, bev_u = ps.acquire()
            bi_g, bank_g, bev_g = ps.acquire()
            for ec in range(NCH):
                mm_u = s.emit("tensor", (lambda e, b_=bank_u, ec=ec, slot=slot, tg=tg,
                                         st=(ec == 0), sp=(ec == NCH - 1):
                                         e.matmul(b_[:, :], ub[:, slot * NCH * 128 + ec * 128: slot * NCH * 128 + (ec + 1) * 128],
                                                  x2_t[ec][:, tg * 512:(tg + 1) * 512], start=st, stop=sp)),
                              deps=[ulev] + (x2_ev + [bev_u] if ec == 0 else []),
                              sig=(True if ec == NCH - 1 else None))
            for ec in range(NCH):
                mm_g = s.emit("tensor", (lambda e, b_=bank_g, ec=ec, slot=slot, tg=tg,
                                         st=(ec == 0), sp=(ec == NCH - 1):
                                         e.matmul(b_[:, :], gb[:, slot * NCH * 128 + ec * 128: slot * NCH * 128 + (ec + 1) * 128],
                                                  x2_t[ec][:, tg * 512:(tg + 1) * 512], start=st, stop=sp)),
                              deps=[glev] + ([bev_g] if ec == 0 else []),
                              sig=(True if ec == NCH - 1 else None))
            e_sg = s.emit("scalar", (lambda e, b_=bank_g, fi=fi:
                                     e.activation(sgb[:, :], b_[:, :], AF.Silu, scale=scal[:, 128 + fi:129 + fi])),
                          deps=[mm_g, es3] + list(st2["xh"]), sig=True)
            e_xu = s.emit("scalar", (lambda e, b_=bank_u, fi=fi:
                                     e.activation(xub[:, :], b_[:, :], AF.Copy, scale=scal[:, 64 + fi:65 + fi])),
                          deps=[mm_u, es2] + list(st2["xh"]), sig=True)
            ps.release(bi_g, e_sg)
            ps.release(bi_u, e_xu)
            xsl = (fi * 2 + tg) % 2
            e_xh = s.emit("vector", (lambda e, fi=fi, xsl=xsl:
                                     e.scalar_tensor_tensor(xhb[xsl][:, :], sgb[:, :], scal[:, fi:fi + 1], xub[:, :],
                                                            ALU.mult, ALU.mult)),
                          deps=[e_sg, e_xu, es1, st2["xh"][xsl]], sig=True)
            pend_xh.append((fi, tg, xsl, e_xh))
            if len(pend_xh) > 1:
                _flush_xh(pend_xh.pop(0))
        ub_ev[slot] = mm_u
        gb_ev[slot] = mm_g

    if upto == "y2":
        for fi in range(16):
            dev = s.dma("gpsimd", g0[fi * 128:(fi + 1) * 128, :], y2[:, fi * COLS:(fi + 1) * COLS],
                        deps=y2_evs[0] + y2_evs[1])
            l.store("g0", dev)
        close_stack(stack)
        return
    while pend_xh:
        _flush_xh(pend_xh.pop(0))
    wdn, wd_q, wdn_d = l.wget("wdownT")
    y2f = y2.rearrange("r c -> (r c)")
    _z2tiles = [mk_sbuf(nc, stack, f"fz2_{i}", [128, 512], BF16) for i in range(64)]
    z2h = [_z2tiles, _z2tiles]
    dnb = mk_sbuf(nc, stack, "fdnb", [128, 64 * 128 * 2], BF16)
    dnb_ev = [None, None]
    g0b = [mk_sbuf(nc, stack, f"fg0b{i}", [128, 512], F32) for i in range(2)]
    g0b_dma = [None, None]
    it2 = 0

    def z_phase(th, extra=()):
        z2 = z2h[th]
        z_ev = []
        for jp in range(64):
            src = bass.AP(y2f.tensor, jp * (64 * COLS) + th * 512, [[64 * (64 * COLS), 2], [COLS, 64], [1, 512]])
            zev = s.dma("sync", z2[jp][:, :], src, deps=list(y2_evs[th]) + list(extra))
            bi, bank, bev = ps.acquire()
            mm = s.emit("tensor", (lambda e, b_=bank, jp=jp, z2=z2:
                                   e.matmul(b_[:, :], a2[:, :], z2[jp][:, :], start=True, stop=True)),
                        deps=[zev, ev_a2, bev], sig=True)
            eev = s.emit("scalar", (lambda e, b_=bank, jp=jp, z2=z2: e.copy(z2[jp][:, :], b_[:, :])), deps=[mm], sig=True)
            ps.release(bi, eev)
            z_ev.append(eev)
        return z_ev

    def down_phase(th, z_ev):
        nonlocal it2
        z2 = z2h[th]
        for oc in range(NCH):
            slot = oc % 2
            src = wdn[:, oc * 128:(oc + 1) * 128].rearrange("(ec p) f -> p ec f", p=128)
            dst = dnb[:, slot * 64 * 128:(slot + 1) * 64 * 128].rearrange("p (ec f) -> p ec f", f=128)
            lev = s.dma(wd_q, dst, src, deps=(dnb_ev[slot],) + wdn_d)
            bi, bank, bev = ps.acquire()
            mm = None
            for jp in range(64):
                mm = s.emit("tensor", (lambda e, b_=bank, jp=jp, slot=slot, z2=z2, st=(jp == 0), sp=(jp == 63):
                                       e.matmul(b_[:, :], dnb[:, slot * 64 * 128 + jp * 128: slot * 64 * 128 + (jp + 1) * 128],
                                                z2[jp][:, :], start=st, stop=sp)),
                             deps=[lev] + (z_ev + [bev] if jp == 0 else []),
                             sig=(True if jp == 63 else None))
            dnb_ev[slot] = mm
            if oc == NCH - 1:
                l.ffn_last_down = mm
            sl2 = it2 % 2
            eev = s.emit("scalar", (lambda e, b_=bank, oc=oc, sl2=sl2:
                                    e.activation(g0b[sl2][:, :], b_[:, :], AF.Silu, scale=scal[:, 192 + oc:193 + oc])),
                         deps=[mm, es4, g0b_dma[sl2]], sig=True)
            ps.release(bi, eev)
            dev = s.dma("sync", g0[oc * 128:(oc + 1) * 128, th * 512:(th + 1) * 512], g0b[sl2][:, :], deps=[eev])
            g0b_dma[sl2] = dev
            l.store("g0", dev)
            it2 += 1

    z0 = z_phase(0)
    down_phase(0, z0)
    z1b = z_phase(1, extra=(l.ffn_last_down,))
    down_phase(1, z1b)
    close_stack(stack)


def emit_moe(l, upto="all"):
    s, nc, ps = l.s, l.nc, l.ps
    stack = []
    xn1 = l.din("xn1")
    g1 = l.dout("g1")
    ident, ev_id = l.const("ident", F32, stack)
    ones1, ev_o1 = l.const("ones1", F32)
    xn_t = [mk_sbuf(nc, stack, f"mx{i}", [128, COLS], BF16) for i in range(NCH)]
    xn_ev = [s.dma("sync", xn_t[i][:, :], xn1[i * 128:(i + 1) * 128, :], deps=l.ldeps("xn1")) for i in range(NCH)]

    gateT = l.din("gateT")
    comb_b = [mk_sbuf(nc, stack, f"mcb{e}", [128, COLS], BF16) for e in range(8)]
    rstack = []
    gt_sb = mk_sbuf(nc, rstack, "mgt", [128, NCH * 8], F32)
    ev_gt = s.dma("sync", gt_sb[:, :].rearrange("p (ec f) -> p ec f", f=8),
                  gateT[:, :].rearrange("(ec p) f -> p ec f", p=128))
    xnf = [mk_sbuf(nc, rstack, f"mxf{i}", [128, 512], F32) for i in range(2)]
    xnf_mm = [None, None]
    lgT = mk_sbuf(nc, rstack, "mlg", [8, COLS], F32)
    lg_ev = []
    for tg in range(2):
        bi, bank, bev = ps.acquire()
        mm = None
        for ec in range(NCH):
            sl = ec % 2
            xev = s.dma("gpsimd", xnf[sl][:, :], xn1[ec * 128:(ec + 1) * 128, tg * 512:(tg + 1) * 512],
                        deps=list(l.ldeps("xn1")) + [xnf_mm[sl]])
            mm = s.emit("tensor", (lambda e, bank=bank, sl=sl, ec=ec, st=(ec == 0), sp=(ec == NCH - 1):
                                   e.matmul(bank[0:8, :], gt_sb[:, ec * 8:(ec + 1) * 8], xnf[sl][:, :],
                                            start=st, stop=sp)),
                        deps=[xev, ev_gt] + ([bev] if ec == 0 else []), sig=True)
            xnf_mm[sl] = mm
        eev = s.emit("scalar", (lambda e, bank=bank, tg=tg: e.copy(lgT[:, tg * 512:(tg + 1) * 512], bank[0:8, :])),
                     deps=[mm], sig=True)
        ps.release(bi, eev)
        lg_ev.append(eev)

    if upto == "logits":
        dev = s.dma("sync", g1[0:8, :], lgT[:, :], deps=lg_ev)
        l.store("g1", dev)
        close_stack(stack)
        return
    lgm = mk_sbuf(nc, rstack, "mlgm", [128, 8], F32)
    scr = mk_sbuf(nc, rstack, "mscr", [128, 16], F32)
    eqs = mk_sbuf(nc, rstack, "meq", [128, 24], F32)
    combT = mk_sbuf(nc, rstack, "mcbT", [8, COLS], F32)
    comb_b_ev = [None] * 8
    crow = mk_sbuf(nc, rstack, "mcrow", [1, COLS], F32)
    cb_evs = []
    prev_route = None
    for cb_i in range(8):
        bi, bank, bev = ps.acquire()
        et = s.emit("tensor", (lambda e, bank=bank, cb_i=cb_i:
                               e.transpose(bank[0:128, 0:8], lgT[0:8, cb_i * 128:(cb_i + 1) * 128], ident[0:8, 0:8])),
                    deps=lg_ev + [ev_id, bev], sig=True)
        e0 = s.emit("scalar", lambda e, bank=bank: e.copy(lgm[:, :], bank[0:128, 0:8]), deps=[et, prev_route], sig=True)
        ps.release(bi, e0)
        if upto == "lgm":
            dev = s.dma("sync", g1[cb_i * 128:(cb_i + 1) * 128, 0:8], lgm[:, :], deps=[e0])
            l.store("g1", dev)
            prev_route = dev
            continue
        m1, m2, m1n = scr[:, 0:1], scr[:, 1:2], scr[:, 2:3]
        Z, Zi, w2, e2m = scr[:, 3:4], scr[:, 4:5], scr[:, 5:6], scr[:, 6:7]
        ex = scr[:, 8:16]
        e1 = s.emit("vector", lambda e: e.reduce_max(m1, lgm[:, :], axis=AX), deps=[e0], sig=True)
        e2a = s.emit("vector", lambda e: e.tensor_tensor(eqs[:, 0:8], lgm[:, :], m1.broadcast_to([128, 8]), op=ALU.subtract),
                     deps=[e1], sig=True)
        e2b = s.emit("vector", lambda e: e.tensor_mul(eqs[:, 0:8], eqs[:, 0:8], eqs[:, 0:8]), deps=[e2a], sig=True)
        e2c = s.emit("vector", lambda e: e.tensor_scalar_mul(eqs[:, 0:8], eqs[:, 0:8], 1.0e30),
                     deps=[e2b], sig=True)
        e2d = s.emit("vector", lambda e: e.tensor_scalar_min(eqs[:, 0:8], eqs[:, 0:8], 1.0),
                     deps=[e2c], sig=True)
        e2e = s.emit("vector", lambda e: e.tensor_scalar_mul(eqs[:, 0:8], eqs[:, 0:8], -1.0), deps=[e2d], sig=True)
        e2 = s.emit("vector", lambda e: e.tensor_scalar_add(eqs[:, 0:8], eqs[:, 0:8], 1.0), deps=[e2e], sig=True)
        e3 = s.emit("vector", lambda e: e.scalar_tensor_tensor(eqs[:, 16:24], eqs[:, 0:8], -1.0e30, lgm[:, :],
                                                               ALU.mult, ALU.add), deps=[e2], sig=True)
        e4 = s.emit("vector", lambda e: e.reduce_max(m2, eqs[:, 16:24], axis=AX), deps=[e3], sig=True)
        e5 = s.emit("vector", lambda e: e.tensor_scalar_mul(m1n, m1, -1.0), deps=[e1], sig=True)
        e6 = s.emit("scalar", lambda e: e.activation(ex, lgm[:, :], AF.Exp, bias=m1n), deps=[e5, e0], sig=True)
        e7 = s.emit("vector", lambda e: e.reduce_sum(Z, ex, axis=AX), deps=[e6], sig=True)
        e8 = s.emit("vector", lambda e: e.reciprocal(Zi, Z), deps=[e7], sig=True)
        e9 = s.emit("scalar", lambda e: e.activation(e2m, m2, AF.Exp, bias=m1n), deps=[e4, e5], sig=True)
        e10 = s.emit("vector", lambda e: e.tensor_mul(w2, e2m, Zi), deps=[e8, e9], sig=True)
        e11a = s.emit("vector", lambda e: e.tensor_tensor(eqs[:, 8:16], lgm[:, :], m2.broadcast_to([128, 8]), op=ALU.subtract),
                      deps=[e4, e6], sig=True)
        e11b = s.emit("vector", lambda e: e.tensor_mul(eqs[:, 8:16], eqs[:, 8:16], eqs[:, 8:16]), deps=[e11a], sig=True)
        e11c = s.emit("vector", lambda e: e.tensor_scalar_mul(eqs[:, 8:16], eqs[:, 8:16], 1.0e30),
                      deps=[e11b], sig=True)
        e11d = s.emit("vector", lambda e: e.tensor_scalar_min(eqs[:, 8:16], eqs[:, 8:16], 1.0),
                      deps=[e11c], sig=True)
        e11e = s.emit("vector", lambda e: e.tensor_scalar_mul(eqs[:, 8:16], eqs[:, 8:16], -1.0), deps=[e11d], sig=True)
        e11 = s.emit("vector", lambda e: e.tensor_scalar_add(eqs[:, 8:16], eqs[:, 8:16], 1.0), deps=[e11e], sig=True)
        e12 = s.emit("vector", lambda e: e.tensor_scalar(eqs[:, 8:16], eqs[:, 8:16], w2, None, ALU.mult),
                     deps=[e10, e11], sig=True)
        e13 = s.emit("vector", lambda e: e.scalar_tensor_tensor(lgm[:, :], eqs[:, 0:8], Zi, eqs[:, 8:16],
                                                                ALU.mult, ALU.add), deps=[e8, e12, e2], sig=True)
        prev_route = e13
        if upto == "route":
            dev = s.dma("sync", g1[cb_i * 128:(cb_i + 1) * 128, 0:8], lgm[:, :], deps=[e13])
            l.store("g1", dev)
            prev_route = dev
            continue
        bi2, bank2, bev2 = ps.acquire()
        et2 = s.emit("tensor", (lambda e, bank2=bank2: e.transpose(bank2[0:8, 0:128], lgm[:, :], ident[:, :])),
                     deps=[e13, bev2], sig=True)
        ecp = s.emit("scalar", (lambda e, bank2=bank2, cb_i=cb_i:
                                e.copy(combT[:, cb_i * 128:(cb_i + 1) * 128], bank2[0:8, 0:128])),
                     deps=[et2], sig=True)
        ps.release(bi2, ecp)
        cb_evs.append(ecp)
    if upto in ("route", "lgm"):
        close_stack(stack)
        return
    for e_ in range(8):
        er = s.dma("sync", crow[0:1, :], combT[e_:e_ + 1, :],
                   deps=cb_evs + [comb_b_ev[e_ - 1] if e_ > 0 else None])
        for tg in range(2):
            bi, bank, bev = ps.acquire()
            eb = s.emit("tensor", (lambda e, bank=bank, tg=tg:
                                   e.matmul(bank[:, :], ones1[:, :], crow[0:1, tg * 512:(tg + 1) * 512],
                                            start=True, stop=True)),
                        deps=[er, bev, ev_o1], sig=True)
            ecb = s.emit("scalar", (lambda e, bank=bank, e_=e_, tg=tg:
                                    e.copy(comb_b[e_][:, tg * 512:(tg + 1) * 512], bank[:, :])), deps=[eb], sig=True)
            ps.release(bi, ecb)
            comb_b_ev[e_] = ecb

    if upto == "comb":
        for e_ in range(8):
            dev = s.dma("gpsimd", g1[e_ * 128:(e_ + 1) * 128, :], comb_b[e_][:, :], deps=[comb_b_ev[e_]])
            l.store("g1", dev)
        for tg in range(2):
            dev = s.dma("gpsimd", g1[1024 + tg * 128:1024 + (tg + 1) * 128, 0:8].rearrange("p f -> p f"),
                        lgm[:, :], deps=[prev_route])
            l.store("g1", dev)
        close_stack(stack)
        return
    close_stack(rstack)
    y_t = [mk_sbuf(nc, stack, f"my{i}", [128, COLS], F32) for i in range(NCH)]
    y_ev = [None] * NCH
    guT, gu_q, gu_d = l.wget("guT")
    mdT, md_q, md_d = l.wget("moedT")
    act_t = [mk_sbuf(nc, stack, f"ma{i}", [128, COLS], BF16) for i in range(11)]
    gub = mk_sbuf(nc, stack, "mgub", [128, NCH * 128 * 3], BF16)
    gub_ev = [None, None, None]
    sgb = [mk_sbuf(nc, stack, f"msg{i}", [128, 512], BF16) for i in range(2)]
    sg_prev = [None, None]
    dnb = mk_sbuf(nc, stack, "mdnb", [128, 11 * 128 * 3], BF16)
    dnb_ev = [None, None, None]
    for e_ in range(8):
        act_ev = [[None, None] for _ in range(11)]
        for fi in range(11):
            for part in range(2):
                fr = fi if part == 0 else 11 + fi
                slot = (fi * 2 + part) % 3
                src = guT[e_ * DM:(e_ + 1) * DM, fr * 128:(fr + 1) * 128].rearrange("(ec p) f -> p ec f", p=128) \
                    if gu_q == "sync" else guT[e_, :, fr * 128:(fr + 1) * 128].rearrange("(ec p) f -> p ec f", p=128)
                dst = gub[:, slot * NCH * 128:(slot + 1) * NCH * 128].rearrange("p (ec f) -> p ec f", f=128)
                lev = s.dma(gu_q, dst, src, deps=(gub_ev[slot],) + gu_d)
                mm = None
                for tg in range(2):
                    bi, bank, bev = ps.acquire()
                    for ec in range(NCH):
                        mm = s.emit("tensor", (lambda e, bank=bank, slot=slot, ec=ec, tg=tg,
                                               st=(ec == 0), sp=(ec == NCH - 1):
                                               e.matmul(bank[:, :],
                                                        gub[:, slot * NCH * 128 + ec * 128: slot * NCH * 128 + (ec + 1) * 128],
                                                        xn_t[ec][:, tg * 512:(tg + 1) * 512], start=st, stop=sp)),
                                      deps=[lev, xn_ev[ec]] + ([bev] if ec == 0 else []),
                                      sig=(True if ec == NCH - 1 else None))
                    if part == 0:
                        esl = s.emit("scalar", (lambda e, bank=bank, tg=tg:
                                                e.activation(sgb[tg][:, :], bank[:, :], AF.Silu)),
                                     deps=[mm, sg_prev[tg]], sig=True)
                        ps.release(bi, esl)
                        act_ev[fi][tg] = esl
                    else:
                        ea = s.emit("vector", (lambda e, bank=bank, fi=fi, tg=tg:
                                               e.tensor_mul(act_t[fi][:, tg * 512:(tg + 1) * 512], sgb[tg][:, :], bank[:, :])),
                                    deps=[mm, act_ev[fi][tg]], sig=True)
                        sg_prev[tg] = ea
                        eb2 = s.emit("vector", (lambda e, fi=fi, tg=tg, e_=e_:
                                                e.tensor_mul(act_t[fi][:, tg * 512:(tg + 1) * 512],
                                                             act_t[fi][:, tg * 512:(tg + 1) * 512],
                                                             comb_b[e_][:, tg * 512:(tg + 1) * 512])),
                                     deps=[ea, comb_b_ev[e_]], sig=True)
                        ps.release(bi, eb2)
                        act_ev[fi][tg] = eb2
                gub_ev[slot] = mm
        for oc in range(NCH):
            slot = oc % 3
            src = mdT[e_ * 1408:(e_ + 1) * 1408, oc * 128:(oc + 1) * 128].rearrange("(ec p) f -> p ec f", p=128) \
                if md_q == "sync" else mdT[e_, :, oc * 128:(oc + 1) * 128].rearrange("(ec p) f -> p ec f", p=128)
            dst = dnb[:, slot * 11 * 128:(slot + 1) * 11 * 128].rearrange("p (ec f) -> p ec f", f=128)
            lev = s.dma(md_q, dst, src, deps=(dnb_ev[slot],) + md_d)
            mm = None
            for tg in range(2):
                bi, bank, bev = ps.acquire()
                for fi in range(11):
                    mm = s.emit("tensor", (lambda e, bank=bank, slot=slot, fi=fi, tg=tg,
                                           st=(fi == 0), sp=(fi == 10):
                                           e.matmul(bank[:, :],
                                                    dnb[:, slot * 11 * 128 + fi * 128: slot * 11 * 128 + (fi + 1) * 128],
                                                    act_t[fi][:, tg * 512:(tg + 1) * 512], start=st, stop=sp)),
                                  deps=[lev, act_ev[fi][tg]] + ([bev] if fi == 0 else []),
                                  sig=(True if fi == 10 else None))
                if e_ == 0:
                    ey = s.emit("scalar", (lambda e, bank=bank, oc=oc, tg=tg:
                                           e.copy(y_t[oc][:, tg * 512:(tg + 1) * 512], bank[:, :])),
                                deps=[mm, y_ev[oc]], sig=True)
                else:
                    ey = s.emit("vector", (lambda e, bank=bank, oc=oc, tg=tg:
                                           e.tensor_add(y_t[oc][:, tg * 512:(tg + 1) * 512],
                                                        y_t[oc][:, tg * 512:(tg + 1) * 512], bank[:, :])),
                                deps=[mm, y_ev[oc]], sig=True)
                ps.release(bi, ey)
                y_ev[oc] = ey
            dnb_ev[slot] = mm

    sguT, sg_q, sg_d = l.wget("sguT")
    sdT, sd_q, sd_d = l.wget("sdT")
    sact = [mk_sbuf(nc, stack, f"msa{i}", [128, 512], BF16) for i in range(22)]
    sdb = mk_sbuf(nc, stack, "msdb", [128, 22 * 128 * 2], BF16)
    sdb_ev = [None, None]
    g1b = [mk_sbuf(nc, stack, f"mg1{i}", [128, 512], F32) for i in range(2)]
    g1b_dma = [None, None]
    it3 = 0
    for tg in range(2):
        sact_ev = [None] * 22
        for fi in range(22):
            for part in range(2):
                fr = fi if part == 0 else 22 + fi
                slot = (fi * 2 + part) % 3
                src = sguT[:, fr * 128:(fr + 1) * 128].rearrange("(ec p) f -> p ec f", p=128)
                dst = gub[:, slot * NCH * 128:(slot + 1) * NCH * 128].rearrange("p (ec f) -> p ec f", f=128)
                lev = s.dma(sg_q, dst, src, deps=(gub_ev[slot],) + sg_d)
                bi, bank, bev = ps.acquire()
                mm = None
                for ec in range(NCH):
                    mm = s.emit("tensor", (lambda e, bank=bank, slot=slot, ec=ec, tg=tg,
                                           st=(ec == 0), sp=(ec == NCH - 1):
                                           e.matmul(bank[:, :],
                                                    gub[:, slot * NCH * 128 + ec * 128: slot * NCH * 128 + (ec + 1) * 128],
                                                    xn_t[ec][:, tg * 512:(tg + 1) * 512], start=st, stop=sp)),
                                  deps=[lev] + ([bev] if ec == 0 else []),
                                  sig=(True if ec == NCH - 1 else None))
                gub_ev[slot] = mm
                if part == 0:
                    esl = s.emit("scalar", (lambda e, bank=bank, tg=tg: e.activation(sgb[tg][:, :], bank[:, :], AF.Silu)),
                                 deps=[mm, sg_prev[tg]], sig=True)
                    ps.release(bi, esl)
                    sact_ev[fi] = esl
                else:
                    ea = s.emit("vector", (lambda e, bank=bank, fi=fi, tg=tg:
                                           e.tensor_mul(sact[fi][:, :], sgb[tg][:, :], bank[:, :])),
                                deps=[mm, sact_ev[fi]], sig=True)
                    sg_prev[tg] = ea
                    ps.release(bi, ea)
                    sact_ev[fi] = ea
        for oc in range(NCH):
            slot = oc % 2
            src = sdT[:, oc * 128:(oc + 1) * 128].rearrange("(ec p) f -> p ec f", p=128)
            dst = sdb[:, slot * 22 * 128:(slot + 1) * 22 * 128].rearrange("p (ec f) -> p ec f", f=128)
            lev = s.dma(sd_q, dst, src, deps=(sdb_ev[slot],) + sd_d)
            bi, bank, bev = ps.acquire()
            mm = None
            for fi in range(22):
                mm = s.emit("tensor", (lambda e, bank=bank, slot=slot, fi=fi, st=(fi == 0), sp=(fi == 21):
                                       e.matmul(bank[:, :],
                                                sdb[:, slot * 22 * 128 + fi * 128: slot * 22 * 128 + (fi + 1) * 128],
                                                sact[fi][:, :], start=st, stop=sp)),
                              deps=[lev, sact_ev[fi]] + ([bev] if fi == 0 else []),
                              sig=(True if fi == 21 else None))
            sdb_ev[slot] = mm
            ey = s.emit("vector", (lambda e, bank=bank, oc=oc, tg=tg:
                                   e.tensor_add(y_t[oc][:, tg * 512:(tg + 1) * 512],
                                                y_t[oc][:, tg * 512:(tg + 1) * 512], bank[:, :])),
                        deps=[mm, y_ev[oc]], sig=True)
            ps.release(bi, ey)
            y_ev[oc] = ey
            sl = it3 % 2
            eg = s.emit("scalar", (lambda e, oc=oc, tg=tg, sl=sl:
                                   e.activation(g1b[sl][:, :], y_t[oc][:, tg * 512:(tg + 1) * 512], AF.Silu)),
                        deps=[ey, g1b_dma[sl]], sig=True)
            dev = s.dma("sync", g1[oc * 128:(oc + 1) * 128, tg * 512:(tg + 1) * 512], g1b[sl][:, :], deps=[eg])
            g1b_dma[sl] = dev
            l.store("g1", dev)
            it3 += 1
    close_stack(stack)


def emit_delta(l, upto="all"):
    s, nc, ps = l.s, l.nc, l.ps
    stack = []
    xn1 = l.din("xn1")
    s1 = l.dout("s1")
    identb, ev_idb = l.const("ident", BF16, stack)
    gnorm, ev_gn = l.const("gnorm", F32, stack)
    xn_t = [mk_sbuf(nc, stack, f"dx{i}", [128, COLS], BF16) for i in range(NCH)]
    xn_ev = [s.dma("sync", xn_t[i][:, :], xn1[i * 128:(i + 1) * 128, :], deps=l.ldeps("xn1")) for i in range(NCH)]

    conv = {nm: l.const(nm, F32, stack) for nm in ("convq", "convk", "convv")}
    qkv_t, qkv_ev = {}, {}
    tmp = mk_sbuf(nc, stack, "dtmp", [128, 512], BF16)
    dwring = (mk_sbuf(nc, stack, "dwring", [128, NCH * 128 * 2], BF16), [None, None])
    for nm, wnm, cnm in (("q", "wq1T", "convq"), ("k", "wk1T", "convk"), ("v", "wv1T", "convv")):
        tiles = [mk_sbuf(nc, stack, f"d{nm}{i}", [128, COLS], BF16) for i in range(NCH)]
        qkv_t[nm] = tiles
        evs = [[None, None] for _ in range(NCH)]

        def ep(fi, tg, bank, mm_ev, f0, fw, c0, cw, tiles=tiles, evs=evs):
            eev = s.emit("scalar", (lambda e, bank=bank, fi=fi, c0=c0: e.copy(tiles[fi][:, c0:c0 + 512], bank[:, :])),
                         deps=[mm_ev], sig=True)
            evs[fi][tg] = eev
            return eev

        emit_proj(l, wnm, [(h * 128, 128) for h in range(NH)], xn_t, xn_ev, ep, stack, f"d{nm}w", wring=dwring)
        csb, ev_c = conv[cnm]
        outevs = []
        tmp_prev = None
        for h in range(NH):
            t0 = tiles[h][:, 0:512]
            t1 = tiles[h][:, 512:1024]
            d0, d1 = evs[h]
            e1 = s.emit("vector", (lambda e, t1=t1, h=h, cs=csb:
                                   e.tensor_scalar(tmp[:, :], t1, cs[:, h * 4 + 3:h * 4 + 4], None, ALU.mult)),
                        deps=[d1, ev_c, tmp_prev], sig=True)
            e2 = s.emit("vector", (lambda e, t0=t0, t1=t1, h=h, cs=csb:
                                   e.scalar_tensor_tensor(t1, t0, cs[:, h * 4 + 2:h * 4 + 3], tmp[:, :],
                                                          ALU.mult, ALU.add)),
                        deps=[e1, d0], sig=True)
            tmp_prev = e2
            e3 = s.emit("vector", (lambda e, t0=t0, h=h, cs=csb:
                                   e.tensor_scalar(t0, t0, cs[:, h * 4 + 3:h * 4 + 4], None, ALU.mult)),
                        deps=[d0, e2], sig=True)
            e4 = s.emit("scalar", (lambda e, t=tiles[h]: e.activation(t[:, :], t[:, :], AF.Silu)),
                        deps=[e2, e3], sig=True)
            outevs.append(e4)
        qkv_ev[nm] = outevs

    if upto == "conv":
        for i in range(NCH):
            dev = s.dma("gpsimd", s1[i * 128:(i + 1) * 128, :], qkv_t["q"][i][:, :], deps=qkv_ev["q"])
            l.store("s1", dev)
        close_stack(stack)
        return
    wb = l.din("wbT")
    wbsb = mk_sbuf(nc, stack, "dwb", [128, NCH * 16], BF16)
    ev_wb = s.dma("gpsimd", wbsb[:, :].rearrange("p (ec f) -> p ec f", f=16),
                  wb[:, :].rearrange("(ec p) f -> p ec f", p=128))
    btf = mk_sbuf(nc, stack, "dbtf", [16, COLS], F32)
    bt_ev = []
    for tg in range(2):
        bi, bank, bev = ps.acquire()
        mm = None
        for ec in range(NCH):
            mm = s.emit("tensor", (lambda e, bank=bank, ec=ec, tg=tg, st=(ec == 0), sp=(ec == NCH - 1):
                                   e.matmul(bank[0:16, :], wbsb[:, ec * 16:(ec + 1) * 16],
                                            xn_t[ec][:, tg * 512:(tg + 1) * 512], start=st, stop=sp)),
                        deps=[ev_wb, xn_ev[ec]] + ([bev] if ec == 0 else []), sig=(True if ec == NCH - 1 else None))
        eev = s.emit("scalar", (lambda e, bank=bank, tg=tg: e.copy(btf[:, tg * 512:(tg + 1) * 512], bank[0:16, :])),
                     deps=[mm], sig=True)
        ps.release(bi, eev)
        bt_ev.append(eev)
    esg = s.emit("scalar", lambda e: e.activation(btf[:, :], btf[:, :], AF.Sigmoid), deps=bt_ev, sig=True)
    beta_tm = mk_sbuf(nc, stack, "dbtm", [128, 8 * 16], F32)
    beta_ev = []
    btmp = mk_sbuf(nc, stack, "dbtmp", [16, 128], BF16)
    prev_bt = None
    for cb_i in range(8):
        ecp0 = s.emit("vector", (lambda e, cb_i=cb_i: e.tensor_copy(btmp[0:16, 0:128], btf[0:16, cb_i * 128:(cb_i + 1) * 128])),
                      deps=[esg, prev_bt], sig=True)
        bi, bank, bev = ps.acquire()
        bkb = bank.bitcast(BF16)
        et = s.emit("tensor", (lambda e, bkb=bkb: e.transpose(bkb[0:128, 0:16], btmp[0:16, 0:128], identb[0:16, 0:16])),
                    deps=[ecp0, ev_idb, bev], sig=True)
        prev_bt = et
        ee = s.emit("vector", (lambda e, bkb=bkb, cb_i=cb_i: e.tensor_copy(beta_tm[:, cb_i * 16:(cb_i + 1) * 16],
                                                                           bkb[0:128, 0:16])),
                    deps=[et], sig=True)
        ps.release(bi, ee)
        beta_ev.append(ee)

    tm = {nm: [mk_sbuf(nc, stack, f"dtm{nm}{sb_}", [128, NH * 128], BF16) for sb_ in range(2)]
          for nm in ("q", "k", "v")}
    prod = mk_sbuf(nc, stack, "dprod", [128, 2048], BF16)
    dots = mk_sbuf(nc, stack, "ddots", [128, 8 * 16], F32)
    scal = mk_sbuf(nc, stack, "dscal", [128, 16 * 16], F32)
    scbf = mk_sbuf(nc, stack, "dscbf", [128, 8 * 16], BF16)
    u1 = mk_sbuf(nc, stack, "du1", [128, 2048], BF16)
    u2 = mk_sbuf(nc, stack, "du2", [128, 2048], BF16)
    o0b = [mk_sbuf(nc, stack, f"do0{i}", [128, 2048], BF16) for i in range(2)]
    o1b = [mk_sbuf(nc, stack, f"do1{i}", [128, 2048], BF16) for i in range(2)]
    oT = xn_t
    oT_evs = []


    prev_tm_reads = [None, None]
    o_reads = [None, None]
    pend_oT = []

    def flush_oT(item):
        jj, par, oo0, oo1, of0, of1 = item
        reads = []
        for sb_, (oo, oev, colblk) in enumerate(((oo0, of0, jj), (oo1, of1, jj + 4))):
            for h in range(NH):
                bi, bank, bev = ps.acquire()
                bkb = bank.bitcast(BF16)
                et = s.emit("tensor", (lambda e, bkb=bkb, oo=oo, h=h:
                                       e.transpose(bkb[0:128, 0:128], oo[:, h * 128:(h + 1) * 128], identb[:, :])),
                            deps=[oev, ev_idb, bev], sig=True)
                ee = s.emit("vector", (lambda e, bkb=bkb, h=h, colblk=colblk:
                                       e.tensor_scalar(oT[h][:, colblk * 128:(colblk + 1) * 128], bkb[0:128, 0:128],
                                                       gnorm[:, 0:1], None, ALU.mult)),
                            deps=[et, ev_gn], sig=True)
                ps.release(bi, ee)
                reads.append(ee)
                oT_evs.append(ee)
        o_reads[par] = reads[-1]

    for j in range(4):
        tm_ev = []
        for sb_, colblk in ((0, j), (1, j + 4)):
            for nm in ("q", "k", "v"):
                for h in range(NH):
                    bi, bank, bev = ps.acquire()
                    bkb = bank.bitcast(BF16)
                    et = s.emit("tensor", (lambda e, bkb=bkb, nm=nm, h=h, colblk=colblk:
                                           e.transpose(bkb[0:128, 0:128],
                                                       qkv_t[nm][h][:, colblk * 128:(colblk + 1) * 128], identb[:, :])),
                                deps=[qkv_ev[nm][h], ev_idb, bev], sig=True)
                    ee = s.emit("vector", (lambda e, bkb=bkb, nm=nm, sb_=sb_, h=h:
                                           e.tensor_copy(tm[nm][sb_][:, h * 128:(h + 1) * 128], bkb[0:128, 0:128])),
                                deps=[et, prev_tm_reads[sb_]], sig=True)
                    ps.release(bi, ee)
                    tm_ev.append(ee)
        q0_, q1_ = tm["q"][0], tm["q"][1]
        k0_, k1_ = tm["k"][0], tm["k"][1]
        v0_, v1_ = tm["v"][0], tm["v"][1]
        pr_specs = [(q0_, q0_), (q1_, q1_), (k0_, k0_), (k1_, k1_),
                    (q0_, k0_), (k1_, k0_), (q1_, k0_), (q1_, k1_)]
        dot_evs = []
        red_prev = None
        for idx, (a_, b_) in enumerate(pr_specs):
            ep1 = s.emit("vector", (lambda e, a_=a_, b_=b_: e.tensor_mul(prod[:, :], a_[:, :], b_[:, :])),
                         deps=tm_ev + [red_prev], sig=True)
            red_prev = s.emit("vector", (lambda e, idx=idx:
                                         e.reduce_sum(dots[:, idx * 16:(idx + 1) * 16],
                                                      prod[:, :].rearrange("p (h d) -> p h d", d=128), axis=AX)),
                              deps=[ep1], sig=True)
            dot_evs.append(red_prev)
        sc_evs = {}
        for idx in range(4):
            eps12, ev_eps12 = l.epsap(1.0e-12)
            eL = s.emit("scalar", (lambda e, idx=idx:
                                   e.activation(scal[:, idx * 16:(idx + 1) * 16], dots[:, idx * 16:(idx + 1) * 16],
                                                AF.Ln, bias=eps12[:, 0:1])), deps=[dot_evs[idx], ev_eps12], sig=True)
            sc_evs[idx] = s.emit("scalar", (lambda e, idx=idx:
                                            e.activation(scal[:, idx * 16:(idx + 1) * 16], scal[:, idx * 16:(idx + 1) * 16],
                                                         AF.Exp, scale=-0.5)), deps=[eL], sig=True)

        def dotscale(dst_off, pidx, r1o, r2o, const, deps_):
            ea = s.emit("vector", (lambda e, dst_off=dst_off, pidx=pidx, r1o=r1o, const=const:
                                   e.scalar_tensor_tensor(scal[:, dst_off * 16:(dst_off + 1) * 16],
                                                          dots[:, pidx * 16:(pidx + 1) * 16], const,
                                                          scal[:, r1o * 16:(r1o + 1) * 16], ALU.mult, ALU.mult)),
                        deps=deps_, sig=True)
            return s.emit("vector", (lambda e, dst_off=dst_off, r2o=r2o:
                                     e.tensor_mul(scal[:, dst_off * 16:(dst_off + 1) * 16],
                                                  scal[:, dst_off * 16:(dst_off + 1) * 16],
                                                  scal[:, r2o * 16:(r2o + 1) * 16])),
                          deps=[ea], sig=True)

        e_c0 = dotscale(4, 4, 0, 2, RSD, [dot_evs[4], sc_evs[0], sc_evs[2]])
        e_a = dotscale(5, 5, 3, 2, 1.0, [dot_evs[5], sc_evs[3], sc_evs[2]])
        e_b1 = dotscale(6, 6, 1, 2, RSD, [dot_evs[6], sc_evs[1], sc_evs[2]])
        e_c1 = dotscale(7, 7, 1, 3, RSD, [dot_evs[7], sc_evs[1], sc_evs[3]])
        par = j % 2
        o0 = o0b[par]
        o1 = o1b[par]
        b0sl = beta_tm[:, j * 16:(j + 1) * 16]
        b1sl = beta_tm[:, (j + 4) * 16:(j + 5) * 16]

        def perhead(out_t, in_t, col_ap, deps_):
            evs_ = []
            for h in range(NH):
                evs_.append(s.emit("vector", (lambda e, h=h, out_t=out_t, in_t=in_t, col_ap=col_ap:
                                              e.tensor_scalar(out_t[:, h * 128:(h + 1) * 128],
                                                              in_t[:, h * 128:(h + 1) * 128],
                                                              col_ap[:, h:h + 1], None, ALU.mult)),
                                   deps=deps_ if h == 0 else [evs_[-1]], sig=True))
            return evs_[-1]

        r1_ = perhead(u1, v0_, b0sl, tm_ev + beta_ev)
        r2_ = perhead(o0, u1, scal[:, 64:80], [r1_, e_c0, o_reads[par]])
        r3_ = perhead(u2, u1, scal[:, 80:96], [r1_, e_a])
        r4_ = s.emit("vector", (lambda e: e.tensor_sub(u2[:, :], v1_[:, :], u2[:, :])), deps=[r3_] + tm_ev, sig=True)
        r5_ = perhead(u2, u2, b1sl, [r4_])
        r6_ = perhead(o1, u1, scal[:, 96:112], [r1_, e_b1, r3_, o_reads[par]])
        r7_ = perhead(u1, u2, scal[:, 112:128], [r5_, e_c1, r2_, r6_])
        r8_ = s.emit("vector", (lambda e, o1=o1: e.tensor_add(o1[:, :], o1[:, :], u1[:, :])), deps=[r6_, r7_], sig=True)
        o_fin = []
        for oo, oev, soff in ((o0, r2_, 8), (o1, r8_, 9)):
            ep1 = s.emit("vector", (lambda e, oo=oo: e.tensor_mul(prod[:, :], oo[:, :], oo[:, :])),
                         deps=[oev, red_prev], sig=True)
            ep2 = s.emit("vector", (lambda e, soff=soff:
                                    e.reduce_sum(scal[:, soff * 16:(soff + 1) * 16],
                                                 prod[:, :].rearrange("p (h d) -> p h d", d=128), axis=AX)),
                         deps=[ep1], sig=True)
            red_prev = ep2
            eps5b, ev_eps5b = l.epsap(1.0e-5)
            eL = s.emit("scalar", (lambda e, soff=soff:
                                   e.activation(scal[:, soff * 16:(soff + 1) * 16], scal[:, soff * 16:(soff + 1) * 16],
                                                AF.Ln, scale=1.0 / 128.0, bias=eps5b[:, 0:1])), deps=[ep2, ev_eps5b], sig=True)
            eE = s.emit("scalar", (lambda e, soff=soff:
                                   e.activation(scal[:, soff * 16:(soff + 1) * 16], scal[:, soff * 16:(soff + 1) * 16],
                                                AF.Exp, scale=-0.5)), deps=[eL], sig=True)
            eo = perhead(oo, oo, scal[:, soff * 16:(soff + 1) * 16], [eE, oev])
            o_fin.append(eo)
        pend_oT.append((j, par, o0, o1, o_fin[0], o_fin[1]))
        if len(pend_oT) > 1:
            flush_oT(pend_oT.pop(0))
        prev_tm_reads = [dot_evs[-1], r4_]

    while pend_oT:
        flush_oT(pend_oT.pop(0))
    sbuf_o = [mk_sbuf(nc, stack, f"dsb{i}", [128, 512], F32) for i in range(2)]
    sb_dma = [None, None]
    st = {"it": 0}

    def ep2(fi, tg, bank, mm_ev, f0, fw, c0, cw):
        sl = st["it"] % 2
        eev = s.emit("scalar", lambda e, bank=bank, sl=sl: e.copy(sbuf_o[sl][:, :], bank[:, :]),
                     deps=[mm_ev, sb_dma[sl]], sig=True)
        dev = s.dma("sync", s1[f0:f0 + 128, c0:c0 + cw], sbuf_o[sl][:, :], deps=[eev])
        sb_dma[sl] = dev
        l.store("s1", dev)
        st["it"] += 1
        return eev

    emit_proj(l, "woT", [(h * 128, 128) for h in range(NH)], oT, oT_evs, ep2, stack, "dwo", wring=dwring)
    close_stack(stack)


LAUNCHES = [
    ("ln0", emit_ln0, ["xT", "onesf", "ones1"], ["xn0"],
     [("onesf", F32), ("ones1", F32)]),
    ("ffn0", emit_ffn0,
     ["xn0", "h128", "a1", "a2", "su_ug", "su_dp", "ws_up", "ws_gate", "ws_dp",
      "wupT", "wgateT", "wdownT"], ["g0"],
     [("h128", BF16), ("a1", BF16), ("a2", BF16), ("su_ug", F32), ("su_dp", F32),
      ("ws_up", F32), ("ws_gate", F32), ("ws_dp", F32)]),
    ("attq", emit_attq,
     ["xng", "wqhT", "wkhT", "wvhT", "wprojhT", "cs2", "ss2", "tri", "onesf", "ones1"], ["pattn", "ps0"],
     [("cs2", F32), ("ss2", F32), ("tri", BF16), ("ones1", F32)]),
    ("cmb0", emit_cmb0, ["xT", "s0", "g0", "onesf", "ones1"], ["x1", "xn1"],
     [("onesf", F32), ("ones1", F32)]),
    ("moe", emit_moe, ["xn1", "gateT", "guT", "moedT", "sguT", "sdT", "ident", "ones1"], ["g1"],
     [("ident", F32), ("ones1", F32)]),
    ("delta", emit_delta,
     ["xn1", "wq1T", "wk1T", "wv1T", "wbT", "convq", "convk", "convv", "gnorm", "woT", "ident"], ["s1"],
     [("ident", BF16), ("gnorm", F32), ("convq", F32), ("convk", F32), ("convv", F32)]),
    ("cmb1", emit_cmb1, ["x1", "s1", "g1"], ["outT"], []),
]


def _run_single(name, emitf, ins, outs, consts, sh, pc, hand):
    nc = bass.Bass("TRN2", target_bir_lowering=False)
    s = Sched(nc)
    ps = PsumMgr(nc)
    lctx = L(nc, s, ps, "unfused")
    with nc.Block() as blk:
        lctx.const("onesf", F32)
        lctx.const("ones1", F32)
        lctx.epsap(1.0e-5)
        lctx.epsap(1.0e-12)
        emitf(lctx)
        s.finalize()
        s.replay(blk)
    close_stack(lctx.cstack)
    ps.close()
    s.close()
    in_maps = []
    for c in range(NCORE):
        m = {}
        for nm in list(ins) + ["onesf", "ones1"]:
            if nm in sh:
                m[nm] = sh[nm]
            elif nm in pc[c]:
                m[nm] = pc[c][nm]
            elif nm in hand[c]:
                m[nm] = hand[c][nm]
            else:
                raise KeyError(nm)
        in_maps.append(m)
    res = run_bass_kernel_spmd(nc, in_maps, core_ids=list(range(NCORE)))
    return res.results


def run_unfused(sh, pc, only=None, hand=None):
    if hand is None:
        hand = [dict() for _ in range(NCORE)]
    for name, emitf, ins, outs, consts in LAUNCHES:
        if only is not None and name not in only:
            continue
        results = _run_single(name, emitf, ins, outs, consts, sh, pc, hand)
        for c in range(NCORE):
            for o in outs:
                hand[c][o] = np.asarray(results[c][o])
        if name == "ln0":
            xng = np.concatenate([hand[c]["xn0"] for c in range(NCORE)], axis=0)
            for c in range(NCORE):
                hand[c]["xng"] = xng
        if name == "attq":
            for c in range(NCORE):
                hand[c]["s0"] = np.sum(
                    [np.asarray(hand[i]["ps0"][DM * c:DM * (c + 1)], dtype=np.float32) for i in range(NCORE)],
                    axis=0).astype(np.float32)
    return hand


ALL_CONSTS = [("onesf", F32), ("ones1", F32)]

FUSED_INPUTS = ["xT", "wqhT", "wkhT", "wvhT", "wprojhT", "cs2", "ss2", "tri", "ident",
                "onesf", "ones1", "h128", "a1", "a2", "su_ug", "su_dp", "ws_up",
                "ws_gate", "ws_dp", "wupT", "wgateT", "wdownT", "gateT", "guT",
                "moedT", "sguT", "sdT", "wq1T", "wk1T", "wv1T", "wbT", "convq",
                "convk", "convv", "gnorm", "woT"]


def run_fused(sh, pc):
    nc = bass.Bass("TRN2", target_bir_lowering=False)
    s = Sched(nc)
    ps = PsumMgr(nc)
    lctx = L(nc, s, ps, "fused")
    rg = [list(range(NCORE))]
    with nc.Block() as blk:
        for cn, cd in ALL_CONSTS:
            lctx.const(cn, cd)
        lctx.epsap(1.0e-5)
        lctx.epsap(1.0e-12)
        emit_ln0(lctx)
        xn0_ap = lctx.dram["xn0"]
        xng_ap = lctx.din("xng")
        ev = s.emit("gpsimd",
                    (lambda e: e.collective_compute("AllGather", ALU.bypass, rg,
                                                    [xn0_ap.opt()], [xng_ap.opt()])),
                    deps=lctx.ldeps("xn0"), sig="coll")
        lctx.store("xng", ev)
        ffn_ostack = []
        emit_ffn0(lctx, phase="head", outer_stack=ffn_ostack)
        if USE_WBF:
            for wn in ("wqhT", "wkhT", "wvhT", "wprojhT", "tri"):
                lctx.wbf(wn)
        s.dma_guard = (lctx.ffn_state[1][-1],)

        def _rs_hook(b):
            nm_in = "ps0a" if b == 0 else "ps0b"
            nm_out = "s0a" if b == 0 else "s0b"
            pin = lctx.dout(nm_in)
            pout_ = lctx.din(nm_out)
            evc = s.emit("gpsimd",
                         (lambda e, pin=pin, pout_=pout_: e.collective_compute(
                             "ReduceScatter", ALU.add, rg, [pin.opt()], [pout_.opt()])),
                         deps=lctx.ldeps(nm_in), sig="coll")
            lctx.store(nm_out, evc)
        lctx.rs_hook = _rs_hook
        emit_attq(lctx)
        lctx.rs_hook = None
        _wupa, _wu_q, _wu_d = lctx.wget("wupT")
        _wgta, _wg_q, _wg_d = lctx.wget("wgateT")
        for _fi in (0, 1):
            _slot = _fi % 2
            lctx.ffn_pref[("u", _fi)] = s.dma(
                _wu_q,
                lctx.ffn_ub[:, _slot * NCH * 128:(_slot + 1) * NCH * 128].rearrange("p (ec f) -> p ec f", f=128),
                _wupa[:, _fi * 128:(_fi + 1) * 128].rearrange("(ec p) f -> p ec f", p=128), deps=_wu_d)
            lctx.ffn_pref[("g", _fi)] = s.dma(
                _wg_q,
                lctx.ffn_gb[:, _slot * NCH * 128:(_slot + 1) * NCH * 128].rearrange("p (ec f) -> p ec f", f=128),
                _wgta[:, _fi * 128:(_fi + 1) * 128].rearrange("(ec p) f -> p ec f", p=128), deps=_wg_d)
        s0_ap = lctx.dout("s0")
        eva = s.dma("sync", s0_ap[:, 0:512], lctx.din("s0a"), deps=lctx.ldeps("s0a"))
        evb = s.dma("sync", s0_ap[:, 512:1024], lctx.din("s0b"), deps=lctx.ldeps("s0b"))
        lctx.store("s0", eva)
        lctx.store("s0", evb)
        s.phase_fence()
        emit_ffn0_tail(lctx)
        close_stack(ffn_ostack)
        s.phase_fence()
        emit_cmb0(lctx)
        s.phase_fence()
        emit_moe(lctx)
        s.phase_fence()
        emit_delta(lctx)
        s.phase_fence()
        emit_cmb1(lctx)
        s.finalize()
        s.replay(blk)
    close_stack(lctx.cstack)
    ps.close()
    s.close()
    in_maps = []
    for c in range(NCORE):
        m = {}
        for nm in FUSED_INPUTS:
            m[nm] = sh[nm] if nm in sh else pc[c][nm]
        in_maps.append(m)
    res = run_bass_kernel_spmd(nc, in_maps, core_ids=list(range(NCORE)))
    return [{"outT": np.asarray(r["outT"])} for r in res.results]


FUSED = True


def kernel(**inputs):
    sh, pc = host_prep(inputs)
    if FUSED:
        hand = run_fused(sh, pc)
    else:
        hand = run_unfused(sh, pc)
    out = np.empty((2, 4096, 2048), np.float32)
    for c in range(NCORE):
        arr = np.asarray(hand[c]["outT"], dtype=np.float32)
        out[0, 512 * c:512 * (c + 1), :] = arr[:, 0:512].T
        out[1, 512 * c:512 * (c + 1), :] = arr[:, 512:1024].T
    return out
